# version 02
# baseline (speedup 1.0000x reference)
from contextlib import ExitStack

import numpy as np
import ml_dtypes
import concourse.bass as bass
import concourse.mybir as mybir
from concourse.bass import IndirectOffsetOnAxis
from concourse.bass_utils import run_bass_kernel_spmd

F32 = mybir.dt.float32
BF16 = mybir.dt.bfloat16
U32 = mybir.dt.uint32
AF = mybir.ActivationFunctionType
ALU = mybir.AluOpType
AX = mybir.AxisListType

BFNP = ml_dtypes.bfloat16

NCORES = 8
B, T, D = 4, 2048, 2048
TLOC = 1024
NT = TLOC // 128
ND = D // 128
EPS = 1e-5

H, DK, DV = 16, 64, 128
KD, VD = 1024, 2048
CHK = 64
NCHK = TLOC // CHK

DIN, DST, DCONV, DTR = 4096, 16, 4, 128
E64, CAP, TOPK6 = 64, 1024, 6

ENGS = ("sync", "gpsimd", "scalar", "vector", "tensor")


def _bf(x):
    return np.ascontiguousarray(np.asarray(x)).astype(BFNP)


def _f32(x):
    return np.ascontiguousarray(np.asarray(x), dtype=np.float32)


class SemC:

    def __init__(self, handle):
        self.h = handle
        self.n = 0

    def inc(self, k=1):
        self.n += k
        return self.n


class Prog:
    def __init__(self):
        self.nc = bass.Bass("TRN2", target_bir_lowering=False)
        self.stk = ExitStack()

    def sb(self, name, shape, dt):
        return self.stk.enter_context(self.nc.sbuf_tensor(name, shape, dt))

    def ps(self, name, shape, dt=F32):
        return self.stk.enter_context(self.nc.psum_tensor(name, shape, dt))

    def sem(self, name):
        return SemC(self.stk.enter_context(self.nc.semaphore(name)))

    def din(self, name, shape, dt):
        return self.nc.dram_tensor(name, shape, dt, kind="ExternalInput").ap()

    def dout(self, name, shape, dt):
        return self.nc.dram_tensor(name, shape, dt, kind="ExternalOutput").ap()

    def enable_consts(self, s, sem):
        self.cin = self.din("cconst", [128, 16], F32)
        self.ct = self.sb("cconst_t", [128, 16], F32)
        self.cmap = {}
        s.dma("sync", self.ct[:], self.cin, sem)

    def cc(self, val):
        if val not in self.cmap:
            assert len(self.cmap) < 16
            self.cmap[val] = len(self.cmap)
        return self.ct[:, self.cmap[val]:self.cmap[val] + 1]

    def const_host(self):
        v = np.zeros(16, np.float32)
        for k, i in self.cmap.items():
            v[i] = k
        return np.tile(v[None, :], (128, 1))

    def close(self):
        self.stk.close()


class Sched:

    def __init__(self, p):
        self.p = p
        self.ops = {e: [] for e in ENGS}

    def op(self, eng, fn, waits=(), incs=()):
        self.ops[eng].append((tuple(waits), fn, tuple(incs)))

    def wait(self, eng, sem, val):
        self.ops[eng].append((((sem, val),), None, ()))

    def dma(self, eng, out, in_, sem, **kw):
        tot = sem.inc(16)
        self.op(eng, lambda h, o=out, i=in_, kw=kw: h.dma_start(out=o, in_=i, **kw),
                incs=[(sem, 16)])
        self.wait(eng, sem, tot)
        return tot

    def build(self):
        nc = self.p.nc
        ops = self.ops
        with nc.Block() as blk:
            def mk(eng_ops):
                def body(h):
                    for waits, fn, incs in eng_ops:
                        for sem, val in waits:
                            h.wait_ge(sem.h, val)
                        if fn is None:
                            continue
                        ins = fn(h)
                        for sem, k in incs:
                            ins = ins.then_inc(sem.h, k)
                return body
            for e in ENGS:
                if ops[e]:
                    getattr(blk, e)(mk(ops[e]))


def emit_ln(s, p, xt, ot, stc, sv, vv, n_free, scr_full, ot2=None):
    inv = 1.0 / n_free
    s.op("scalar", lambda h: h.activation(scr_full, xt, AF.Copy,
                                          accum_out=stc[:, 0:1]))
    s.op("scalar", lambda h: h.activation(scr_full, xt, AF.Square,
                                          accum_out=stc[:, 1:2]),
         incs=[(sv, 1)])
    t0 = sv.inc()
    s.op("vector", lambda h: h.tensor_scalar(stc[:, 2:3], stc[:, 0:1], inv,
                                             None, ALU.mult),
         waits=[(sv, t0)], incs=[(vv, 1)])
    t1 = vv.inc()
    s.op("vector", lambda h: h.tensor_scalar(stc[:, 5:6], stc[:, 1:2], inv,
                                             None, ALU.mult),
         incs=[(vv, 1)])
    t2 = vv.inc()
    s.op("vector", lambda h: h.scalar_tensor_tensor(stc[:, 6:7], stc[:, 2:3],
                                                    -1.0, stc[:, 2:3],
                                                    ALU.mult, ALU.mult),
         waits=[(vv, t1)], incs=[(vv, 1)])
    t3 = vv.inc()
    s.op("vector", lambda h: h.tensor_add(stc[:, 4:5], stc[:, 5:6], stc[:, 6:7]),
         waits=[(vv, t3)], incs=[(vv, 1)])
    t4 = vv.inc()
    s.op("scalar", lambda h: h.activation(stc[:, 5:6], stc[:, 4:5], AF.Ln,
                                          bias=p.cc(EPS)),
         waits=[(vv, t4)], incs=[(sv, 1)])
    t5 = sv.inc()
    s.op("scalar", lambda h: h.activation(stc[:, 3:4], stc[:, 5:6], AF.Exp,
                                          scale=p.cc(-0.5)),
         waits=[(sv, t5)], incs=[(sv, 1)])
    t6 = sv.inc()
    s.op("vector", lambda h: h.tensor_scalar(ot, xt, stc[:, 2:3], stc[:, 3:4],
                                             ALU.subtract, ALU.mult),
         waits=[(sv, t6)], incs=[(vv, 1)])
    vv.inc()
    if ot2 is not None:
        s.op("vector", lambda h: h.tensor_scalar(ot2, xt, stc[:, 2:3],
                                                 stc[:, 3:4],
                                                 ALU.subtract, ALU.mult),
             incs=[(vv, 1)])
        vv.inc()


class MMPipe:

    def __init__(self, s, banks, tsm, esm, xsm):
        self.s = s
        self.banks = banks
        self.nb = len(banks)
        self.tsm = tsm
        self.xsm = xsm
        self.bank_free = [0] * self.nb
        self.j = 0

    def job(self, mms, steps, after=None, extra_tensor_waits=()):
        s, j = self.s, self.j
        bi = j % self.nb
        bank = self.banks[bi]
        waits = list(extra_tensor_waits)
        if self.bank_free[bi]:
            waits.append((self.xsm, self.bank_free[bi]))
        for i, fn in enumerate(mms):
            inc = [(self.tsm, 1)] if i == len(mms) - 1 else ()
            s.op("tensor", (lambda h, f=fn, b=bank: f(h, b)),
                 waits=waits if i == 0 else (), incs=inc)
        tj = self.tsm.inc()
        last_rp = max(i for i, st_ in enumerate(steps) if st_[2])
        gate = (self.tsm, tj)
        xv = None
        for i, (eng, fn, rp) in enumerate(steps):
            s.op(eng, (lambda h, f=fn, b=bank: f(h, b)),
                 waits=[gate], incs=[(self.xsm, 1)])
            xv = self.xsm.inc()
            gate = (self.xsm, xv)
            if i == last_rp:
                self.bank_free[bi] = xv
        self.j += 1
        if after is not None:
            after(xv)
        return xv


class BufMgr:

    def __init__(self, s):
        self.s = s
        self.last = {}

    def guard(self, eng, buf, sem):
        c = self.last.get(id(buf))
        if c:
            self.s.wait(eng, sem, c)

    def mark(self, buf, cnt):
        self.last[id(buf)] = cnt



def build_A1():
    p = Prog()
    nc = p.nc
    x_in = p.din("x", [TLOC, D], F32)
    wq = p.din("wq", [D, KD], F32)
    wk = p.din("wk", [D, KD], F32)
    wv = p.din("wv", [D, VD], F32)
    wg = p.din("wg", [D, VD], F32)
    wgk1 = p.din("wgk1", [D, 16], F32)
    wgk2 = p.din("wgk2", [16, KD], F32)
    bgk_bc = p.din("bgk_bc", [128, KD], F32)
    bgk_col = p.din("bgk_col", [128, 8], F32)
    ident_in = p.din("ident", [128, 128], BF16)

    o_qfm = p.dout("q_fm", [KD, TLOC], BF16)
    o_kfm = p.dout("k_fm", [KD, TLOC], BF16)
    o_ktok = p.dout("k_tok", [TLOC, KD], BF16)
    o_v = p.dout("v_tok", [TLOC, VD], BF16)
    o_g = p.dout("g_tok", [TLOC, VD], BF16)
    o_gkfm = p.dout("gk_fm", [KD, TLOC], F32)
    o_gktok = p.dout("gk_tok", [TLOC, KD], F32)

    xt = [p.sb(f"xt{i}", [128, D], F32) for i in range(NT)]
    lnt = [p.sb(f"lnt{i}", [128, D], BF16) for i in range(NT)]
    lnfm = [p.sb(f"lnfm{i}", [128, TLOC], BF16) for i in range(ND)]
    st = p.sb("st", [128, 8 * NT], F32)
    lnscr = p.sb("lnscr", [128, D], F32)
    ident = p.sb("identt", [128, 128], BF16)
    wsl = [p.sb(f"wsl{i}", [128, ND * 512], BF16) for i in range(2)]
    wgk1b = p.sb("wgk1b", [128, ND * 16], BF16)
    wgk2b = p.sb("wgk2b", [16, KD], BF16)
    lowfm = p.sb("lowfm", [16, TLOC], BF16)
    bgkbc = p.sb("bgkbc", [128, KD], F32)
    bgkcol = p.sb("bgkcol", [128, 8], F32)
    ev = [p.sb(f"ev{i}", [128, 512], BF16) for i in range(8)]
    scrA = [p.sb(f"scrA{i}", [128, 512], F32) for i in range(4)]
    scrC = [p.sb(f"scrC{i}", [128, 512], F32) for i in range(4)]

    pst = [p.ps(f"pst{i}", [128, 1024], BF16) for i in range(2)]
    pmm = [p.ps(f"pmm{i}", [128, 512], F32) for i in range(6)]

    ld = p.sem("ld")
    wsm = p.sem("wsm")
    sv = p.sem("sv")
    vv = p.sem("vv")
    tsm = p.sem("tsm")
    esm = p.sem("esm")
    osm = p.sem("osm")
    xsm = p.sem("xsm")

    s = Sched(p)
    p.enable_consts(s, ld)
    s.wait("scalar", ld, 16)
    xdone = []
    for i in range(NT):
        xdone.append(s.dma("sync", xt[i][:], x_in[i * 128:(i + 1) * 128, :], ld))
    s.dma("sync", bgkbc[:], bgk_bc, ld)
    s.dma("sync", bgkcol[:], bgk_col, ld)
    iddone = s.dma("sync", ident[:], ident_in, ld)

    s.dma("gpsimd", wgk1b[:].rearrange("p (c n) -> p c n", n=16), wgk1.rearrange("(c p) n -> p c n", p=128), wsm)
    s.dma("gpsimd", wgk2b[:], wgk2, wsm)
    s.dma("gpsimd", wsl[0][:].rearrange("p (c n) -> p c n", n=512), wq[:, 0:512].rearrange("(c p) n -> p c n", p=128), wsm)
    s.dma("gpsimd", wsl[1][:].rearrange("p (c n) -> p c n", n=512), wq[:, 512:1024].rearrange("(c p) n -> p c n", p=128), wsm)

    ln_done = []
    for i in range(NT):
        s.wait("scalar", ld, xdone[i])
        emit_ln(s, p, xt[i][:], lnt[i][:], st[:, 8 * i:8 * i + 8], sv, vv, D,
                lnscr[:])
        ln_done.append(vv.n)

    s.wait("tensor", ld, iddone)
    tp = MMPipe(s, pst, tsm, esm, xsm)
    for i in range(NT):
        s.wait("tensor", vv, ln_done[i])
        for dc in range(ND):
            def mmfn(h, b, i=i, dc=dc):
                return h.transpose(b[:, 0:128],
                                   lnt[i][:, dc * 128:(dc + 1) * 128], ident[:])
            def evfn(h, b, i=i, dc=dc):
                return h.tensor_copy(lnfm[dc][:, i * 128:(i + 1) * 128],
                                     b[:, 0:128])
            tp.job([mmfn], [("vector", evfn, True)])
    s.build()

    s = Sched(p)
    mp = MMPipe(s, pmm, tsm, esm, xsm)
    bm = BufMgr(s)
    stj = [0]

    def simple_store(dst_ap, pool, dtype_cast="copy"):
        j = stj[0]
        stj[0] += 1
        buf = pool[j % len(pool)]
        bm.guard("vector", buf, osm)
        def ev_fn(h, b, buf=buf):
            return h.tensor_copy(buf[:], b[:])
        def after(last_x, buf=buf, dst=dst_ap):
            s.op("sync", lambda h, d=dst, bf=buf: h.dma_start(out=d, in_=bf[:]),
                 waits=[(xsm, last_x)], incs=[(osm, 16)])
            bm.mark(buf, osm.inc(16))
        return [("vector", ev_fn, True)], after

    def run_roleB(slab_buf, wait_w, dst, nch0):
        first = True
        for nch in range(4):
            for th in range(2):
                mms = []
                for dc in range(ND):
                    def f(h, b, dc=dc, nch=nch, th=th, sb_=slab_buf):
                        return h.matmul(
                            b[:],
                            sb_[:, dc * 512 + nch * 128:dc * 512 + nch * 128 + 128],
                            lnfm[dc][:, th * 512:th * 512 + 512],
                            start=(dc == 0), stop=(dc == ND - 1))
                    mms.append(f)
                steps, after = simple_store(
                    dst[(nch0 + nch) * 128:(nch0 + nch + 1) * 128,
                        th * 512:th * 512 + 512], ev)
                mp.job(mms, steps, after,
                       extra_tensor_waits=[(wsm, wait_w)] if first else ())
                first = False

    def run_roleA(slab_buf, wait_w, dst, col0, ncols):
        first = True
        for tt in range(NT):
            mms = []
            for dc in range(ND):
                def f(h, b, dc=dc, tt=tt, sb_=slab_buf):
                    return h.matmul(
                        b[:], lnfm[dc][:, tt * 128:(tt + 1) * 128],
                        sb_[:, dc * ncols:dc * ncols + ncols],
                        start=(dc == 0), stop=(dc == ND - 1))
                mms.append(f)
            steps, after = simple_store(
                dst[tt * 128:(tt + 1) * 128, col0:col0 + ncols], ev)
            mp.job(mms, steps, after,
                   extra_tensor_waits=[(wsm, wait_w)] if first else ())
            first = False

    run_roleB(wsl[0], 16 * 3, o_qfm, 0)
    t_q0 = tsm.n
    run_roleB(wsl[1], 16 * 4, o_qfm, 4)
    t_q1 = tsm.n

    s.wait("gpsimd", tsm, t_q0)
    w_k0 = s.dma("gpsimd", wsl[0][:].rearrange("p (c n) -> p c n", n=512),
                 wk[:, 0:512].rearrange("(c p) n -> p c n", p=128), wsm)
    run_roleB(wsl[0], w_k0, o_kfm, 0)
    run_roleA(wsl[0], w_k0, o_ktok, 0, 512)
    t_k0 = tsm.n
    s.wait("gpsimd", tsm, t_q1)
    w_k1 = s.dma("gpsimd", wsl[1][:].rearrange("p (c n) -> p c n", n=512),
                 wk[:, 512:1024].rearrange("(c p) n -> p c n", p=128), wsm)
    run_roleB(wsl[1], w_k1, o_kfm, 4)
    run_roleA(wsl[1], w_k1, o_ktok, 512, 512)
    prev_t = [t_k0, tsm.n]
    for phase, w_ap, dst in ((0, wv, o_v), (1, wg, o_g)):
        for si in range(4):
            buf = wsl[si % 2]
            s.wait("gpsimd", tsm, prev_t[si % 2])
            w_i = s.dma("gpsimd", buf[:].rearrange("p (c n) -> p c n", n=512),
                        w_ap[:, si * 512:(si + 1) * 512]
                        .rearrange("(c p) n -> p c n", p=128), wsm)
            run_roleA(buf, w_i, dst, si * 512, 512)
            prev_t[si % 2] = tsm.n

    for th in range(2):
        mms = []
        for dc in range(ND):
            def f(h, b, dc=dc, th=th):
                return h.matmul(b[0:16, :], wgk1b[:, dc * 16:dc * 16 + 16],
                                lnfm[dc][:, th * 512:th * 512 + 512],
                                start=(dc == 0), stop=(dc == ND - 1))
            mms.append(f)
        def evl(h, b, th=th):
            return h.tensor_copy(lowfm[0:16, th * 512:th * 512 + 512],
                                 b[0:16, :])
        mp.job(mms, [("vector", evl, True)],
               extra_tensor_waits=[(wsm, 32)] if th == 0 else ())

    def gk_steps(a, c, u_maker):
        return [
            u_maker,
            ("scalar", lambda h, b: h.activation(c[:], a[:], AF.Abs), False),
            ("scalar", lambda h, b: h.activation(c[:], c[:], AF.Exp, scale=-1.0),
             False),
            ("scalar", lambda h, b: h.activation(c[:], c[:], AF.Ln, bias=1.0),
             False),
            ("vector", lambda h, b: h.tensor_scalar(a[:], a[:], 0.0, 1.0 / 16,
                                                    ALU.min, ALU.mult), False),
            ("vector", lambda h, b: h.scalar_tensor_tensor(a[:], c[:],
                                                           -1.0 / 16, a[:],
                                                           ALU.mult, ALU.add),
             False),
        ]

    gk_j = [0]

    def gk_job(mms, u_maker, dst_ap):
        j = gk_j[0]
        gk_j[0] += 1
        a = scrA[j % 4]
        c = scrC[j % 4]
        bm.guard("vector", a, osm)
        bm.guard("scalar", a, osm)
        steps = gk_steps(a, c, u_maker(a))
        def after(last_x, a=a, dst=dst_ap):
            s.op("sync", lambda h: h.dma_start(out=dst, in_=a[:]),
                 waits=[(xsm, last_x)], incs=[(osm, 16)])
            bm.mark(a, osm.inc(16))
        mp.job(mms, steps, after)

    for tt in range(NT):
        for nsl in range(2):
            mms = [lambda h, b, tt=tt, nsl=nsl: h.matmul(
                b[:], lowfm[0:16, tt * 128:(tt + 1) * 128],
                wgk2b[0:16, nsl * 512:nsl * 512 + 512], start=True, stop=True)]
            def um(a, nsl=nsl):
                return ("vector",
                        lambda h, b, a=a, nsl=nsl: h.tensor_add(
                            a[:], b[:], bgkbc[:, nsl * 512:nsl * 512 + 512]),
                        True)
            gk_job(mms, um,
                   o_gktok[tt * 128:(tt + 1) * 128, nsl * 512:nsl * 512 + 512])

    for nch in range(8):
        for th in range(2):
            mms = [lambda h, b, nch=nch, th=th: h.matmul(
                b[:], wgk2b[0:16, nch * 128:(nch + 1) * 128],
                lowfm[0:16, th * 512:th * 512 + 512], start=True, stop=True)]
            def um(a, nch=nch):
                return ("vector",
                        lambda h, b, a=a, nch=nch: h.tensor_scalar(
                            a[:], b[:], bgkcol[:, nch:nch + 1], None, ALU.add),
                        True)
            gk_job(mms, um,
                   o_gkfm[nch * 128:(nch + 1) * 128, th * 512:th * 512 + 512])

    s.wait("sync", osm, osm.n)
    s.build()
    p.close()
    return p


def bcast_free(ap, n):
    return bass.AP(ap.tensor, ap.offset, list(ap.ap) + [[0, n]])



def build_A2():
    p = Prog()
    nc = p.nc
    i_qfm = p.din("q_fm", [KD, TLOC], BF16)
    i_kfm = p.din("k_fm", [KD, TLOC], BF16)
    i_ktok = p.din("k_tok", [TLOC, KD], BF16)
    i_v = p.din("v_tok", [TLOC, VD], BF16)
    i_g = p.din("g_tok", [TLOC, VD], BF16)
    i_gkfm = p.din("gk_fm", [KD, TLOC], F32)
    i_gktok = p.din("gk_tok", [TLOC, KD], F32)
    i_wo = p.din("wo", [VD * 1, D], F32)
    i_xres = p.din("x_res", [TLOC, D], F32)
    i_normw = p.din("normw_bc", [128, DV], F32)
    i_pairsel = p.din("pairsel", [64, 8], F32)
    i_tri = p.din("tri64", [CHK, CHK], F32)
    i_mask = p.din("mask64", [CHK, CHK], BF16)
    i_rst = p.din("rst2048", [64, 2048], F32)
    i_rst16 = p.din("rst16", [64, 15], F32)
    i_ident = p.din("ident", [128, 128], BF16)

    o_x1 = p.dout("x1", [TLOC, D], F32)
    o_ln1fm = p.dout("ln1_fm", [D, TLOC], BF16)

    cc_in = nc.dram_tensor("cc_in", [64, 16 * DV], F32)
    cc_out = nc.dram_tensor("cc_out", [64 * NCORES, 16 * DV], F32)

    obf = p.sb("obf", [128, 8 * VD], BF16)
    S = p.sb("S", [64, 16 * DV], F32)
    Sbf = p.sb("Sbf", [64, 16 * DV], BF16)
    Sin = p.sb("Sin", [64, 16 * DV], F32)
    Sinbf = p.sb("Sinbf", [64, 16 * DV], BF16)
    gend = p.sb("gend", [64, 16 * NCHK], F32)
    egnd = p.sb("egnd", [64, 16 * NCHK], F32)
    gstb = p.sb("gstb", [64, 16 * NCHK], F32)
    tri = p.sb("tri", [CHK, CHK], F32)
    msk = p.sb("msk", [CHK, CHK], BF16)
    rst = p.sb("rst", [64, 2048], F32)
    rst16 = p.sb("rst16_t", [64, 15], F32)
    psel = p.sb("psel", [64, 8], F32)
    normw = p.sb("normw", [128, DV], F32)
    ident = p.sb("identt", [128, 128], BF16)
    agb = p.sb("agb", [64, 16 * DV], F32)

    qq8 = p.sb("qq8", [64, 16 * 128], BF16)
    kq8 = p.sb("kq8", [64, 16 * 128], BF16)
    gkq8 = p.sb("gkq8", [64, 16 * 128], F32)
    Gq8 = p.sb("Gq8", [64, 16 * 128], F32)
    qts = p.sb("qts", [64, 16 * 128], BF16)
    kts = p.sb("kts", [64, 16 * 128], BF16)
    ktc = p.sb("ktc", [64, KD], BF16)
    vc = p.sb("vc", [64, VD], BF16)
    gktc = p.sb("gktc", [64, KD], F32)
    egmt = p.sb("egmt", [64, KD], F32)
    ktil = p.sb("ktil", [64, KD], BF16)
    atb = [p.sb(f"atb{i}", [64, 64], BF16) for i in range(4)]
    qhb = [p.sb(f"qhb{i}", [64, 64], BF16) for i in range(4)]

    gt = p.sb("gt", [128, VD], BF16)
    sgt = p.sb("sgt", [128, D], F32)
    sqt = p.sb("sqt", [128, D], F32)
    ms = p.sb("ms", [128, 16 * 2], F32)
    xrt = p.sb("xrt", [128, D], F32)
    aot = p.sb("aot", [128, D], F32)
    x1t = p.sb("x1t", [128, D], F32)
    ln1t = p.sb("ln1t", [128, D], BF16)
    ogfm = p.sb("ogfm", [128, ND * 128], BF16)
    wosl = p.sb("wosl0", [128, ND * 512], BF16)
    st8 = p.sb("st8", [128, 16], F32)
    lnao = sgt
    lnscr = sqt

    psA = [p.ps(f"psA{i}", [64, 512], F32) for i in range(2)]
    psO = [p.ps(f"psO{i}", [128, 512], F32) for i in range(2)]
    psS2 = [p.ps(f"psS2{i}", [64, 512], F32) for i in range(2)]
    pstr = p.ps("pstr", [128, 1024], BF16)
    pmm = [p.ps("pmm0", [128, 512], F32)]

    ld = p.sem("ld")
    qld = p.sem("qld")
    sv = p.sem("sv")
    vv = p.sem("vv")
    tsm = p.sem("tsm")
    xsm = p.sem("xsm")
    osm = p.sem("osm")
    ccs = p.sem("ccs")

    s = Sched(p)
    p.enable_consts(s, ld)
    for dst, src_ in ((tri[:], i_tri), (msk[:], i_mask), (rst[:], i_rst),
                      (rst16[:], i_rst16), (psel[:], i_pairsel),
                      (normw[:], i_normw), (ident[:], i_ident)):
        s.dma("sync", dst, src_, ld)
    s.op("vector", lambda h: h.memset(S[:], 0.0))
    s.op("vector", lambda h: h.memset(Sbf[:], 0.0))
    s.op("vector", lambda h: h.memset(Sin[:], 0.0))
    s.wait("sync", ld, ld.n)
    s.build()

    obf3 = obf[:].rearrange("p (c n) -> p c n", n=VD)

    def emit_qkprep(s, e8, with_q=True, with_k=True):
        sl0 = slice(0, 16 * 128)
        c0 = e8 * 128
        wq_ = s.dma("sync", qq8[:].rearrange("p (a t) -> p a t", t=128),
                    i_qfm[:, c0:c0 + 128].rearrange("(a p) t -> p a t", p=64),
                    qld) if with_q else None
        wk_ = s.dma("sync", kq8[:].rearrange("p (a t) -> p a t", t=128),
                    i_kfm[:, c0:c0 + 128].rearrange("(a p) t -> p a t", p=64),
                    qld) if with_k else None
        wg_ = s.dma("sync", gkq8[:].rearrange("p (a t) -> p a t", t=128),
                    i_gkfm[:, c0:c0 + 128].rearrange("(a p) t -> p a t", p=64),
                    qld)
        s.wait("vector", qld, wg_)
        s.op("vector", lambda h: h.tensor_tensor_scan(
            Gq8[:], rst[:], gkq8[:], 0.0, ALU.mult, ALU.add), incs=[(vv, 1)])
        tG = vv.inc()
        s.op("scalar", lambda h: h.activation(gkq8[:], Gq8[:], AF.Exp),
             waits=[(vv, tG)], incs=[(sv, 1)])
        t1 = sv.inc()
        if with_q:
            s.wait("vector", qld, wq_)
            s.op("vector", lambda h: h.scalar_tensor_tensor(
                qts[:], qq8[:], 0.125, gkq8[:], ALU.mult, ALU.mult),
                waits=[(sv, t1)], incs=[(vv, 1)])
        else:
            s.op("vector", lambda h: h.engine_nop(), waits=[(sv, t1)],
                 incs=[(vv, 1)])
        tq = vv.inc()
        if with_k:
            s.op("scalar", lambda h: h.activation(gkq8[:], Gq8[:], AF.Exp,
                                                  scale=-1.0),
                 waits=[(vv, tq)], incs=[(sv, 1)])
            t2 = sv.inc()
            s.wait("vector", qld, wk_)
            s.op("vector", lambda h: h.tensor_mul(kts[:], kq8[:], gkq8[:]),
                 waits=[(sv, t2)], incs=[(vv, 1)])
            vv.inc()
        return vv.n, tG

    for e8 in range(8):
        s = Sched(p)
        c0 = e8 * 128
        prep_v, tG = emit_qkprep(s, e8)
        gsl = Gq8[:]
        gv = bass.AP(gsl.tensor, gsl.offset + 63,
                     [[gsl.ap[0][0], 64], [128, 16], [64, 2]])
        gd = bass.AP(gend[:].tensor, gend[:].offset + 2 * e8,
                     [[gend[:].ap[0][0], 64], [NCHK, 16], [1, 2]])
        ed = bass.AP(egnd[:].tensor, egnd[:].offset + 2 * e8,
                     [[egnd[:].ap[0][0], 64], [NCHK, 16], [1, 2]])
        s.op("vector", lambda h, gv=gv, gd=gd: h.tensor_copy(gd, gv),
             waits=[(vv, prep_v)], incs=[(vv, 1)])
        tgd = vv.inc()
        s.op("scalar", lambda h, gd=gd, ed=ed: h.activation(ed, gd, AF.Exp),
             waits=[(vv, tgd)], incs=[(sv, 1)])
        tegn = sv.inc()
        for e in ("tensor", "vector"):
            s.wait(e, sv, sv.n)
            s.wait(e, vv, vv.n)

        hist = {}
        for cl in range(2):
            cg = 2 * e8 + cl
            rp = cl * 64
            if cl == 1:
                s.wait("sync", tsm, tsm.n)
                s.wait("sync", vv, vv.n)
            wk1 = s.dma("sync", ktc[:],
                        i_ktok[c0 + cl * 64:c0 + cl * 64 + 64, :], qld)
            wv1 = s.dma("sync", vc[:],
                        i_v[c0 + cl * 64:c0 + cl * 64 + 64, :], qld)
            wg1 = s.dma("sync", gktc[:],
                        i_gktok[c0 + cl * 64:c0 + cl * 64 + 64, :], qld)
            s.wait("tensor", qld, wg1)
            for half in range(2):
                s.op("tensor", lambda h, half=half: h.matmul(
                    psS2[0][0:64, :], tri[:],
                    gktc[:, half * 512:half * 512 + 512],
                    start=True, stop=True), incs=[(tsm, 1)])
                tj = tsm.inc()
                s.op("scalar", lambda h, half=half: h.activation(
                    egmt[:, half * 512:half * 512 + 512], psS2[0][0:64, :],
                    AF.Exp, scale=-1.0), waits=[(tsm, tj)], incs=[(sv, 1)])
                tk = sv.inc()
                s.wait("tensor", sv, tk)
            s.wait("vector", qld, wk1)
            s.op("vector", lambda h: h.tensor_mul(ktil[:], ktc[:], egmt[:]),
                 waits=[(sv, sv.n)], incs=[(vv, 1)])
            kt_ready = vv.inc()
            s.wait("tensor", vv, kt_ready)
            s.wait("tensor", qld, wv1)

            for hh in range(16):
                j = cg * 16 + hh
                qsl = qts[0:64, hh * 128 + cl * 64:hh * 128 + cl * 64 + 64]
                ksl = kts[0:64, hh * 128 + cl * 64:hh * 128 + cl * 64 + 64]
                vsl = vc[0:64, hh * DV:hh * DV + DV]
                bA, bO, bS = psA[hh % 2], psO[hh % 2], psS2[hh % 2]
                wA = [(vv, hist[('m', j - 2)])] if ('m', j - 2) in hist else []
                s.op("tensor", lambda h, b=bA, ksl=ksl, qsl=qsl: h.matmul(
                    b[0:64, 0:64], ksl, qsl, start=True, stop=True),
                    waits=wA, incs=[(tsm, 1)])
                t_at = tsm.inc()
                s.op("vector", lambda h, b=bA, j=j: h.tensor_mul(
                    atb[j % 4][:], b[0:64, 0:64], msk[:]),
                    waits=[(tsm, t_at)], incs=[(vv, 1)])
                hist[('m', j)] = t_msk = vv.inc()
                wO = [(vv, t_msk)]
                if ('oe', j - 2) in hist:
                    wO.append((vv, hist[('oe', j - 2)]))
                s.op("tensor", lambda h, b=bO, j=j, vsl=vsl, rp=rp: h.matmul(
                    b[rp:rp + 64, 0:DV], atb[j % 4][:], vsl, start=True,
                    stop=False, tile_position=(0, rp)), waits=wO)
                s.op("tensor", lambda h, b=bO, qsl=qsl, hh=hh, rp=rp: h.matmul(
                    b[rp:rp + 64, 0:DV], qsl, Sbf[:, hh * DV:hh * DV + DV],
                    start=False, stop=True, tile_position=(0, rp)),
                    incs=[(tsm, 1)])
                t_o = tsm.inc()
                wS = []
                if ('su', j - 2) in hist:
                    wS.append((vv, hist[('su', j - 2)]))
                if cl == 0 and hh < 2:
                    wS.append((sv, sv.n))
                s.op("tensor", lambda h, b=bS, hh=hh, vsl=vsl: h.matmul(
                    b[0:64, 0:DV], ktil[:, hh * 64:hh * 64 + 64], vsl,
                    start=True, stop=True), waits=wS, incs=[(tsm, 1)])
                t_s = tsm.inc()
                oc = obf3[rp:rp + 64, cg // 2, hh * DV:hh * DV + DV]
                s.op("vector", lambda h, b=bO, oc=oc, rp=rp: h.tensor_copy(
                    oc, b[rp:rp + 64, 0:DV]), waits=[(tsm, t_o)],
                    incs=[(vv, 1)])
                hist[('oe', j)] = vv.inc()
                egcol = egnd[0:64, hh * NCHK + cg:hh * NCHK + cg + 1]
                s.op("vector", lambda h, b=bS, hh=hh: h.tensor_add(
                    S[:, hh * DV:hh * DV + DV], S[:, hh * DV:hh * DV + DV],
                    b[0:64, 0:DV]), waits=[(tsm, t_s)], incs=[(vv, 1)])
                t_u1 = vv.inc()
                s.op("vector", lambda h, hh=hh, egcol=egcol: h.tensor_scalar(
                    S[:, hh * DV:hh * DV + DV], S[:, hh * DV:hh * DV + DV],
                    egcol, None, ALU.mult), waits=[(vv, t_u1)], incs=[(vv, 1)])
                t_u2 = vv.inc()
                s.op("vector", lambda h, hh=hh: h.tensor_copy(
                    Sbf[:, hh * DV:hh * DV + DV],
                    S[:, hh * DV:hh * DV + DV]), waits=[(vv, t_u2)],
                    incs=[(vv, 1)])
                hist[('su', j)] = vv.inc()
            s.wait("tensor", vv, vv.n)
        s.build()

    s = Sched(p)
    w1 = s.dma("gpsimd", cc_in.ap(), S[:], ccs)
    s.op("gpsimd", lambda h: h.collective_compute(
        "AllGather", ALU.bypass, replica_groups=[list(range(NCORES))],
        ins=[cc_in.ap().opt()], outs=[cc_out.ap().opt()]),
        waits=[(ccs, w1)], incs=[(ccs, 1)])
    wcc = ccs.inc()
    s.wait("sync", ccs, wcc)
    for r in range(NCORES):
        wr = s.dma("sync", agb[:], cc_out.ap()[r * 64:(r + 1) * 64, :], ccs)
        s.wait("vector", ccs, wr)
        s.op("vector", lambda h, r=r: h.scalar_tensor_tensor(
            Sin[:], agb[:], psel[:, r:r + 1], Sin[:], ALU.mult, ALU.add),
            incs=[(vv, 1)])
        wv2 = vv.inc()
        if r < NCORES - 1:
            s.wait("sync", vv, wv2)
    s.op("vector", lambda h: h.tensor_copy(Sinbf[:], Sin[:]),
         waits=[(vv, vv.n)], incs=[(vv, 1)])
    vv.inc()
    for hhh in range(16):
        b0 = hhh * NCHK
        s.op("vector", lambda h, b0=b0: h.memset(gstb[:, b0:b0 + 1], 0.0),
             incs=[(vv, 1)])
        t0 = vv.inc()
        s.op("vector", lambda h, b0=b0: h.tensor_tensor_scan(
            gstb[:, b0 + 1:b0 + NCHK], rst16[:], gend[:, b0:b0 + NCHK - 1],
            0.0, ALU.mult, ALU.add), waits=[(vv, t0)], incs=[(vv, 1)])
        vv.inc()
    s.op("scalar", lambda h: h.activation(gstb[:], gstb[:], AF.Exp),
         waits=[(vv, vv.n)], incs=[(sv, 1)])
    sv.inc()
    s.build()

    for e8 in range(8):
        s = Sched(p)
        if e8 == 0:
            for e in ("vector", "tensor"):
                s.wait(e, sv, sv.n)
        prep_v, _ = emit_qkprep(s, e8, with_q=True, with_k=False)
        s.wait("tensor", vv, prep_v)
        hist = {}
        for cl in range(2):
            cg = 2 * e8 + cl
            rp = cl * 64
            for hh in range(16):
                j = cg * 16 + hh
                wq_ = []
                if ('cm', j - 4) in hist:
                    wq_.append((tsm, hist[('cm', j - 4)]))
                s.op("vector", lambda h, j=j, hh=hh, cl=cl, cg=cg: h.tensor_scalar(
                    qhb[j % 4][:],
                    qts[0:64, hh * 128 + cl * 64:hh * 128 + cl * 64 + 64],
                    gstb[0:64, hh * NCHK + cg:hh * NCHK + cg + 1], None,
                    ALU.mult), waits=wq_, incs=[(vv, 1)])
                t_q = vv.inc()
                wmm = [(vv, t_q)]
                if ('ca', j - 2) in hist:
                    wmm.append((vv, hist[('ca', j - 2)]))
                s.op("tensor", lambda h, j=j, hh=hh, rp=rp: h.matmul(
                    psO[j % 2][rp:rp + 64, 0:DV], qhb[j % 4][:],
                    Sinbf[:, hh * DV:hh * DV + DV], start=True, stop=True,
                    tile_position=(0, rp)), waits=wmm, incs=[(tsm, 1)])
                hist[('cm', j)] = t_m = tsm.inc()
                oc = obf3[rp:rp + 64, cg // 2, hh * DV:hh * DV + DV]
                s.op("vector", lambda h, j=j, oc=oc, rp=rp: h.tensor_add(
                    oc, oc, psO[j % 2][rp:rp + 64, 0:DV]),
                    waits=[(tsm, t_m)], incs=[(vv, 1)])
                hist[('ca', j)] = vv.inc()
        s.build()

    s = Sched(p)
    for e in ("vector", "tensor", "scalar"):
        s.wait(e, vv, vv.n)
    mp = MMPipe(s, pmm, tsm, None, xsm)
    tp_ = MMPipe(s, [pstr], tsm, None, xsm)
    wo_w = [0]
    inv128 = 1.0 / DV
    for tt in range(NT):
        wg_ = s.dma("sync", gt[:], i_g[tt * 128:(tt + 1) * 128, :], qld)
        wx_ = s.dma("sync", xrt[:], i_xres[tt * 128:(tt + 1) * 128, :], qld)
        s.wait("scalar", qld, wg_)
        otile = obf[:, tt * VD * 8 // 8:tt * VD + VD] if False else \
            obf[:, tt * VD:(tt + 1) * VD]
        s.op("scalar", lambda h, otile=otile: h.activation(
            sqt[:, 0:VD], otile, AF.Square), incs=[(sv, 1)])
        t0 = sv.inc()
        s.op("vector", lambda h: h.tensor_reduce(
            ms[:, 0:16], sqt[:, 0:VD].rearrange("p (a b) -> p a b", b=DV),
            AX.X, ALU.add), waits=[(sv, t0)], incs=[(vv, 1)])
        t1 = vv.inc()
        s.op("scalar", lambda h: h.activation(
            ms[:, 16:32], ms[:, 0:16], AF.Ln, bias=p.cc(EPS),
            scale=p.cc(inv128)), waits=[(vv, t1)], incs=[(sv, 1)])
        t2 = sv.inc()
        s.op("scalar", lambda h: h.activation(
            ms[:, 0:16], ms[:, 16:32], AF.Exp, scale=p.cc(-0.5)),
            waits=[(sv, t2)], incs=[(sv, 1)])
        t3 = sv.inc()
        s.op("scalar", lambda h: h.activation(sgt[:, 0:VD], gt[:], AF.Silu),
             incs=[(sv, 1)])
        t4 = sv.inc()
        rstd_bc = bass.AP(ms[:, 0:16].tensor, ms[:, 0:16].offset,
                          [[ms[:, 0:16].ap[0][0], 128], [1, 16], [0, DV]])
        s.op("vector", lambda h, otile=otile, rb=rstd_bc: h.tensor_mul(
            sqt[:, 0:VD].rearrange("p (a b) -> p a b", b=DV),
            otile.rearrange("p (a b) -> p a b", b=DV), rb),
            waits=[(sv, t3)], incs=[(vv, 1)])
        t5 = vv.inc()
        nw_bc = bass.AP(normw[:].tensor, normw[:].offset,
                        [[normw[:].ap[0][0], 128], [0, 16], [1, DV]])
        s.op("vector", lambda h, nb=nw_bc: h.tensor_mul(
            sqt[:, 0:VD].rearrange("p (a b) -> p a b", b=DV),
            sqt[:, 0:VD].rearrange("p (a b) -> p a b", b=DV), nb),
            waits=[(vv, t5)], incs=[(vv, 1)])
        t6 = vv.inc()
        s.op("vector", lambda h: h.tensor_mul(sgt[:, 0:VD], sqt[:, 0:VD],
                                              sgt[:, 0:VD]),
             waits=[(vv, t6), (sv, t4)], incs=[(vv, 1)])
        t7 = vv.inc()
        s.op("vector", lambda h: h.tensor_copy(gt[:], sgt[:, 0:VD]),
             waits=[(vv, t7)], incs=[(vv, 1)])
        t8 = vv.inc()
        s.wait("tensor", vv, t8)
        for dc in range(ND):
            def tfn(h, b, dc=dc):
                return h.transpose(b[:, 0:128], gt[:, dc * 128:(dc + 1) * 128],
                                   ident[:])
            def efn(h, b, dc=dc):
                return h.tensor_copy(ogfm[:, dc * 128:(dc + 1) * 128],
                                     b[:, 0:128])
            tp_.job([tfn], [("vector", efn, True)])
        for nsl in range(4):
            if wo_w[0]:
                s.wait("gpsimd", tsm, wo_w[0])
            wld = s.dma("gpsimd", wosl[:].rearrange("p (c n) -> p c n", n=512),
                        i_wo[:, nsl * 512:(nsl + 1) * 512].rearrange(
                            "(c p) n -> p c n", p=128), ld)
            mms = []
            for dc in range(ND):
                def f(h, b, dc=dc):
                    return h.matmul(b[:], ogfm[:, dc * 128:(dc + 1) * 128],
                                    wosl[:, dc * 512:dc * 512 + 512],
                                    start=(dc == 0), stop=(dc == ND - 1))
                mms.append(f)
            def evf(h, b, nsl=nsl):
                return h.tensor_copy(aot[:, nsl * 512:nsl * 512 + 512], b[:])
            mp.job(mms, [("vector", evf, True)],
                   extra_tensor_waits=[(ld, wld)])
            wo_w[0] = tsm.n
        s.wait("scalar", xsm, xsm.n)
        s.wait("vector", xsm, xsm.n)
        s.wait("vector", qld, wx_)
        emit_ln(s, p, aot[:], lnao[:, 0:D], st8[:, 0:8], sv, vv, D, lnscr[:])
        s.op("vector", lambda h: h.tensor_add(x1t[:], xrt[:], lnao[:, 0:D]),
             waits=[(vv, vv.n)], incs=[(vv, 1)])
        tx1 = vv.inc()
        s.wait("sync", vv, tx1)
        s.dma("sync", o_x1[tt * 128:(tt + 1) * 128, :], x1t[:], osm)
        s.wait("scalar", vv, tx1)
        emit_ln(s, p, x1t[:], ln1t[:], st8[:, 8:16], sv, vv, D, lnscr[:])
        s.wait("tensor", vv, vv.n)
        for dc in range(ND):
            def tfn(h, b, dc=dc):
                return h.transpose(b[:, 0:128], ln1t[:, dc * 128:(dc + 1) * 128],
                                   ident[:])
            def efn(h, b, dc=dc):
                return h.tensor_copy(ogfm[:, dc * 128:(dc + 1) * 128],
                                     b[:, 0:128])
            tp_.job([tfn], [("vector", efn, True)])
        s.wait("sync", xsm, xsm.n)
        s.dma("sync", o_ln1fm[:, tt * 128:(tt + 1) * 128].rearrange(
            "(c p) t -> p c t", p=128),
            ogfm[:].rearrange("p (c t) -> p c t", t=128), osm)
        s.wait("vector", osm, osm.n)
        s.wait("scalar", osm, osm.n)
        s.wait("vector", tsm, tsm.n)
    s.wait("sync", osm, osm.n)
    s.build()
    p.close()
    return p



def build_F1(stop=3):
    import os
    stop = int(os.environ.get("F1_STOP", stop))
    p = Prog()
    nc = p.nc
    TB = 2048
    i_ln = p.din("ln1_fm", [D, TB], BF16)
    i_gwt = p.din("gate_WT", [D, 8], F32)
    i_w1r = p.din("W1r", [6 * D, 1408], F32)
    i_w2r = p.din("W2r", [6 * 1408, D], F32)
    i_b1t = p.din("b1T", [1408, 8], F32)
    i_b2 = p.din("b2rows", [6, D], F32)
    i_sel2 = p.din("sel2", [1, 2], F32)
    i_sel8 = p.din("sel8", [1, 8], F32)
    i_iotap = p.din("iota1e9", [1, 8], F32)
    o_out = p.dout("out_slot", [TB, D], F32)
    o_row = p.dout("row_dbg", [1, 48], F32)
    o_m8 = p.dout("m8_dbg", [1, 8], F32)
    o_idx = p.dout("idx_dbg", [128, 16], U32)
    o_idx2 = p.dout("idx2_dbg", [128, 16], U32)
    o_irep = p.dout("irep_dbg", [128, 1], U32)

    lnb = p.sb("lnb", [128, ND * 1024], BF16)
    gwt = p.sb("gwt", [128, ND * 8], BF16)
    w1b = p.sb("w1b", [128, ND * 1408], BF16)
    w2b = p.sb("w2b", [128, 11 * D], BF16)
    hq = p.sb("hq", [128, 11 * 1024], BF16)
    w1raw = [p.sb(f"w1raw{i}", [128, 1408], F32) for i in range(2)]
    w2raw = [p.sb(f"w2raw{i}", [128, D], F32) for i in range(2)]
    b1t = p.sb("b1t", [128, 11 * 8], F32)
    b1c = p.sb("b1c", [128, 11], F32)
    b2rep = p.sb("b2rep", [128, D], F32)
    sums = p.sb("sums", [128, 16], F32)
    sumsb = p.sb("sumsb", [128, 16], BF16)
    row = p.sb("row", [1, 48], F32)
    row8 = p.sb("row8", [1, 8], F32)
    m8 = p.sb("m8", [1, 8], F32)
    selr = p.sb("selr", [1, 8], F32)
    sel2r = p.sb("sel2r", [1, 2], F32)
    iotar = p.sb("iotar", [1, 8], F32)
    onesc = p.sb("onesc", [1, 128], F32)
    mywc = p.sb("mywc", [128, 1], F32)
    basec = p.sb("basec", [128, 1], F32)
    oh6bc = p.sb("oh6bc", [128, 8], F32)
    iom = p.sb("iom", [128, 16], F32)
    idxf = p.sb("idxf", [128, 16], F32)
    idxu = p.sb("idxu", [128, 16], U32)
    idx2u = p.sb("idx2u", [128, 16], U32)
    idxrep = p.sb("idxrep", [128, 1], U32)
    ev = [p.sb(f"ev{i}", [128, 512], F32) for i in range(8)]
    scr = p.sb("scrF", [128, 2048], F32)

    pmm = [p.ps(f"pmm{i}", [128, 512], F32) for i in range(6)]
    psm = p.ps("psm", [128, 512], F32)

    ld = p.sem("ld")
    gsm = p.sem("gsm")
    sv = p.sem("sv")
    vv = p.sem("vv")
    tsm = p.sem("tsm")
    xsm = p.sem("xsm")
    osm = p.sem("osm")

    s = Sched(p)
    p.enable_consts(s, ld)
    s.dma("gpsimd", gwt[:].rearrange("p (c n) -> p c n", n=8),
          i_gwt.rearrange("(c p) n -> p c n", p=128), ld)
    s.dma("sync", b1t[:].rearrange("p (c n) -> p c n", n=8),
          i_b1t[0:1408, :].rearrange("(c p) n -> p c n", p=128), ld)
    s.dma("sync", selr[:], i_sel8, ld)
    s.dma("sync", sel2r[:], i_sel2, ld)
    s.dma("sync", iotar[:], i_iotap, ld)
    wln = s.dma("sync", lnb[:].rearrange("p (c n) -> p c n", n=1024),
                i_ln[:, 0:1024].rearrange("(c p) n -> p c n", p=128), ld)
    s.op("vector", lambda h: h.memset(onesc[:], 1.0))
    s.op("gpsimd", lambda h: h.iota(iom[:], [[128, 16]], base=0,
                                    channel_multiplier=1,
                                    allow_small_or_imprecise_dtypes=True))
    s.wait("scalar", ld, wln)
    for dc in range(ND):
        s.op("scalar", lambda h, dc=dc: h.activation(
            scr[:, 0:1024], lnb[:, dc * 1024:(dc + 1) * 1024], AF.Copy,
            accum_out=sums[:, dc:dc + 1]),
            incs=[(sv, 1)] if dc == ND - 1 else ())
    t_s1 = sv.inc()
    for dc in range(ND):
        if dc > 0:
            s.wait("sync", sv, sv.n)
        w2_ = s.dma("sync", hq[:, 0:1024],
                    i_ln[dc * 128:(dc + 1) * 128, 1024:2048], ld)
        s.wait("scalar", ld, w2_)
        s.op("scalar", lambda h, dc=dc: h.activation(
            scr[:, 1024:2048], hq[:, 0:1024], AF.Copy,
            accum_out=idxf[:, dc:dc + 1]),
            incs=[(sv, 1)])
        sv.inc()
    s.wait("vector", sv, sv.n)
    s.op("vector", lambda h: h.tensor_add(sums[:], sums[:], idxf[:]),
         incs=[(vv, 1)])
    t_sum = vv.inc()
    s.op("vector", lambda h: h.tensor_copy(sumsb[:], sums[:]),
         waits=[(vv, t_sum)], incs=[(vv, 1)])
    t_sb = vv.inc()
    s.wait("tensor", vv, t_sb)
    s.wait("tensor", ld, 16 * 2)
    for dc in range(ND):
        s.op("tensor", lambda h, dc=dc: h.matmul(
            psm[0:1, 0:8], sumsb[:, dc:dc + 1], gwt[:, dc * 8:dc * 8 + 8],
            start=(dc == 0), stop=(dc == ND - 1)),
            incs=[(tsm, 1)] if dc == ND - 1 else ())
    t_mm = tsm.inc()
    def rop(fn, wait_prev=True):
        s.op("vector", fn, waits=[(vv, vv.n)], incs=[(vv, 1)])
        vv.inc()
    s.op("vector", lambda h: h.tensor_scalar(row[:, 0:8], psm[0:1, 0:8],
                                             1.0 / TB, None, ALU.mult),
         waits=[(tsm, t_mm)], incs=[(vv, 1)])
    vv.inc()
    rop(lambda h: h.scalar_tensor_tensor(row[:, 0:8], selr[:], -1e30,
                                         row[:, 0:8], ALU.mult, ALU.add))
    rop(lambda h: h.reduce_max(row[:, 16:17], row[:, 0:8], AX.X, ALU.max)
        if False else h.reduce_max(row[:, 16:17], row[:, 0:8], AX.X))
    rop(lambda h: h.tensor_scalar(row[:, 8:16], row[:, 0:8], row[:, 16:17],
                                  None, ALU.subtract))
    s.op("scalar", lambda h: h.activation(row[:, 8:16], row[:, 8:16], AF.Exp),
         waits=[(vv, vv.n)], incs=[(sv, 1)])
    sv.inc()
    s.wait("vector", sv, sv.n)
    rop(lambda h: h.tensor_reduce(row[:, 17:18], row[:, 8:16], AX.X, ALU.add))
    rop(lambda h: h.reciprocal(row[:, 18:19], row[:, 17:18]))
    rop(lambda h: h.tensor_scalar(row[:, 8:16], row[:, 8:16], row[:, 18:19],
                                  None, ALU.mult))
    rop(lambda h: h.max(m8[:], row[:, 8:16]))
    rop(lambda h: h.tensor_mul(row[:, 20:22], m8[:, 0:2], sel2r[:]))
    rop(lambda h: h.tensor_reduce(row[:, 22:23], row[:, 20:22], AX.X, ALU.add))
    rop(lambda h: h.tensor_reduce(row[:, 23:24], m8[:, 0:2], AX.X, ALU.add))
    rop(lambda h: h.reciprocal(row[:, 24:25], row[:, 23:24]))
    rop(lambda h: h.tensor_mul(row[:, 25:26], row[:, 22:23], row[:, 24:25]))
    rop(lambda h: h.tensor_scalar(row8[:], row[:, 8:16], m8[:, 0:1], None,
                                  ALU.is_ge))
    rop(lambda h: h.scalar_tensor_tensor(row8[:], row8[:], -1000.0, iotar[:],
                                         ALU.mult, ALU.add))
    rop(lambda h: h.tensor_reduce(row[:, 26:27], row8[:], AX.X, ALU.min))
    rop(lambda h: h.tensor_scalar(row8[:], iotar[:], row[:, 26:27], None,
                                  ALU.subtract))
    rop(lambda h: h.tensor_scalar(row8[:], row8[:], 1000.0, None, ALU.is_equal))
    rop(lambda h: h.scalar_tensor_tensor(row[:, 28:36], row8[:], -2.0,
                                         row[:, 8:16], ALU.mult, ALU.add))
    rop(lambda h: h.tensor_scalar(row[:, 28:36], row[:, 28:36], m8[:, 1:2],
                                  None, ALU.is_ge))
    rop(lambda h: h.scalar_tensor_tensor(row[:, 28:36], row[:, 28:36], -1000.0,
                                         iotar[:], ALU.mult, ALU.add))
    rop(lambda h: h.tensor_reduce(row[:, 27:28], row[:, 28:36], AX.X, ALU.min))
    rop(lambda h: h.tensor_mul(row[:, 36:37], row[:, 26:27], sel2r[:, 0:1]))
    rop(lambda h: h.scalar_tensor_tensor(row[:, 37:38], row[:, 27:28],
                                         1.0, row[:, 36:37], ALU.mult,
                                         ALU.add) if False else
        h.scalar_tensor_tensor(row[:, 37:38], sel2r[:, 1:2], 1.0,
                               row[:, 36:37], ALU.mult, ALU.add))
    rop(lambda h: h.tensor_mul(row[:, 38:39], row[:, 27:28], sel2r[:, 1:2]))
    rop(lambda h: h.tensor_add(row[:, 39:40], row[:, 36:37], row[:, 38:39]))
    rop(lambda h: h.tensor_scalar(row[:, 40:48], iotar[:], 1000.0, None,
                                  ALU.subtract))
    rop(lambda h: h.tensor_scalar(row[:, 40:48], row[:, 40:48],
                                  row[:, 39:40], None, ALU.is_equal))
    t_gate = vv.n

    s.op("vector", lambda h: h.tensor_scalar(row[:, 41 - 41 + 28:29],
                                             row[:, 39:40], float(D), None,
                                             ALU.mult),
         waits=[(vv, t_gate)], incs=[(vv, 1)])
    t_base = vv.inc()
    s.wait("tensor", vv, t_base)
    bc_jobs = [(mywc, row[:, 25:26], 1), (basec, row[:, 28:29], 1),
               (oh6bc, row[:, 40:48], 8)]
    for bi, (dst, srcr, ncol) in enumerate(bc_jobs):
        s.op("tensor", lambda h, srcr=srcr, ncol=ncol: h.matmul(
            psm[0:128, 8:8 + ncol], onesc[:], srcr, start=True, stop=True),
            incs=[(tsm, 1)])
        tb = tsm.inc()
        s.op("vector", lambda h, dst=dst, ncol=ncol: h.tensor_copy(
            dst[:, 0:ncol], psm[0:128, 8:8 + ncol]), waits=[(tsm, tb)],
            incs=[(vv, 1)])
        tv = vv.inc()
        s.wait("tensor", vv, tv)
    s.op("vector", lambda h: h.tensor_scalar(idxf[:], iom[:], basec[:, 0:1],
                                             None, ALU.add),
         waits=[(vv, vv.n)], incs=[(vv, 1)])
    t_i = vv.inc()
    s.op("vector", lambda h: h.tensor_copy(idxu[:], idxf[:]),
         waits=[(vv, t_i)], incs=[(vv, 1)])
    t_iu = vv.inc()
    s.op("vector", lambda h: h.tensor_scalar(idxf[:], row[:, 39:40],
                                             1408.0, None, ALU.mult),
         waits=[(vv, vv.n)], incs=[(vv, 1)]) if False else None
    s.op("vector", lambda h: h.tensor_scalar(
        row[:, 29:30], row[:, 39:40], 1408.0, None, ALU.mult),
        waits=[(vv, t_iu)], incs=[(vv, 1)])
    t_b2 = vv.inc()
    s.op("tensor", lambda h: h.matmul(psm[0:128, 8:9], onesc[:],
                                      row[:, 29:30], start=True, stop=True),
         waits=[(vv, t_b2)], incs=[(tsm, 1)])
    t_mm2 = tsm.inc()
    s.op("vector", lambda h: h.tensor_copy(basec[:], psm[0:128, 8:9]),
         waits=[(tsm, t_mm2)], incs=[(vv, 1)])
    t_b2c = vv.inc()
    s.op("vector", lambda h: h.tensor_scalar(idxf[:], iom[:], basec[:, 0:1],
                                             None, ALU.add),
         waits=[(vv, t_b2c)], incs=[(vv, 1)])
    t_i2 = vv.inc()
    s.op("vector", lambda h: h.tensor_copy(idx2u[:], idxf[:]),
         waits=[(vv, t_i2)], incs=[(vv, 1)])
    t_iu2 = vv.inc()
    for hc in range(11):
        s.op("vector", lambda h, hc=hc: h.tensor_mul(
            scr[:, 0:8], b1t[:, hc * 8:hc * 8 + 8], oh6bc[:]),
            waits=[(vv, vv.n)], incs=[(vv, 1)])
        ta = vv.inc()
        s.op("vector", lambda h, hc=hc: h.tensor_reduce(
            b1c[:, hc:hc + 1], scr[:, 0:8], AX.X, ALU.add),
            waits=[(vv, ta)], incs=[(vv, 1)])
        vv.inc()
    s.op("vector", lambda h: h.tensor_copy(idxrep[:],
                                           basec[:, 0:1]),
         waits=[(vv, t_b2c)], incs=[(vv, 1)]) if False else None
    s.op("tensor", lambda h: h.matmul(psm[0:128, 8:9], onesc[:],
                                      row[:, 39:40], start=True, stop=True),
         waits=[(vv, vv.n)], incs=[(tsm, 1)])
    t_mm3 = tsm.inc()
    s.op("vector", lambda h: h.tensor_copy(basec[:], psm[0:128, 8:9]),
         waits=[(tsm, t_mm3)], incs=[(vv, 1)])
    t_me = vv.inc()
    s.op("vector", lambda h: h.tensor_copy(idxrep[:], basec[:]),
         waits=[(vv, t_me)], incs=[(vv, 1)])
    t_irep = vv.inc()
    s.wait("gpsimd", vv, max(t_iu, t_iu2, t_irep))
    s.wait("sync", vv, vv.n)
    s.dma("sync", o_row, row[:], osm)
    s.dma("sync", o_m8, m8[:], osm)
    s.dma("sync", o_idx, idxu[:], osm)
    s.dma("sync", o_idx2, idx2u[:], osm)
    s.dma("sync", o_irep, idxrep[:], osm)
    s.wait("sync", osm, osm.n)
    if stop == 1:
        s.build()
        p.close()
        return p
    wb2 = s.dma("gpsimd", b2rep[:], None, gsm) if False else None
    s.op("gpsimd", lambda h: h.indirect_dma_start(
        out=b2rep[:], out_offset=None, in_=i_b2,
        in_offset=IndirectOffsetOnAxis(ap=idxrep[:, 0:1], axis=0)),
        incs=[(gsm, 16)])
    s.wait("gpsimd", gsm, gsm.inc(16))
    s.build()

    s = Sched(p)
    for i in range(ND):
        bi = i % 2
        if i >= 2:
            s.wait("gpsimd", vv, gcast[i - 2])
        s.op("gpsimd", lambda h, i=i, bi=bi: h.indirect_dma_start(
            out=w1raw[bi][:], out_offset=None, in_=i_w1r,
            in_offset=IndirectOffsetOnAxis(ap=idxu[:, i:i + 1], axis=0)),
            incs=[(gsm, 16)])
        wg = gsm.inc(16)
        s.wait("gpsimd", gsm, wg)
        s.wait("vector", gsm, wg)
        s.op("vector", lambda h, i=i, bi=bi: h.tensor_copy(
            w1b[:, i * 1408:(i + 1) * 1408], w1raw[bi][:]), incs=[(vv, 1)])
        if i == 0:
            gcast = {}
        gcast[i] = vv.inc()
    for i in range(11):
        bi = i % 2
        if i >= 2:
            s.wait("gpsimd", vv, gcast2[i - 2])
        s.op("gpsimd", lambda h, i=i, bi=bi: h.indirect_dma_start(
            out=w2raw[bi][:], out_offset=None, in_=i_w2r,
            in_offset=IndirectOffsetOnAxis(ap=idx2u[:, i:i + 1], axis=0)),
            incs=[(gsm, 16)])
        wg = gsm.inc(16)
        s.wait("gpsimd", gsm, wg)
        s.wait("vector", gsm, wg)
        s.op("vector", lambda h, i=i, bi=bi: h.tensor_copy(
            w2b[:, i * D:(i + 1) * D], w2raw[bi][:]), incs=[(vv, 1)])
        if i == 0:
            gcast2 = {}
        gcast2[i] = vv.inc()
    s.wait("vector", vv, vv.n)
    s.build()
    if stop == 2:
        p.close()
        return p

    s = Sched(p)
    s.op("vector", lambda h: h.tensor_scalar(b2rep[:], b2rep[:],
                                             mywc[:, 0:1], None, ALU.mult),
         incs=[(vv, 1)])
    vv.inc()
    s.build()

    for half in range(2):
        s = Sched(p)
        if half == 1:
            wlh = s.dma("sync", lnb[:].rearrange("p (c n) -> p c n", n=1024),
                        i_ln[:, 1024:2048].rearrange("(c p) n -> p c n",
                                                     p=128), ld)
            for e in ("tensor",):
                s.wait(e, ld, wlh)
        mp = MMPipe(s, pmm, tsm, None, xsm)
        bm = BufMgr(s)
        stj = [0]
        for hc in range(11):
            for th in range(2):
                mms = []
                for dc in range(ND):
                    def f(h, b, dc=dc, hc=hc, th=th):
                        return h.matmul(
                            b[:], w1b[:, dc * 1408 + hc * 128:
                                      dc * 1408 + hc * 128 + 128],
                            lnb[:, dc * 1024 + th * 512:
                                dc * 1024 + th * 512 + 512],
                            start=(dc == 0), stop=(dc == ND - 1))
                    mms.append(f)
                def evf(h, b, hc=hc, th=th):
                    return h.activation(
                        hq[:, hc * 1024 + th * 512:hc * 1024 + th * 512 + 512],
                        b[:], AF.Gelu, bias=b1c[:, hc:hc + 1])
                mp.job(mms, [("scalar", evf, True)])
        for tt in range(8):
            for osl in range(4):
                mms = []
                for hc in range(11):
                    def f(h, b, hc=hc, tt=tt, osl=osl):
                        return h.matmul(
                            b[:], hq[:, hc * 1024 + tt * 128:
                                     hc * 1024 + tt * 128 + 128],
                            w2b[:, hc * D + osl * 512:hc * D + osl * 512 + 512],
                            start=(hc == 0), stop=(hc == 10))
                    mms.append(f)
                j = stj[0]
                stj[0] += 1
                buf = ev[j % 8]
                bm.guard("vector", buf, osm)
                def evf(h, b, buf=buf, osl=osl):
                    return h.scalar_tensor_tensor(
                        buf[:], b[:], mywc[:, 0:1],
                        b2rep[:, osl * 512:osl * 512 + 512],
                        ALU.mult, ALU.add)
                def after(lx, buf=buf, tt=tt, osl=osl, half=half):
                    s.op("sync", lambda h: h.dma_start(
                        out=o_out[half * 1024 + tt * 128:
                                  half * 1024 + (tt + 1) * 128,
                                  osl * 512:osl * 512 + 512], in_=buf[:]),
                        waits=[(xsm, lx)], incs=[(osm, 16)])
                    bm.mark(buf, osm.inc(16))
                mp.job(mms, [("vector", evf, True)], after)
        s.build()
    p.close()
    return p





def build_F2M1():
    p = Prog()
    nc = p.nc
    i_s0 = p.din("slot0", [TLOC, D], F32)
    i_s1 = p.din("slot1", [TLOC, D], F32)
    i_x1 = p.din("x1", [TLOC, D], F32)
    i_wint = p.din("W_inT", [D, 2 * DIN], F32)
    i_ident = p.din("ident", [128, 128], BF16)
    o_x2 = p.dout("x2", [TLOC, D], F32)
    o_xz = p.dout("xz_fm", [2 * DIN, TLOC], BF16)

    t0 = p.sb("t0", [128, D], F32)
    t1 = p.sb("t1", [128, D], F32)
    x2t = p.sb("x2t", [128, D], F32)
    ln2t = p.sb("ln2t", [128, D], BF16)
    lnscr = p.sb("lnscr", [128, D], F32)
    st8 = p.sb("st8", [128, 16], F32)
    ident = p.sb("identt", [128, 128], BF16)
    ln2fm = p.sb("ln2fm", [128, ND * TLOC], BF16)
    wsl = [p.sb(f"wsl{i}", [128, ND * 512], BF16) for i in range(2)]
    ev = [p.sb(f"ev{i}", [128, 512], BF16) for i in range(8)]

    pstr = p.ps("pstr", [128, 1024], BF16)
    pmm = [p.ps(f"pmm{i}", [128, 512], F32) for i in range(6)]

    ld = p.sem("ld")
    sv = p.sem("sv")
    vv = p.sem("vv")
    tsm = p.sem("tsm")
    xsm = p.sem("xsm")
    osm = p.sem("osm")

    s = Sched(p)
    p.enable_consts(s, ld)
    wid = s.dma("sync", ident[:], i_ident, ld)
    s.wait("tensor", ld, wid)
    s.wait("scalar", ld, 16)
    tp_ = MMPipe(s, [pstr], tsm, None, xsm)
    for tt in range(NT):
        sl = slice(tt * 128, (tt + 1) * 128)
        w0 = s.dma("sync", t0[:], i_s0[sl, :], ld)
        w1 = s.dma("sync", t1[:], i_s1[sl, :], ld)
        wx = s.dma("sync", x2t[:], i_x1[sl, :], ld)
        s.wait("vector", ld, w1)
        s.op("vector", lambda h: h.tensor_add(t0[:], t0[:], t1[:]),
             incs=[(vv, 1)])
        ta = vv.inc()
        s.wait("scalar", vv, ta)
        emit_ln(s, p, t0[:], t1[:], st8[:, 0:8], sv, vv, D, lnscr[:])
        s.wait("vector", ld, wx)
        s.op("vector", lambda h: h.tensor_add(x2t[:], x2t[:], t1[:]),
             waits=[(vv, vv.n)], incs=[(vv, 1)])
        tx2 = vv.inc()
        s.wait("sync", vv, tx2)
        s.dma("sync", o_x2[sl, :], x2t[:], osm)
        s.wait("scalar", vv, tx2)
        emit_ln(s, p, x2t[:], ln2t[:], st8[:, 8:16], sv, vv, D, lnscr[:])
        s.wait("tensor", vv, vv.n)
        for dc in range(ND):
            def tfn(h, b, dc=dc):
                return h.transpose(b[:, 0:128], ln2t[:, dc * 128:(dc + 1) * 128],
                                   ident[:])
            def efn(h, b, dc=dc, tt=tt):
                return h.tensor_copy(
                    ln2fm[:, dc * TLOC + tt * 128:dc * TLOC + (tt + 1) * 128],
                    b[:, 0:128])
            tp_.job([tfn], [("vector", efn, True)])
        s.wait("sync", vv, vv.n)
        s.wait("sync", xsm, xsm.n)
    s.build()

    s = Sched(p)
    mp = MMPipe(s, pmm, tsm, None, xsm)
    bm = BufMgr(s)
    stj = [0]
    wprev = [0, 0]
    for slab in range(16):
        bi = slab % 2
        if wprev[bi]:
            s.wait("gpsimd", tsm, wprev[bi])
        wld = s.dma("gpsimd", wsl[bi][:].rearrange("p (c n) -> p c n", n=512),
                    i_wint[:, slab * 512:(slab + 1) * 512].rearrange(
                        "(c p) n -> p c n", p=128), ld)
        first = True
        for nch in range(4):
            for th in range(2):
                mms = []
                for dc in range(ND):
                    def f(h, b, dc=dc, nch=nch, th=th, bi=bi):
                        return h.matmul(
                            b[:],
                            wsl[bi][:, dc * 512 + nch * 128:dc * 512 + nch * 128 + 128],
                            ln2fm[:, dc * TLOC + th * 512:dc * TLOC + th * 512 + 512],
                            start=(dc == 0), stop=(dc == ND - 1))
                    mms.append(f)
                j = stj[0]
                stj[0] += 1
                buf = ev[j % 8]
                bm.guard("vector", buf, osm)
                def evf(h, b, buf=buf):
                    return h.tensor_copy(buf[:], b[:])
                def after(lx, buf=buf, slab=slab, nch=nch, th=th):
                    s.op("sync", lambda h: h.dma_start(
                        out=o_xz[slab * 512 + nch * 128:slab * 512 + (nch + 1) * 128,
                                 th * 512:th * 512 + 512], in_=buf[:]),
                        waits=[(xsm, lx)], incs=[(osm, 16)])
                    bm.mark(buf, osm.inc(16))
                mp.job(mms, [("vector", evf, True)], after,
                       extra_tensor_waits=[(ld, wld)] if first else ())
                first = False
        wprev[bi] = tsm.n
    s.wait("sync", osm, osm.n)
    s.build()
    p.close()
    return p



def build_M2():
    p = Prog()
    nc = p.nc
    i_xh = p.din("xh_fm", [DIN, TLOC + 3], BF16)
    i_cw = p.din("conv_w", [128, 32 * 4], F32)
    i_cb = p.din("conv_b", [128, 32], F32)
    i_xpt = p.din("x_projT", [DIN, 160], F32)
    i_dpt = p.din("dt_projT", [128, DIN], F32)
    i_dpb2 = p.din("dpb2", [128, 32], F32)
    o_xc = p.dout("xc_fm", [DIN, TLOC], F32)
    o_dt = p.dout("dt_fm", [DIN, TLOC], F32)
    o_bc = p.dout("bc_rows", [32, TLOC], F32)

    xht = p.sb("xht", [128, TLOC + 3], BF16)
    cw = p.sb("cw", [128, 32 * 4], F32)
    cb = p.sb("cb", [128, 32], F32)
    acc = p.sb("acc", [128, TLOC], F32)
    xcf = p.sb("xcf", [128, TLOC], F32)
    xcb = p.sb("xcb", [128, 32 * TLOC], BF16)
    xpt = p.sb("xpt", [128, 32 * 160], BF16)
    dpt = p.sb("dpt", [128, DIN], BF16)
    dpb = p.sb("dpb", [128, 32], F32)
    dtlow = p.sb("dtlow", [128, TLOC], BF16)
    u_ = p.sb("u_", [128, 512], F32)
    uc_ = p.sb("uc_", [128, 512], F32)
    ev = [p.sb(f"ev{i}", [128, 512], F32) for i in range(4)]

    pmm = [p.ps(f"pmm{i}", [128, 512], F32) for i in range(6)]

    ld = p.sem("ld")
    sv = p.sem("sv")
    vv = p.sem("vv")
    tsm = p.sem("tsm")
    xsm = p.sem("xsm")
    osm = p.sem("osm")

    s = Sched(p)
    p.enable_consts(s, ld)
    s.dma("sync", cw[:], i_cw, ld)
    wcb = s.dma("sync", cb[:], i_cb, ld)
    s.dma("gpsimd", xpt[:].rearrange("p (c n) -> p c n", n=160),
          i_xpt.rearrange("(c p) n -> p c n", p=128), ld)
    wdp = s.dma("gpsimd", dpt[:], i_dpt, ld)
    s.dma("sync", dpb[:], i_dpb2, ld)
    s.wait("scalar", ld, ld.n)
    s.wait("vector", ld, ld.n)
    for dc in range(32):
        wx = s.dma("sync", xht[:], i_xh[dc * 128:(dc + 1) * 128, :], ld)
        s.wait("vector", ld, wx)
        s.op("vector", lambda h, dc=dc: h.tensor_scalar(
            acc[:], xht[:, 0:TLOC], cw[:, 4 * dc:4 * dc + 1], None, ALU.mult),
            incs=[(vv, 1)])
        tv = vv.inc()
        for j in range(1, 4):
            s.op("vector", lambda h, dc=dc, j=j: h.scalar_tensor_tensor(
                acc[:], xht[:, j:j + TLOC], cw[:, 4 * dc + j:4 * dc + j + 1],
                acc[:], ALU.mult, ALU.add), waits=[(vv, tv)], incs=[(vv, 1)])
            tv = vv.inc()
        s.op("scalar", lambda h, dc=dc: h.activation(
            xcf[:], acc[:], AF.Silu, bias=cb[:, dc:dc + 1]),
            waits=[(vv, tv)], incs=[(sv, 1)])
        ts_ = sv.inc()
        s.op("vector", lambda h, dc=dc: h.tensor_copy(
            xcb[:, dc * TLOC:(dc + 1) * TLOC], xcf[:]), waits=[(sv, ts_)],
            incs=[(vv, 1)])
        tv2 = vv.inc()
        s.wait("sync", sv, ts_)
        s.dma("sync", o_xc[dc * 128:(dc + 1) * 128, :], xcf[:], osm)
        s.wait("sync", vv, tv)
        s.wait("scalar", osm, osm.n)
    s.build()

    s = Sched(p)
    mp = MMPipe(s, pmm, tsm, None, xsm)
    for nch in range(2):
        ncols = 128 if nch == 0 else 32
        for th in range(2):
            mms = []
            for dc in range(32):
                def f(h, b, dc=dc, nch=nch, th=th, ncols=ncols):
                    return h.matmul(
                        b[0:ncols, :],
                        xpt[:, dc * 160 + nch * 128:dc * 160 + nch * 128 + ncols],
                        xcb[:, dc * TLOC + th * 512:dc * TLOC + th * 512 + 512],
                        start=(dc == 0), stop=(dc == 31))
                mms.append(f)
            if nch == 0:
                def evf(h, b, th=th):
                    return h.tensor_copy(dtlow[:, th * 512:th * 512 + 512],
                                         b[0:128, :])
            else:
                def evf(h, b, th=th):
                    return h.tensor_copy(ev[th][:, 0:512][0:32, :], b[0:32, :])
            def after(lx, nch=nch, th=th):
                if nch == 1:
                    s.op("sync", lambda h, th=th: h.dma_start(
                        out=o_bc[:, th * 512:th * 512 + 512],
                        in_=ev[th][0:32, 0:512]),
                        waits=[(xsm, lx)], incs=[(osm, 16)])
                    osm.inc(16)
            mp.job(mms, [("vector", evf, True)], after)
    for dc in range(32):
        for th in range(2):
            mms = [lambda h, b, dc=dc, th=th: h.matmul(
                b[:], dpt[:, dc * 128:(dc + 1) * 128],
                dtlow[:, th * 512:th * 512 + 512], start=True, stop=True)]
            steps = [
                ("vector", lambda h, b, dc=dc: h.tensor_scalar(
                    u_[:], b[:], dpb[:, dc:dc + 1], None, ALU.add), True),
                ("vector", lambda h, b: h.tensor_scalar(
                    uc_[:], u_[:], 70.0, None, ALU.min), False),
                ("scalar", lambda h, b: h.activation(uc_[:], uc_[:], AF.Exp),
                 False),
                ("scalar", lambda h, b: h.activation(uc_[:], uc_[:], AF.Ln,
                                                     bias=1.0), False),
                ("vector", lambda h, b, dc=dc: h.tensor_max(
                    u_[:], uc_[:], u_[:]), False),
            ]
            def after(lx, dc=dc, th=th):
                s.op("sync", lambda h, dc=dc, th=th: h.dma_start(
                    out=o_dt[dc * 128:(dc + 1) * 128, th * 512:th * 512 + 512],
                    in_=u_[:]), waits=[(xsm, lx)], incs=[(osm, 16)])
                osm.inc(16)
                s.wait("vector", osm, osm.n)
            mp.job(mms, steps, after)
    s.wait("sync", osm, osm.n)
    s.build()
    p.close()
    return p



def build_M3():
    p = Prog()
    nc = p.nc
    TF = 2048
    CK = 256
    NCK = TF // CK
    i_dt = p.din("dt", [2048, TF], F32)
    i_xc = p.din("xc", [2048, TF], F32)
    i_z = p.din("z_fm", [2048, TF], BF16)
    i_B = p.din("Brows", [16, TF], F32)
    i_C = p.din("Crows", [16, TF], F32)
    i_al = p.din("A_log", [128, 16 * 16], F32)
    i_D = p.din("Dcol", [128, 16], F32)
    i_wot = p.din("WoutT_sl", [2048, D], F32)
    o_part = p.dout("part", [TF, D], F32)

    dtq = p.sb("dtq", [128, 16 * CK], F32)
    xcq = p.sb("xcq", [128, 16 * CK], F32)
    dxq = p.sb("dxq", [128, 16 * CK], F32)
    zq = p.sb("zq", [128, 16 * CK], BF16)
    yq = p.sb("yq", [128, 16 * CK], F32)
    yzb = p.sb("yzb", [128, 16 * CK], BF16)
    Bbc = p.sb("Bbc", [128, 16 * CK], BF16)
    Cbc = p.sb("Cbc", [128, 16 * CK], BF16)
    Brow = p.sb("Brow", [1, 16 * CK], F32)
    Crow = p.sb("Crow", [1, 16 * CK], F32)
    onesc = p.sb("onesc", [1, 128], F32)
    acol = p.sb("acol", [128, 16 * 16], F32)
    aq = p.sb("aq", [128, CK], F32)
    bq = p.sb("bq", [128, CK], F32)
    hq = p.sb("hq", [128, CK], F32)
    Hc = p.sb("Hc", [128, 16 * 16], F32)
    Dc = p.sb("Dc", [128, 16], F32)
    wsl = p.sb("wsl0", [128, 16 * 512], BF16)
    ev = [p.sb(f"ev{i}", [128, 512], F32) for i in range(6)]

    pmm = [p.ps(f"pmm{i}", [128, 512], F32) for i in range(6)]

    ld = p.sem("ld")
    sv = p.sem("sv")
    vv = p.sem("vv")
    tsm = p.sem("tsm")
    xsm = p.sem("xsm")
    osm = p.sem("osm")

    s = Sched(p)
    p.enable_consts(s, ld)
    wal = s.dma("sync", acol[:], i_al, ld)
    s.dma("sync", Dc[:], i_D, ld)
    s.op("vector", lambda h: h.memset(onesc[:], 1.0))
    s.op("vector", lambda h: h.memset(Hc[:], 0.0))
    s.wait("scalar", ld, wal)
    s.op("scalar", lambda h: h.activation(acol[:], acol[:], AF.Exp),
         incs=[(sv, 1)])
    ts0 = sv.inc()
    s.op("vector", lambda h: h.tensor_scalar(acol[:], acol[:], -1.0, None,
                                             ALU.mult), waits=[(sv, ts0)],
         incs=[(vv, 1)])
    vv.inc()
    s.build()

    for ck in range(NCK):
        s = Sched(p)
        c0 = ck * CK
        wdt = s.dma("sync", dtq[:].rearrange("p (c n) -> p c n", n=CK),
                    i_dt[:, c0:c0 + CK].rearrange("(c p) n -> p c n", p=128), ld)
        wxc = s.dma("sync", xcq[:].rearrange("p (c n) -> p c n", n=CK),
                    i_xc[:, c0:c0 + CK].rearrange("(c p) n -> p c n", p=128), ld)
        wz = s.dma("sync", zq[:].rearrange("p (c n) -> p c n", n=CK),
                   i_z[:, c0:c0 + CK].rearrange("(c p) n -> p c n", p=128), ld)
        wB = s.dma("sync", Brow[:].rearrange("p (a n) -> p a n", n=CK),
                   i_B[:, c0:c0 + CK].rearrange("(p a) n -> p a n", p=1), ld)
        wC = s.dma("sync", Crow[:].rearrange("p (a n) -> p a n", n=CK),
                   i_C[:, c0:c0 + CK].rearrange("(p a) n -> p a n", p=1), ld)
        s.wait("tensor", ld, wC)
        mpb = MMPipe(s, pmm[0:3], tsm, None, xsm)
        for sx in range(16):
            for dst, rowt in ((Bbc, Brow), (Cbc, Crow)):
                def f(h, b, sx=sx, rowt=rowt):
                    return h.matmul(b[0:128, 0:CK], onesc[:],
                                    rowt[0:1, sx * CK:(sx + 1) * CK],
                                    start=True, stop=True)
                def e(h, b, sx=sx, dst=dst):
                    return h.tensor_copy(dst[:, sx * CK:(sx + 1) * CK],
                                         b[0:128, 0:CK])
                mpb.job([f], [("vector", e, True)])
        s.wait("vector", ld, wxc)
        s.op("vector", lambda h: h.tensor_mul(dxq[:], dtq[:], xcq[:]),
             waits=[(vv, vv.n)], incs=[(vv, 1)])
        vv.inc()
        s.wait("scalar", ld, wdt)
        s.wait("scalar", vv, vv.n)
        s.wait("vector", xsm, xsm.n)
        for dtl in range(16):
            sl = slice(dtl * CK, (dtl + 1) * CK)
            for sx in range(16):
                ssl = slice(sx * CK, (sx + 1) * CK)
                hc = Hc[:, dtl * 16 + sx:dtl * 16 + sx + 1]
                s.op("scalar", lambda h, sl=sl, dtl=dtl, sx=sx: h.activation(
                    aq[:], dtq[:, sl], AF.Exp,
                    scale=acol[:, dtl * 16 + sx:dtl * 16 + sx + 1]),
                    waits=[(vv, vv.n)], incs=[(sv, 1)])
                t_a = sv.inc()
                s.op("vector", lambda h, sl=sl, ssl=ssl: h.tensor_mul(
                    bq[:], dxq[:, sl], Bbc[:, ssl]),
                    waits=[(vv, vv.n)], incs=[(vv, 1)])
                t_b = vv.inc()
                s.op("vector", lambda h, hc=hc: h.tensor_tensor_scan(
                    hq[:], aq[:], bq[:], hc, ALU.mult, ALU.add),
                    waits=[(vv, t_b), (sv, t_a)], incs=[(vv, 1)])
                t_h = vv.inc()
                s.op("vector", lambda h, hc=hc: h.tensor_copy(
                    hq[:, CK - 1:CK], hq[:, CK - 1:CK]) if False else
                    h.tensor_copy(hc, hq[:, CK - 1:CK]),
                    waits=[(vv, t_h)], incs=[(vv, 1)])
                t_hc = vv.inc()
                if sx == 0:
                    s.op("vector", lambda h, sl=sl, ssl=ssl: h.tensor_mul(
                        yq[:, sl], hq[:], Cbc[:, ssl]),
                        waits=[(vv, t_hc)], incs=[(vv, 1)])
                    vv.inc()
                else:
                    s.op("vector", lambda h, ssl=ssl: h.tensor_mul(
                        bq[:], hq[:], Cbc[:, ssl]),
                        waits=[(vv, t_hc)], incs=[(vv, 1)])
                    tm = vv.inc()
                    s.op("vector", lambda h, sl=sl: h.tensor_add(
                        yq[:, sl], yq[:, sl], bq[:]), waits=[(vv, tm)],
                        incs=[(vv, 1)])
                    vv.inc()
            s.op("vector", lambda h, sl=sl, dtl=dtl: h.scalar_tensor_tensor(
                yq[:, sl], xcq[:, sl], Dc[:, dtl:dtl + 1], yq[:, sl],
                ALU.mult, ALU.add), waits=[(vv, vv.n)], incs=[(vv, 1)])
            td = vv.inc()
            s.wait("vector", ld, wz)
            s.op("vector", lambda h, sl=sl: h.tensor_mul(
                yzb[:, sl], yq[:, sl], zq[:, sl]), waits=[(vv, td)],
                incs=[(vv, 1)])
            vv.inc()
        s.wait("tensor", vv, vv.n)
        mp = MMPipe(s, pmm, tsm, None, xsm)
        bm = BufMgr(s)
        stj = [0]
        wpr = [0]
        for osl in range(4):
            if wpr[0]:
                s.wait("gpsimd", tsm, wpr[0])
            wld = s.dma("gpsimd", wsl[:].rearrange("p (c n) -> p c n", n=512),
                        i_wot[:, osl * 512:(osl + 1) * 512].rearrange(
                            "(c p) n -> p c n", p=128), ld)
            first = True
            for ts128 in range(2):
                mms = []
                for dtl in range(16):
                    def f(h, b, dtl=dtl, ts128=ts128):
                        return h.matmul(
                            b[:], yzb[:, dtl * CK + ts128 * 128:
                                      dtl * CK + ts128 * 128 + 128],
                            wsl[:, dtl * 512:dtl * 512 + 512],
                            start=(dtl == 0), stop=(dtl == 15))
                    mms.append(f)
                j = stj[0]
                stj[0] += 1
                buf = ev[j % 6]
                bm.guard("vector", buf, osm)
                def evf(h, b, buf=buf):
                    return h.tensor_copy(buf[:], b[:])
                def after(lx, buf=buf, osl=osl, ts128=ts128, c0=c0):
                    s.op("sync", lambda h: h.dma_start(
                        out=o_part[c0 + ts128 * 128:c0 + (ts128 + 1) * 128,
                                   osl * 512:osl * 512 + 512], in_=buf[:]),
                        waits=[(xsm, lx)], incs=[(osm, 16)])
                    bm.mark(buf, osm.inc(16))
                mp.job(mms, [("vector", evf, True)], after,
                       extra_tensor_waits=[(ld, wld)] if first else ())
                first = False
            wpr[0] = tsm.n
        s.wait("sync", vv, vv.n)
        s.wait("sync", sv, sv.n)
        s.wait("sync", tsm, tsm.n)
        if ck == NCK - 1:
            s.wait("sync", osm, osm.n)
        s.build()
    p.close()
    return p



def build_M5():
    p = Prog()
    nc = p.nc
    i_p0 = p.din("part0", [TLOC, D], F32)
    i_p1 = p.din("part1", [TLOC, D], F32)
    i_x2 = p.din("x2", [TLOC, D], F32)
    i_ident = p.din("ident", [128, 128], BF16)
    o_x3 = p.dout("x3", [TLOC, D], F32)
    o_ln3 = p.dout("ln3_tok", [TLOC, D], BF16)
    o_ln3fm = p.dout("ln3_fm", [D, TLOC], BF16)

    t0 = p.sb("t0", [128, D], F32)
    t1 = p.sb("t1", [128, D], F32)
    x3t = p.sb("x3t", [128, D], F32)
    ln3t = p.sb("ln3t", [128, D], BF16)
    lnscr = p.sb("lnscr", [128, D], F32)
    st8 = p.sb("st8", [128, 16], F32)
    ident = p.sb("identt", [128, 128], BF16)
    ogfm = p.sb("ogfm", [128, ND * 128], BF16)
    pstr = p.ps("pstr", [128, 1024], BF16)

    ld = p.sem("ld")
    sv = p.sem("sv")
    vv = p.sem("vv")
    tsm = p.sem("tsm")
    xsm = p.sem("xsm")
    osm = p.sem("osm")

    s = Sched(p)
    p.enable_consts(s, ld)
    wid = s.dma("sync", ident[:], i_ident, ld)
    s.wait("tensor", ld, wid)
    s.wait("scalar", ld, 16)
    tp_ = MMPipe(s, [pstr], tsm, None, xsm)
    for tt in range(NT):
        sl = slice(tt * 128, (tt + 1) * 128)
        w0 = s.dma("sync", t0[:], i_p0[sl, :], ld)
        w1 = s.dma("sync", t1[:], i_p1[sl, :], ld)
        wx = s.dma("sync", x3t[:], i_x2[sl, :], ld)
        s.wait("vector", ld, w1)
        s.op("vector", lambda h: h.tensor_add(t0[:], t0[:], t1[:]),
             incs=[(vv, 1)])
        ta = vv.inc()
        s.wait("scalar", vv, ta)
        emit_ln(s, p, t0[:], t1[:], st8[:, 0:8], sv, vv, D, lnscr[:])
        s.wait("vector", ld, wx)
        s.op("vector", lambda h: h.tensor_add(x3t[:], x3t[:], t1[:]),
             waits=[(vv, vv.n)], incs=[(vv, 1)])
        tx3 = vv.inc()
        s.wait("sync", vv, tx3)
        s.dma("sync", o_x3[sl, :], x3t[:], osm)
        s.wait("scalar", vv, tx3)
        emit_ln(s, p, x3t[:], ln3t[:], st8[:, 8:16], sv, vv, D, lnscr[:])
        tln = vv.n
        s.wait("sync", vv, tln)
        s.dma("sync", o_ln3[sl, :], ln3t[:], osm)
        s.wait("tensor", vv, tln)
        for dc in range(ND):
            def tfn(h, b, dc=dc):
                return h.transpose(b[:, 0:128], ln3t[:, dc * 128:(dc + 1) * 128],
                                   ident[:])
            def efn(h, b, dc=dc):
                return h.tensor_copy(ogfm[:, dc * 128:(dc + 1) * 128],
                                     b[:, 0:128])
            tp_.job([tfn], [("vector", efn, True)])
        s.wait("sync", xsm, xsm.n)
        s.dma("sync", o_ln3fm[:, sl].rearrange("(c p) t -> p c t", p=128),
              ogfm[:].rearrange("p (c t) -> p c t", t=128), osm)
        s.wait("sync", osm, osm.n)
        s.wait("vector", osm, osm.n)
    s.build()
    p.close()
    return p



def build_G1():
    p = Prog()
    nc = p.nc
    i_ln3fm = p.din("ln3_fm", [D, TLOC], BF16)
    i_wg = p.din("w_gate", [D, 64], F32)
    i_wsu = p.din("w_shared_up", [D, 1024], F32)
    i_wsd = p.din("w_shared_down", [512, D], F32)
    i_iota64 = p.din("iota64_bc", [128, 64], F32)
    o_sh = p.dout("sharedout", [TLOC, D], F32)
    o_m = p.dout("mmat", [TLOC, 64], F32)
    o_widx = p.dout("widx", [TLOC, 8], F32)
    o_wn = p.dout("wnorm", [TLOC, 8], F32)
    o_sc = p.dout("sc_dbg", [TLOC, 64], F32)

    lnfm = p.sb("lnfm", [128, ND * TLOC], BF16)
    wgb = p.sb("wgb", [128, ND * 64], BF16)
    wsub = p.sb("wsub", [128, ND * 1024], BF16)
    wsdb = p.sb("wsdb", [128, 4 * D], BF16)
    sufm = p.sb("sufm", [128, 8 * TLOC], F32)
    hsh = p.sb("hsh", [128, 4 * TLOC], BF16)
    sc = p.sb("sc", [128, 64], F32)
    scw = p.sb("scw", [128, 64], F32)
    m8t = p.sb("m8t", [128, 8], F32)
    macc = p.sb("macc", [128, 64], F32)
    widxt = p.sb("widxt", [128, 8], F32)
    wnt = p.sb("wnt", [128, 8], F32)
    io64 = p.sb("io64", [128, 64], F32)
    r1 = p.sb("r1", [128, 64], F32)
    r2 = p.sb("r2", [128, 1], F32)
    ev = [p.sb(f"ev{i}", [128, 512], F32) for i in range(6)]

    pmm = [p.ps(f"pmm{i}", [128, 512], F32) for i in range(6)]

    ld = p.sem("ld")
    sv = p.sem("sv")
    vv = p.sem("vv")
    tsm = p.sem("tsm")
    xsm = p.sem("xsm")
    osm = p.sem("osm")

    s = Sched(p)
    p.enable_consts(s, ld)
    s.dma("gpsimd", wgb[:].rearrange("p (c n) -> p c n", n=64),
          i_wg.rearrange("(c p) n -> p c n", p=128), ld)
    s.dma("gpsimd", wsub[:].rearrange("p (c n) -> p c n", n=1024),
          i_wsu.rearrange("(c p) n -> p c n", p=128), ld)
    s.dma("gpsimd", wsdb[:].rearrange("p (c n) -> p c n", n=D),
          i_wsd.rearrange("(c p) n -> p c n", p=128), ld)
    wio = s.dma("sync", io64[:], i_iota64, ld)
    wln = s.dma("sync", lnfm[:].rearrange("p (c n) -> p c n", n=TLOC),
                i_ln3fm.rearrange("(c p) n -> p c n", p=128), ld)
    s.wait("tensor", ld, ld.n)
    s.wait("vector", ld, ld.n)
    s.wait("scalar", ld, 16)

    mp = MMPipe(s, pmm, tsm, None, xsm)
    bm = BufMgr(s)
    for nch in range(8):
        for th in range(2):
            mms = []
            for dc in range(ND):
                def f(h, b, dc=dc, nch=nch, th=th):
                    return h.matmul(
                        b[:], wsub[:, dc * 1024 + nch * 128:
                                   dc * 1024 + nch * 128 + 128],
                        lnfm[:, dc * TLOC + th * 512:dc * TLOC + th * 512 + 512],
                        start=(dc == 0), stop=(dc == ND - 1))
                mms.append(f)
            def evf(h, b, nch=nch, th=th):
                return h.tensor_copy(
                    sufm[:, nch * TLOC + th * 512:nch * TLOC + th * 512 + 512],
                    b[:])
            mp.job(mms, [("vector", evf, True)])
    s.wait("scalar", xsm, xsm.n)
    for u in range(4):
        s.op("scalar", lambda h, u=u: h.activation(
            sufm[:, (4 + u) * TLOC:(5 + u) * TLOC],
            sufm[:, (4 + u) * TLOC:(5 + u) * TLOC], AF.Silu), incs=[(sv, 1)])
        tsx = sv.inc()
        s.op("vector", lambda h, u=u: h.tensor_mul(
            hsh[:, u * TLOC:(u + 1) * TLOC], sufm[:, (4 + u) * TLOC:(5 + u) * TLOC],
            sufm[:, u * TLOC:(u + 1) * TLOC]), waits=[(sv, tsx)],
            incs=[(vv, 1)])
        vv.inc()
    s.wait("tensor", vv, vv.n)
    stj = [0]
    for tt in range(NT):
        for osl in range(4):
            mms = []
            for hc in range(4):
                def f(h, b, hc=hc, tt=tt, osl=osl):
                    return h.matmul(
                        b[:], hsh[:, hc * TLOC + tt * 128:hc * TLOC + tt * 128 + 128],
                        wsdb[:, hc * D + osl * 512:hc * D + osl * 512 + 512],
                        start=(hc == 0), stop=(hc == 3))
                mms.append(f)
            j = stj[0]
            stj[0] += 1
            buf = ev[j % 6]
            bm.guard("vector", buf, osm)
            def evf(h, b, buf=buf):
                return h.tensor_copy(buf[:], b[:])
            def after(lx, buf=buf, tt=tt, osl=osl):
                s.op("sync", lambda h: h.dma_start(
                    out=o_sh[tt * 128:(tt + 1) * 128, osl * 512:osl * 512 + 512],
                    in_=buf[:]), waits=[(xsm, lx)], incs=[(osm, 16)])
                bm.mark(buf, osm.inc(16))
            mp.job(mms, [("vector", evf, True)], after)
    sc_guard = [0, 0]
    for tt in range(NT):
        mms = []
        for dc in range(ND):
            def f(h, b, dc=dc, tt=tt):
                return h.matmul(
                    b[0:128, 0:64], wgb[:, dc * 64:(dc + 1) * 64],
                    lnfm[:, dc * TLOC + tt * 128:dc * TLOC + tt * 128 + 128],
                    start=(dc == 0), stop=(dc == ND - 1))
            mms.append(f)
        mms = []
        for dc in range(ND):
            def f(h, b, dc=dc, tt=tt):
                return h.matmul(
                    b[0:128, 0:64],
                    lnfm[:, dc * TLOC + tt * 128:dc * TLOC + tt * 128 + 128],
                    wgb[:, dc * 64:(dc + 1) * 64],
                    start=(dc == 0), stop=(dc == ND - 1))
            mms.append(f)
        steps = [("scalar",
                  lambda h, b: h.activation(sc[:], b[0:128, 0:64], AF.Sigmoid),
                  True)]
        etw = []
        if sc_guard[0]:
            etw = [(vv, sc_guard[0]), (osm, sc_guard[1])]
        xv = mp.job(mms, steps, extra_tensor_waits=etw)
        s.wait("sync", xsm, xv)
        s.dma("sync", o_sc[tt * 128:(tt + 1) * 128, :], sc[:], osm)
        s.op("vector", lambda h: h.tensor_copy(scw[:], sc[:]),
             waits=[(xsm, xv)], incs=[(vv, 1)])
        tw = vv.inc()
        sc_guard = [tw, osm.n]
        s.op("vector", lambda h: h.memset(macc[:], 0.0), incs=[(vv, 1)])
        tm0 = vv.inc()
        s.op("vector", lambda h: h.max(m8t[:], scw[:]),
             waits=[(vv, tw)], incs=[(vv, 1)])
        tmx = vv.inc()
        for k in range(6):
            s.op("vector", lambda h, k=k: h.tensor_scalar(
                r1[:], scw[:], m8t[:, k:k + 1], None, ALU.is_ge),
                waits=[(vv, vv.n)], incs=[(vv, 1)])
            ta = vv.inc()
            s.op("vector", lambda h: h.scalar_tensor_tensor(
                r1[:], r1[:], -1000.0, io64[:], ALU.mult, ALU.add),
                waits=[(vv, ta)], incs=[(vv, 1)])
            tb = vv.inc()
            s.op("vector", lambda h, k=k: h.tensor_reduce(
                widxt[:, k:k + 1], r1[:], AX.X, ALU.min),
                waits=[(vv, tb)], incs=[(vv, 1)])
            tc = vv.inc()
            s.op("vector", lambda h, k=k: h.tensor_scalar(
                r1[:], io64[:], widxt[:, k:k + 1], None, ALU.subtract),
                waits=[(vv, tc)], incs=[(vv, 1)])
            td = vv.inc()
            s.op("vector", lambda h: h.tensor_scalar(
                r1[:], r1[:], 1000.0, None, ALU.is_equal),
                waits=[(vv, td)], incs=[(vv, 1)])
            te = vv.inc()
            s.op("vector", lambda h: h.tensor_add(macc[:], macc[:], r1[:]),
                 waits=[(vv, te), (vv, tm0)], incs=[(vv, 1)])
            tf_ = vv.inc()
            s.op("vector", lambda h: h.scalar_tensor_tensor(
                scw[:], r1[:], -2.0, scw[:], ALU.mult, ALU.add),
                waits=[(vv, tf_)], incs=[(vv, 1)])
            vv.inc()
        s.op("vector", lambda h: h.tensor_reduce(r2[:], m8t[:, 0:6], AX.X,
                                                 ALU.add),
             waits=[(vv, vv.n)], incs=[(vv, 1)])
        tg = vv.inc()
        s.op("vector", lambda h: h.reciprocal(r2[:], r2[:]),
             waits=[(vv, tg)], incs=[(vv, 1)])
        th_ = vv.inc()
        s.op("vector", lambda h: h.tensor_scalar(wnt[:], m8t[:], r2[:, 0:1],
                                                 None, ALU.mult),
             waits=[(vv, th_)], incs=[(vv, 1)])
        ti_ = vv.inc()
        s.wait("sync", vv, ti_)
        sl = slice(tt * 128, (tt + 1) * 128)
        s.dma("sync", o_m[sl, :], macc[:], osm)
        s.dma("sync", o_widx[sl, :], widxt[:], osm)
        s.dma("sync", o_wn[sl, :], wnt[:], osm)
        s.wait("vector", osm, osm.n)
    s.wait("sync", osm, osm.n)
    s.build()
    p.close()
    return p



def build_G2():
    p = Prog()
    nc = p.nc
    NTOK = 8192
    NTI = 64
    i_ln3 = p.din("ln3_all", [NTOK, D], BF16)
    i_m = p.din("m_my", [NTOK, 8], F32)
    i_wup = p.din("wup_sl", [8 * D, 512], F32)
    i_wdn = p.din("wdn_sl", [8 * 256, D], F32)
    i_tri = p.din("tri128x", [128, 128], F32)
    i_off8 = p.din("off8", [128, 8], F32)
    i_ident = p.din("ident", [128, 128], BF16)
    o_outr = p.dout("out_r", [8 * CAP, D], BF16)
    o_pos = p.dout("pos_full", [NTOK, 8], F32)

    x_disp = nc.dram_tensor("x_disp", [8 * CAP + 128, D], BF16)
    bas_d = nc.dram_tensor("bas_d", [NTI, 8], F32)

    m_all = p.sb("m_all", [128, NTI * 8], F32)
    trashc = p.sb("trashc", [128, 1], F32)
    pos_all = p.sb("pos_all", [128, NTI * 8], F32)
    didx = p.sb("didx", [128, NTI * 8], U32)
    ddf = p.sb("ddf", [128, NTI * 8], F32)
    tri = p.sb("tri", [128, 128], F32)
    off8 = p.sb("off8t", [128, 8], F32)
    onescol = p.sb("onescol", [128, 1], F32)
    onesrow = p.sb("onesrow", [1, 128], F32)
    psC_sb = p.sb("psC_sb", [8, NTI], F32)
    basrow = p.sb("basrow", [1, NTI * 8], F32)
    ident = p.sb("identt", [128, 128], BF16)
    identf = p.sb("identtf", [128, 128], F32)
    lnt = [p.sb(f"lnt{i}", [128, D], BF16) for i in range(2)]
    xdfm = p.sb("xdfm", [128, ND * CAP], BF16)
    wup = p.sb("wup", [128, ND * 512], BF16)
    wdn = p.sb("wdn", [128, 2 * D], BF16)
    hfm = p.sb("hfm", [128, 4 * CAP], F32)
    hr = p.sb("hr", [128, 2 * CAP], BF16)
    ev = [p.sb(f"ev{i}", [128, 512], BF16) for i in range(6)]

    pmm = [p.ps(f"pmm{i}", [128, 512], F32) for i in range(4)]
    pst = p.ps("pstG", [128, 1024], BF16)
    pstf = p.ps("pstGf", [128, 512], F32)

    ld = p.sem("ld")
    gsm = p.sem("gsm")
    sv = p.sem("sv")
    vv = p.sem("vv")
    tsm = p.sem("tsm")
    xsm = p.sem("xsm")
    osm = p.sem("osm")

    s = Sched(p)
    p.enable_consts(s, ld)
    s.dma("sync", tri[:], i_tri, ld)
    s.dma("sync", off8[:], i_off8, ld)
    s.dma("sync", ident[:], i_ident, ld)
    wm = s.dma("sync", m_all[:].rearrange("p (a n) -> p a n", n=8),
               i_m.rearrange("(a p) n -> p a n", p=128), ld)
    s.op("gpsimd", lambda h: h.iota(trashc[:], [[1, 1]], base=8 * CAP,
                                    channel_multiplier=1,
                                    allow_small_or_imprecise_dtypes=True),
         incs=[(vv, 1)])
    vv.inc()
    s.op("vector", lambda h: h.memset(onescol[:], 1.0))
    s.op("vector", lambda h: h.memset(onesrow[:], 1.0), incs=[(vv, 1)])
    tone = vv.inc()
    s.wait("tensor", ld, wm)
    s.wait("tensor", vv, tone)
    s.op("vector", lambda h: h.tensor_copy(identf[:], ident[:]),
         incs=[(vv, 1)])
    tidf = vv.inc()
    for ti in range(NTI):
        s.op("tensor", lambda h, ti=ti: h.matmul(
            pmm[3][0:8, ti:ti + 1], m_all[:, ti * 8:(ti + 1) * 8], onescol[:],
            start=True, stop=True),
            incs=[(tsm, 1)] if ti == NTI - 1 else ())
    t_cs = tsm.inc()
    s.op("vector", lambda h: h.tensor_copy(psC_sb[:], pmm[3][0:8, 0:NTI]),
         waits=[(tsm, t_cs)], incs=[(vv, 1)])
    t_e = vv.inc()
    s.op("vector", lambda h: h.memset(ddf[0:8, 0:1], 0.0),
         waits=[(vv, t_e)], incs=[(vv, 1)])
    t0_ = vv.inc()
    s.op("vector", lambda h: h.memset(ddf[0:8, NTI:NTI + NTI], 1.0),
         incs=[(vv, 1)])
    t1_ = vv.inc()
    s.op("vector", lambda h: h.tensor_tensor_scan(
        ddf[0:8, 1:NTI], ddf[0:8, NTI:NTI + NTI - 1], psC_sb[:, 0:NTI - 1],
        0.0, ALU.mult, ALU.add), waits=[(vv, t1_), (vv, t0_)], incs=[(vv, 1)])
    t2_ = vv.inc()
    s.wait("sync", vv, t2_)
    wb = s.dma("sync", bass.AP(bas_d.ap().tensor, 0, [[1, 8], [8, NTI]]),
               ddf[0:8, 0:NTI], ld, allow_slow_non_contiguous=True)
    s.wait("sync", ld, wb)
    wb2 = s.dma("sync", basrow[:].rearrange("p (a n) -> p a n", n=8),
                bas_d.ap().rearrange("(p a) n -> p a n", p=1), ld)
    s.wait("tensor", ld, wb2)
    for ti in range(NTI):
        bank = pmm[ti % 3]
        if ti >= 3:
            s.wait("tensor", vv, posev[ti - 3])
        s.op("tensor", lambda h, ti=ti, bank=bank: h.matmul(
            bank[0:128, 0:8], tri[:], m_all[:, ti * 8:(ti + 1) * 8],
            start=True, stop=False))
        s.op("tensor", lambda h, ti=ti, bank=bank: h.matmul(
            bank[0:128, 0:8], onesrow[:], basrow[:, ti * 8:(ti + 1) * 8],
            start=False, stop=True), incs=[(tsm, 1)])
        tmm = tsm.inc()
        s.op("vector", lambda h, ti=ti, bank=bank: h.tensor_copy(
            pos_all[:, ti * 8:(ti + 1) * 8], bank[0:128, 0:8]),
            waits=[(tsm, tmm)], incs=[(vv, 1)])
        if ti == 0:
            posev = {}
        posev[ti] = vv.inc()
    s.wait("sync", vv, posev[NTI - 1])
    wps = s.dma("sync", o_pos.rearrange("(a p) n -> p a n", p=128),
                pos_all[:].rearrange("p (a n) -> p a n", n=8), osm)
    s.wait("vector", osm, wps)
    s.wait("vector", vv, posev[NTI - 1])
    for ti in range(NTI):
        slt = slice(ti * 8, (ti + 1) * 8)
        s.op("vector", lambda h, slt=slt: h.tensor_scalar(
            ddf[:, slt], pos_all[:, slt], float(CAP), None, ALU.is_lt),
            incs=[(vv, 1)])
        ta = vv.inc()
        s.op("vector", lambda h, slt=slt: h.tensor_mul(
            ddf[:, slt], ddf[:, slt], m_all[:, slt]),
            waits=[(vv, ta)], incs=[(vv, 1)])
        tb = vv.inc()
        s.op("vector", lambda h, slt=slt: h.tensor_add(
            pos_all[:, slt], pos_all[:, slt], off8[:]),
            waits=[(vv, tb)], incs=[(vv, 1)])
        tc = vv.inc()
        s.op("vector", lambda h, slt=slt: h.tensor_scalar(
            pos_all[:, slt], pos_all[:, slt], trashc[:, 0:1], None,
            ALU.subtract), waits=[(vv, tc)], incs=[(vv, 1)])
        td = vv.inc()
        s.op("vector", lambda h, slt=slt: h.tensor_mul(
            ddf[:, slt], ddf[:, slt], pos_all[:, slt]),
            waits=[(vv, td)], incs=[(vv, 1)])
        te = vv.inc()
        s.op("vector", lambda h, slt=slt: h.tensor_scalar(
            ddf[:, slt], ddf[:, slt], trashc[:, 0:1], None, ALU.add),
            waits=[(vv, te)], incs=[(vv, 1)])
        tf_ = vv.inc()
        s.op("vector", lambda h, slt=slt: h.tensor_copy(didx[:, slt],
                                                        ddf[:, slt]),
            waits=[(vv, tf_)], incs=[(vv, 1)])
        vv.inc()
    s.build()

    lw = {}
    for tg in range(16):
        s = Sched(p)
        for tl in range(4):
            ti = tg * 4 + tl
            bi = ti % 2
            if tl >= 2:
                s.wait("sync", gsm, lw[ti - 2])
            wt = s.dma("sync", lnt[bi][:], i_ln3[ti * 128:(ti + 1) * 128, :],
                       ld)
            s.wait("gpsimd", ld, wt)
            for e in range(8):
                s.op("gpsimd", lambda h, ti=ti, e=e, bi=bi: h.indirect_dma_start(
                    out=x_disp.ap(), out_offset=IndirectOffsetOnAxis(
                        ap=didx[:, ti * 8 + e:ti * 8 + e + 1], axis=0),
                    in_=lnt[bi][:], in_offset=None),
                    incs=[(gsm, 16)])
                lw[ti] = gsm.inc(16)
                s.wait("gpsimd", gsm, lw[ti])
        if tg == 15:
            s.wait("gpsimd", gsm, gsm.n)
        s.build()

    for e in range(8):
        s = Sched(p)
        wu = s.dma("gpsimd", wup[:].rearrange("p (c n) -> p c n", n=512),
                   i_wup[e * D:(e + 1) * D, :].rearrange("(c p) n -> p c n",
                                                         p=128), ld)
        wd = s.dma("gpsimd", wdn[:].rearrange("p (c n) -> p c n", n=D),
                   i_wdn[e * 256:(e + 1) * 256, :].rearrange(
                       "(c p) n -> p c n", p=128), ld)
        tp_ = MMPipe(s, [pst], tsm, None, xsm)
        for st in range(8):
            bi = st % 2
            wt = s.dma("sync", lnt[bi][:],
                       x_disp.ap()[e * CAP + st * 128:e * CAP + (st + 1) * 128, :],
                       ld)
            s.wait("tensor", ld, wt)
            for dc in range(ND):
                def tfn(h, b, dc=dc, bi=bi):
                    return h.transpose(b[:, 0:128],
                                       lnt[bi][:, dc * 128:(dc + 1) * 128],
                                       ident[:])
                def efn(h, b, dc=dc, st=st):
                    return h.tensor_copy(
                        xdfm[:, dc * CAP + st * 128:dc * CAP + st * 128 + 128],
                        b[:, 0:128])
                tp_.job([tfn], [("vector", efn, True)])
            if st >= 1:
                pass
            s.wait("sync", tsm, tsm.n)
        mp = MMPipe(s, pmm, tsm, None, xsm)
        s.wait("tensor", ld, wu)
        s.wait("vector", xsm, xsm.n)
        for uc in range(4):
            for sh in range(2):
                mms = []
                for dc in range(ND):
                    def f(h, b, dc=dc, uc=uc, sh=sh):
                        return h.matmul(
                            b[:], wup[:, dc * 512 + uc * 128:dc * 512 + uc * 128 + 128],
                            xdfm[:, dc * CAP + sh * 512:dc * CAP + sh * 512 + 512],
                            start=(dc == 0), stop=(dc == ND - 1))
                    mms.append(f)
                def evf(h, b, uc=uc, sh=sh):
                    return h.tensor_copy(
                        hfm[:, uc * CAP + sh * 512:uc * CAP + sh * 512 + 512],
                        b[:])
                mp.job(mms, [("vector", evf, True)])
        s.wait("scalar", xsm, xsm.n)
        for u in range(2):
            s.op("scalar", lambda h, u=u: h.activation(
                hfm[:, u * CAP:(u + 1) * CAP], hfm[:, u * CAP:(u + 1) * CAP],
                AF.Silu), incs=[(sv, 1)])
            tsx = sv.inc()
            s.op("vector", lambda h, u=u: h.tensor_mul(
                hr[:, u * CAP:(u + 1) * CAP], hfm[:, u * CAP:(u + 1) * CAP],
                hfm[:, (2 + u) * CAP:(3 + u) * CAP]), waits=[(sv, tsx)],
                incs=[(vv, 1)])
            vv.inc()
        s.wait("tensor", vv, vv.n)
        s.wait("tensor", ld, wd)
        bm = BufMgr(s)
        stj = [0]
        for st in range(8):
            for osl in range(4):
                mms = []
                for hc in range(2):
                    def f(h, b, hc=hc, st=st, osl=osl):
                        return h.matmul(
                            b[:], hr[:, hc * CAP + st * 128:hc * CAP + st * 128 + 128],
                            wdn[:, hc * D + osl * 512:hc * D + osl * 512 + 512],
                            start=(hc == 0), stop=(hc == 1))
                    mms.append(f)
                j = stj[0]
                stj[0] += 1
                buf = ev[j % 6]
                bm.guard("vector", buf, osm)
                def evf(h, b, buf=buf):
                    return h.tensor_copy(buf[:], b[:])
                def after(lx, buf=buf, st=st, osl=osl, e=e):
                    s.op("sync", lambda h: h.dma_start(
                        out=o_outr[e * CAP + st * 128:e * CAP + (st + 1) * 128,
                                   osl * 512:osl * 512 + 512], in_=buf[:]),
                        waits=[(xsm, lx)], incs=[(osm, 16)])
                    bm.mark(buf, osm.inc(16))
                mp.job(mms, [("vector", evf, True)], after)
        if e == 7:
            s.wait("sync", osm, osm.n)
        s.build()
    p.close()
    return p




def build_G3():
    p = Prog()
    nc = p.nc
    i_outr = p.din("outr_all", [E64 * CAP, D], BF16)
    i_pos = p.din("pos_my", [TLOC, 64], F32)
    i_widx = p.din("widx", [TLOC, 8], F32)
    i_wn = p.din("wnorm", [TLOC, 8], F32)
    i_sh = p.din("sharedout", [TLOC, D], F32)
    i_x3 = p.din("x3", [TLOC, D], F32)
    i_io64 = p.din("iota64p", [128, 64], F32)
    o_x4 = p.dout("x4", [TLOC, D], F32)

    post = p.sb("post", [128, 64], F32)
    wit = p.sb("wit", [128, 8], F32)
    wnt = p.sb("wnt", [128, 8], F32)
    io64 = p.sb("io64", [128, 64], F32)
    r1 = p.sb("r1", [128, 64], F32)
    dstc = p.sb("dstc", [128, 8], F32)
    pvc = p.sb("pvc", [128, 8], F32)
    vvc = p.sb("vvc", [128, 8], F32)
    wtl = p.sb("wtl", [128, 8], F32)
    dstu = p.sb("dstu", [128, 8], U32)
    gb = [p.sb(f"gb{i}", [128, D], BF16) for i in range(2)]
    yac = p.sb("yac", [128, D], F32)
    sht = p.sb("sht", [128, D], F32)
    x4t = p.sb("x4t", [128, D], F32)
    lnt_ = p.sb("lnt_", [128, D], F32)
    st8 = p.sb("st8", [128, 8], F32)
    lnscr = p.sb("lnscr", [128, D], F32)

    ld = p.sem("ld")
    gsm = p.sem("gsm")
    sv = p.sem("sv")
    vv = p.sem("vv")
    osm = p.sem("osm")

    nblk = 0
    for tt in range(NT):
        s = Sched(p)
        if tt == 0:
            p.enable_consts(s, ld)
            wio = s.dma("sync", io64[:], i_io64, ld)
            s.op("vector", lambda h: h.memset(gb[0][:], 0.0))
            s.op("vector", lambda h: h.memset(gb[1][:], 0.0))
            s.wait("scalar", ld, 16)
        sl = slice(tt * 128, (tt + 1) * 128)
        wp = s.dma("sync", post[:], i_pos[sl, :], ld)
        wi = s.dma("sync", wit[:], i_widx[sl, :], ld)
        wn = s.dma("sync", wnt[:], i_wn[sl, :], ld)
        wsh = s.dma("sync", sht[:], i_sh[sl, :], ld)
        wx3 = s.dma("sync", x4t[:], i_x3[sl, :], ld)
        s.wait("vector", ld, ld.n)
        for k in range(TOPK6):
            s.op("vector", lambda h, k=k: h.tensor_scalar(
                r1[:], io64[:], wit[:, k:k + 1], None, ALU.is_equal),
                waits=[(vv, vv.n)], incs=[(vv, 1)])
            ta = vv.inc()
            s.op("vector", lambda h: h.scalar_tensor_tensor(
                r1[:], io64[:], 1024.0, r1[:], ALU.mult, ALU.mult),
                waits=[(vv, ta)], incs=[(vv, 1)])
            tb = vv.inc()
            s.op("vector", lambda h, k=k: h.tensor_scalar(
                lnscr[:, 0:64], io64[:], wit[:, k:k + 1], None, ALU.is_equal),
                waits=[(vv, tb)], incs=[(vv, 1)])
            tc = vv.inc()
            s.op("vector", lambda h: h.tensor_mul(
                lnscr[:, 0:64], lnscr[:, 0:64], post[:]),
                waits=[(vv, tc)], incs=[(vv, 1)])
            td = vv.inc()
            s.op("vector", lambda h: h.tensor_add(r1[:], r1[:],
                                                  lnscr[:, 0:64]),
                 waits=[(vv, td)], incs=[(vv, 1)])
            te = vv.inc()
            s.op("vector", lambda h, k=k: h.tensor_reduce(
                dstc[:, k:k + 1], r1[:], AX.X, ALU.add),
                waits=[(vv, te)], incs=[(vv, 1)])
            tf_ = vv.inc()
            s.op("vector", lambda h, k=k: h.scalar_tensor_tensor(
                pvc[:, k:k + 1], wit[:, k:k + 1], -1024.0, dstc[:, k:k + 1],
                ALU.mult, ALU.add), waits=[(vv, tf_)], incs=[(vv, 1)])
            tg = vv.inc()
            s.op("vector", lambda h, k=k: h.tensor_scalar(
                vvc[:, k:k + 1], pvc[:, k:k + 1], float(CAP), None, ALU.is_lt),
                waits=[(vv, tg)], incs=[(vv, 1)])
            th_ = vv.inc()
            s.op("vector", lambda h, k=k: h.tensor_mul(
                wtl[:, k:k + 1], wnt[:, k:k + 1], vvc[:, k:k + 1]),
                waits=[(vv, th_)], incs=[(vv, 1)])
            ti_ = vv.inc()
            s.op("vector", lambda h, k=k: h.tensor_mul(
                dstc[:, k:k + 1], dstc[:, k:k + 1], vvc[:, k:k + 1]),
                waits=[(vv, ti_)], incs=[(vv, 1)])
            tj = vv.inc()
            s.op("vector", lambda h, k=k: h.tensor_copy(
                dstu[:, k:k + 1], dstc[:, k:k + 1]),
                waits=[(vv, tj)], incs=[(vv, 1)])
            vv.inc()
        for k in range(TOPK6):
            bi = k % 2
            s.wait("gpsimd", vv, vv.n)
            if k >= 2:
                s.wait("gpsimd", vv, gdone[k - 2])
            s.op("gpsimd", lambda h, k=k, bi=bi: h.indirect_dma_start(
                out=gb[bi][:], out_offset=None, in_=i_outr,
                in_offset=IndirectOffsetOnAxis(ap=dstu[:, k:k + 1], axis=0)),
                incs=[(gsm, 16)])
            wg = gsm.inc(16)
            s.wait("gpsimd", gsm, wg)
            s.wait("vector", gsm, wg)
            if k == 0:
                s.op("vector", lambda h, bi=bi: h.tensor_scalar(
                    yac[:], gb[bi][:], wtl[:, 0:1], None, ALU.mult),
                    incs=[(vv, 1)])
            else:
                s.op("vector", lambda h, k=k, bi=bi: h.scalar_tensor_tensor(
                    yac[:], gb[bi][:], wtl[:, k:k + 1], yac[:],
                    ALU.mult, ALU.add), incs=[(vv, 1)])
            if k == 0:
                gdone = {}
            gdone[k] = vv.inc()
        s.op("vector", lambda h: h.tensor_add(yac[:], yac[:], sht[:]),
             waits=[(vv, vv.n)], incs=[(vv, 1)])
        ty = vv.inc()
        s.wait("scalar", vv, ty)
        emit_ln(s, p, yac[:], lnt_[:], st8[:, 0:8], sv, vv, D, lnscr[:])
        s.op("vector", lambda h: h.tensor_add(x4t[:], x4t[:], lnt_[:]),
             waits=[(vv, vv.n)], incs=[(vv, 1)])
        tx4 = vv.inc()
        s.wait("sync", vv, tx4)
        s.dma("sync", o_x4[sl, :], x4t[:], osm)
        s.wait("sync", osm, osm.n)
        s.build()
    p.close()
    return p



_PROGS = {}


def _prog(name, builder):
    if name not in _PROGS:
        _PROGS[name] = builder()
    return _PROGS[name]


def _run(p, in_maps):
    res = run_bass_kernel_spmd(p.nc, in_maps, core_ids=list(range(NCORES)))
    return res.results


def kernel(**inp):
    x = _f32(inp["L0_attn_x"])
    ident = np.eye(128).astype(BFNP)
    tri64 = np.triu(np.ones((CHK, CHK), np.float32))
    mask64 = np.triu(np.ones((CHK, CHK), np.float32)).astype(BFNP)
    rst2048 = np.ones((64, 2048), np.float32)
    rst2048[:, ::64] = 0.0
    rst16 = np.ones((64, 15), np.float32)
    iota1000_8 = (np.arange(8, dtype=np.float32) + 1000.0)[None, :]
    io64bc = np.tile(np.arange(64, dtype=np.float32)[None, :] + 1000.0,
                     (128, 1))
    io64p = np.tile(np.arange(64, dtype=np.float32)[None, :], (128, 1))
    tri128x = np.triu(np.ones((128, 128), np.float32), 1)
    off8 = np.tile((np.arange(8, dtype=np.float32) * CAP)[None, :], (128, 1))
    padsel = np.array([[0, 0, 0, 0, 0, 0, 1, 1]], np.float32)

    pA1 = _prog("A1", build_A1)
    bgk2 = _f32(inp["L0_attn_bgk2"])
    mapsA1 = []
    for c in range(NCORES):
        b, hh = c // 2, c % 2
        mapsA1.append({
            "x": _f32(x[b, hh * TLOC:(hh + 1) * TLOC]),
            "wq": _f32(inp["L0_attn_Wq"]), "wk": _f32(inp["L0_attn_Wk"]),
            "wv": _f32(inp["L0_attn_Wv"]), "wg": _f32(inp["L0_attn_Wg"]),
            "wgk1": _f32(inp["L0_attn_Wgk1"]),
            "wgk2": _f32(inp["L0_attn_Wgk2"]),
            "bgk_bc": np.tile(bgk2[None, :], (128, 1)),
            "bgk_col": np.ascontiguousarray(bgk2.reshape(8, 128).T),
            "ident": ident, "cconst": pA1.const_host(),
        })
    rA1 = _run(pA1, mapsA1)

    pA2 = _prog("A2", build_A2)
    normw_bc = np.tile(_f32(inp["L0_attn_norm_w"])[None, :], (128, 1))
    mapsA2 = []
    for c in range(NCORES):
        b, hh = c // 2, c % 2
        psel = np.zeros((64, 8), np.float32)
        if hh == 1:
            psel[:, c - 1] = 1.0
        r = rA1[c]
        mapsA2.append({
            "q_fm": r["q_fm"], "k_fm": r["k_fm"], "k_tok": r["k_tok"],
            "v_tok": r["v_tok"], "g_tok": r["g_tok"], "gk_fm": r["gk_fm"],
            "gk_tok": r["gk_tok"], "wo": _f32(inp["L0_attn_Wo"]),
            "x_res": _f32(x[b, hh * TLOC:(hh + 1) * TLOC]),
            "normw_bc": normw_bc, "pairsel": psel, "tri64": tri64,
            "mask64": mask64, "rst2048": rst2048, "rst16": rst16,
            "ident": ident, "cconst": pA2.const_host(),
        })
    rA2 = _run(pA2, mapsA2)

    pF1 = _prog("F1", build_F1)
    gwt = np.zeros((D, 8), np.float32)
    gwt[:, :6] = np.ascontiguousarray(inp["L0_ffn_gate_W"].T)
    w1r = _f32(np.asarray(inp["L0_ffn_W1"]).reshape(6 * D, 1408))
    w2r = _f32(np.asarray(inp["L0_ffn_W2"]).reshape(6 * 1408, D))
    b1t = np.zeros((1408, 8), np.float32)
    b1t[:, :6] = np.ascontiguousarray(inp["L0_ffn_b1"].T)
    b2r = _f32(inp["L0_ffn_b2"])
    ln1c = {b: np.concatenate([np.asarray(rA2[2 * b]["ln1_fm"]),
                               np.asarray(rA2[2 * b + 1]["ln1_fm"])], axis=1)
            for b in range(4)}
    mapsF1 = []
    for c in range(NCORES):
        b, k = c // 2, c % 2
        sel2 = np.zeros((1, 2), np.float32)
        sel2[0, k] = 1.0
        mapsF1.append({
            "ln1_fm": ln1c[b], "gate_WT": gwt, "W1r": w1r, "W2r": w2r,
            "b1T": b1t, "b2rows": b2r, "sel2": sel2, "sel8": padsel,
            "iota1e9": iota1000_8, "cconst": pF1.const_host(),
        })
    rF1 = _run(pF1, mapsF1)

    pF2 = _prog("F2M1", build_F2M1)
    wint = np.ascontiguousarray(_f32(inp["L1_attn_in_proj_w"]).T)
    mapsF2 = []
    for c in range(NCORES):
        b, hh = c // 2, c % 2
        sl = slice(hh * TLOC, (hh + 1) * TLOC)
        mapsF2.append({
            "slot0": np.asarray(rF1[2 * b]["out_slot"])[sl],
            "slot1": np.asarray(rF1[2 * b + 1]["out_slot"])[sl],
            "x1": np.asarray(rA2[c]["x1"]),
            "W_inT": wint, "ident": ident, "cconst": pF2.const_host(),
        })
    rF2 = _run(pF2, mapsF2)

    pM2 = _prog("M2", build_M2)
    cw = _f32(inp["L1_attn_conv_w"]).reshape(32, 128, 4).transpose(1, 0, 2)\
        .reshape(128, 128)
    cb = np.ascontiguousarray(_f32(inp["L1_attn_conv_b"]).reshape(32, 128).T)
    xpt = np.ascontiguousarray(_f32(inp["L1_attn_x_proj_w"]).T)
    dpt = np.ascontiguousarray(_f32(inp["L1_attn_dt_proj_w"]).T)
    dpb2 = np.ascontiguousarray(
        (2.0 * _f32(inp["L1_attn_dt_proj_b"])).reshape(32, 128).T)
    mapsM2 = []
    for c in range(NCORES):
        b, hh = c // 2, c % 2
        xz = np.asarray(rF2[c]["xz_fm"])
        xh = xz[0:DIN]
        if hh == 0:
            halo = np.zeros((DIN, 3), BFNP)
        else:
            halo = np.asarray(rF2[c - 1]["xz_fm"])[0:DIN, TLOC - 3:TLOC]
        mapsM2.append({
            "xh_fm": np.concatenate([halo, xh], axis=1),
            "conv_w": cw, "conv_b": cb, "x_projT": xpt, "dt_projT": dpt,
            "dpb2": dpb2, "cconst": pM2.const_host(),
        })
    rM2 = _run(pM2, mapsM2)

    pM3 = _prog("M3", build_M3)
    al = _f32(inp["L1_attn_A_log"])
    Dv = _f32(inp["L1_attn_D"])
    wout = np.ascontiguousarray(_f32(inp["L1_attn_out_proj_w"]).T)
    mapsM3 = []
    for c in range(NCORES):
        b, dg = c // 2, c % 2
        rsl = slice(2048 * dg, 2048 * (dg + 1))
        dt_c = np.concatenate([np.asarray(rM2[2 * b]["dt_fm"])[rsl],
                               np.asarray(rM2[2 * b + 1]["dt_fm"])[rsl]], 1)
        xc_c = np.concatenate([np.asarray(rM2[2 * b]["xc_fm"])[rsl],
                               np.asarray(rM2[2 * b + 1]["xc_fm"])[rsl]], 1)
        z_c = np.concatenate(
            [np.asarray(rF2[2 * b]["xz_fm"])[DIN + 2048 * dg:DIN + 2048 * (dg + 1)],
             np.asarray(rF2[2 * b + 1]["xz_fm"])[DIN + 2048 * dg:DIN + 2048 * (dg + 1)]],
            1)
        B_c = np.concatenate([np.asarray(rM2[2 * b]["bc_rows"])[0:16],
                              np.asarray(rM2[2 * b + 1]["bc_rows"])[0:16]], 1)
        C_c = np.concatenate([np.asarray(rM2[2 * b]["bc_rows"])[16:32],
                              np.asarray(rM2[2 * b + 1]["bc_rows"])[16:32]], 1)
        al_sl = al[rsl].reshape(16, 128, 16).transpose(1, 0, 2).reshape(128, 256)
        D_sl = np.ascontiguousarray(Dv[rsl].reshape(16, 128).T)
        mapsM3.append({
            "dt": dt_c, "xc": xc_c, "z_fm": z_c, "Brows": B_c, "Crows": C_c,
            "A_log": np.ascontiguousarray(al_sl), "Dcol": D_sl,
            "WoutT_sl": np.ascontiguousarray(wout[rsl]),
            "cconst": pM3.const_host(),
        })
    rM3 = _run(pM3, mapsM3)

    pM5 = _prog("M5", build_M5)
    mapsM5 = []
    for c in range(NCORES):
        b, hh = c // 2, c % 2
        sl = slice(hh * TLOC, (hh + 1) * TLOC)
        mapsM5.append({
            "part0": np.asarray(rM3[2 * b]["part"])[sl],
            "part1": np.asarray(rM3[2 * b + 1]["part"])[sl],
            "x2": np.asarray(rF2[c]["x2"]),
            "ident": ident, "cconst": pM5.const_host(),
        })
    rM5 = _run(pM5, mapsM5)

    pG1 = _prog("G1", build_G1)
    mapsG1 = []
    for c in range(NCORES):
        mapsG1.append({
            "ln3_fm": np.asarray(rM5[c]["ln3_fm"]),
            "w_gate": _f32(inp["L1_ffn_w_gate"]),
            "w_shared_up": _f32(inp["L1_ffn_w_shared_up"]),
            "w_shared_down": _f32(inp["L1_ffn_w_shared_down"]),
            "iota64_bc": io64bc, "cconst": pG1.const_host(),
        })
    rG1 = _run(pG1, mapsG1)

    pG2 = _prog("G2", build_G2)
    ln3_all = np.concatenate([np.asarray(rM5[c]["ln3_tok"])
                              for c in range(NCORES)], axis=0)
    m_full = np.concatenate([np.asarray(rG1[c]["mmat"])
                             for c in range(NCORES)], axis=0)
    wup = _f32(np.asarray(inp["L1_ffn_w_up"]))
    wdn = _f32(np.asarray(inp["L1_ffn_w_down"]))
    mapsG2 = []
    for c in range(NCORES):
        mapsG2.append({
            "ln3_all": ln3_all,
            "m_my": np.ascontiguousarray(m_full[:, 8 * c:8 * c + 8]),
            "wup_sl": np.ascontiguousarray(
                wup[8 * c:8 * c + 8].reshape(8 * D, 512)),
            "wdn_sl": np.ascontiguousarray(
                wdn[8 * c:8 * c + 8].reshape(8 * 256, D)),
            "tri128x": tri128x, "off8": off8, "ident": ident,
            "cconst": pG2.const_host(),
        })
    rG2 = _run(pG2, mapsG2)

    pG3 = _prog("G3", build_G3)
    outr_all = np.concatenate([np.asarray(rG2[c]["out_r"])
                               for c in range(NCORES)], axis=0)
    pos_full = np.concatenate([np.asarray(rG2[c]["pos_full"])
                               for c in range(NCORES)], axis=1)
    mapsG3 = []
    for c in range(NCORES):
        sl = slice(c * TLOC, (c + 1) * TLOC)
        mapsG3.append({
            "outr_all": outr_all,
            "pos_my": np.ascontiguousarray(pos_full[sl]),
            "widx": np.asarray(rG1[c]["widx"]),
            "wnorm": np.asarray(rG1[c]["wnorm"]),
            "sharedout": np.asarray(rG1[c]["sharedout"]),
            "x3": np.asarray(rM5[c]["x3"]),
            "iota64p": io64p, "cconst": pG3.const_host(),
        })
    rG3 = _run(pG3, mapsG3)

    out = np.empty((4, T, D), np.float32)
    for c in range(NCORES):
        b, hh = c // 2, c % 2
        out[b, hh * TLOC:(hh + 1) * TLOC] = np.asarray(rG3[c]["x4"])
    return out


# version 03
# speedup vs baseline: 1.1594x; 1.1594x over previous
from contextlib import ExitStack

import numpy as np
import ml_dtypes
import concourse.bass as bass
import concourse.mybir as mybir
from concourse.bass import IndirectOffsetOnAxis
from concourse.bass_utils import run_bass_kernel_spmd

F32 = mybir.dt.float32
BF16 = mybir.dt.bfloat16
U32 = mybir.dt.uint32
AF = mybir.ActivationFunctionType
ALU = mybir.AluOpType
AX = mybir.AxisListType

BFNP = ml_dtypes.bfloat16

NCORES = 8
B, T, D = 4, 2048, 2048
TLOC = 1024
NT = TLOC // 128
ND = D // 128
EPS = 1e-5

H, DK, DV = 16, 64, 128
KD, VD = 1024, 2048
CHK = 64
NCHK = TLOC // CHK

DIN, DST, DCONV, DTR = 4096, 16, 4, 128
E64, CAP, TOPK6 = 64, 1024, 6

ENGS = ("sync", "gpsimd", "scalar", "vector", "tensor")


def _bf(x):
    return np.ascontiguousarray(np.asarray(x)).astype(BFNP)


def _f32(x):
    return np.ascontiguousarray(np.asarray(x), dtype=np.float32)


class SemC:

    def __init__(self, handle):
        self.h = handle
        self.n = 0

    def inc(self, k=1):
        self.n += k
        return self.n


class Prog:
    def __init__(self):
        self.nc = bass.Bass("TRN2", target_bir_lowering=False)
        self.stk = ExitStack()

    def sb(self, name, shape, dt):
        return self.stk.enter_context(self.nc.sbuf_tensor(name, shape, dt))

    def ps(self, name, shape, dt=F32):
        return self.stk.enter_context(self.nc.psum_tensor(name, shape, dt))

    def sem(self, name):
        return SemC(self.stk.enter_context(self.nc.semaphore(name)))

    def din(self, name, shape, dt):
        return self.nc.dram_tensor(name, shape, dt, kind="ExternalInput").ap()

    def dout(self, name, shape, dt):
        return self.nc.dram_tensor(name, shape, dt, kind="ExternalOutput").ap()

    def enable_consts(self, s, sem):
        self.cin = self.din("cconst", [128, 16], F32)
        self.ct = self.sb("cconst_t", [128, 16], F32)
        self.cmap = {}
        s.dma("sync", self.ct[:], self.cin, sem)

    def cc(self, val):
        if val not in self.cmap:
            assert len(self.cmap) < 16
            self.cmap[val] = len(self.cmap)
        return self.ct[:, self.cmap[val]:self.cmap[val] + 1]

    def const_host(self):
        v = np.zeros(16, np.float32)
        for k, i in self.cmap.items():
            v[i] = k
        return np.tile(v[None, :], (128, 1))

    def close(self):
        self.stk.close()


class Sched:

    def __init__(self, p):
        self.p = p
        self.ops = {e: [] for e in ENGS}

    def op(self, eng, fn, waits=(), incs=()):
        self.ops[eng].append((tuple(waits), fn, tuple(incs)))

    def wait(self, eng, sem, val):
        self.ops[eng].append((((sem, val),), None, ()))

    def dma(self, eng, out, in_, sem, **kw):
        tot = sem.inc(16)
        self.op(eng, lambda h, o=out, i=in_, kw=kw: h.dma_start(out=o, in_=i, **kw),
                incs=[(sem, 16)])
        self.wait(eng, sem, tot)
        return tot

    def build(self):
        nc = self.p.nc
        ops = self.ops
        with nc.Block() as blk:
            def mk(eng_ops):
                def body(h):
                    for waits, fn, incs in eng_ops:
                        for sem, val in waits:
                            h.wait_ge(sem.h, val)
                        if fn is None:
                            continue
                        ins = fn(h)
                        for sem, k in incs:
                            ins = ins.then_inc(sem.h, k)
                return body
            for e in ENGS:
                if ops[e]:
                    getattr(blk, e)(mk(ops[e]))


def emit_ln(s, p, xt, ot, stc, sv, vv, n_free, scr_full, ot2=None):
    inv = 1.0 / n_free
    s.op("scalar", lambda h: h.activation(scr_full, xt, AF.Copy,
                                          accum_out=stc[:, 0:1]),
         incs=[(sv, 1)])
    t00 = sv.inc()
    s.op("scalar", lambda h: h.activation(scr_full, xt, AF.Square,
                                          accum_out=stc[:, 1:2]),
         waits=[(sv, t00)], incs=[(sv, 1)])
    t0 = sv.inc()
    s.op("vector", lambda h: h.tensor_scalar(stc[:, 2:3], stc[:, 0:1], inv,
                                             None, ALU.mult),
         waits=[(sv, t0)], incs=[(vv, 1)])
    t1 = vv.inc()
    s.op("vector", lambda h: h.tensor_scalar(stc[:, 5:6], stc[:, 1:2], inv,
                                             None, ALU.mult),
         incs=[(vv, 1)])
    t2 = vv.inc()
    s.op("vector", lambda h: h.scalar_tensor_tensor(stc[:, 6:7], stc[:, 2:3],
                                                    -1.0, stc[:, 2:3],
                                                    ALU.mult, ALU.mult),
         waits=[(vv, t1)], incs=[(vv, 1)])
    t3 = vv.inc()
    s.op("vector", lambda h: h.tensor_add(stc[:, 4:5], stc[:, 5:6], stc[:, 6:7]),
         waits=[(vv, t3)], incs=[(vv, 1)])
    t4 = vv.inc()
    s.op("scalar", lambda h: h.activation(stc[:, 5:6], stc[:, 4:5], AF.Ln,
                                          bias=p.cc(EPS)),
         waits=[(vv, t4)], incs=[(sv, 1)])
    t5 = sv.inc()
    s.op("scalar", lambda h: h.activation(stc[:, 3:4], stc[:, 5:6], AF.Exp,
                                          scale=p.cc(-0.5)),
         waits=[(sv, t5)], incs=[(sv, 1)])
    t6 = sv.inc()
    s.op("vector", lambda h: h.tensor_scalar(ot, xt, stc[:, 2:3], stc[:, 3:4],
                                             ALU.subtract, ALU.mult),
         waits=[(sv, t6)], incs=[(vv, 1)])
    vv.inc()
    if ot2 is not None:
        s.op("vector", lambda h: h.tensor_scalar(ot2, xt, stc[:, 2:3],
                                                 stc[:, 3:4],
                                                 ALU.subtract, ALU.mult),
             incs=[(vv, 1)])
        vv.inc()


class MMPipe:

    def __init__(self, s, banks, tsm, esm, xsm):
        self.s = s
        self.banks = banks
        self.nb = len(banks)
        self.tsm = tsm
        self.xsm = xsm
        self.bank_free = [0] * self.nb
        self.j = 0

    def job(self, mms, steps, after=None, extra_tensor_waits=()):
        s, j = self.s, self.j
        bi = j % self.nb
        bank = self.banks[bi]
        waits = list(extra_tensor_waits)
        if self.bank_free[bi]:
            waits.append((self.xsm, self.bank_free[bi]))
        for i, fn in enumerate(mms):
            inc = [(self.tsm, 1)] if i == len(mms) - 1 else ()
            s.op("tensor", (lambda h, f=fn, b=bank: f(h, b)),
                 waits=waits if i == 0 else (), incs=inc)
        tj = self.tsm.inc()
        last_rp = max(i for i, st_ in enumerate(steps) if st_[2])
        gate = (self.tsm, tj)
        xv = None
        for i, (eng, fn, rp) in enumerate(steps):
            s.op(eng, (lambda h, f=fn, b=bank: f(h, b)),
                 waits=[gate], incs=[(self.xsm, 1)])
            xv = self.xsm.inc()
            gate = (self.xsm, xv)
            if i == last_rp:
                self.bank_free[bi] = xv
        self.j += 1
        if after is not None:
            after(xv)
        return xv


class BufMgr:

    def __init__(self, s):
        self.s = s
        self.last = {}

    def guard(self, eng, buf, sem):
        c = self.last.get(id(buf))
        if c:
            self.s.wait(eng, sem, c)

    def mark(self, buf, cnt):
        self.last[id(buf)] = cnt



def build_A1():
    p = Prog()
    nc = p.nc
    x_in = p.din("x", [TLOC, D], F32)
    wq = p.din("wq", [D, KD], F32)
    wk = p.din("wk", [D, KD], F32)
    wv = p.din("wv", [D, VD], F32)
    wg = p.din("wg", [D, VD], F32)
    wgk1 = p.din("wgk1", [D, 16], F32)
    wgk2 = p.din("wgk2", [16, KD], F32)
    bgk_bc = p.din("bgk_bc", [128, KD], F32)
    bgk_col = p.din("bgk_col", [128, 8], F32)
    ident_in = p.din("ident", [128, 128], BF16)

    o_qfm = p.dout("q_fm", [KD, TLOC], BF16)
    o_kfm = p.dout("k_fm", [KD, TLOC], BF16)
    o_ktok = p.dout("k_tok", [TLOC, KD], BF16)
    o_v = p.dout("v_tok", [TLOC, VD], BF16)
    o_g = p.dout("g_tok", [TLOC, VD], BF16)
    o_gkfm = p.dout("gk_fm", [KD, TLOC], F32)
    o_gktok = p.dout("gk_tok", [TLOC, KD], F32)

    xt = [p.sb(f"xt{i}", [128, D], F32) for i in range(NT)]
    lnt = [p.sb(f"lnt{i}", [128, D], BF16) for i in range(NT)]
    lnfm = [p.sb(f"lnfm{i}", [128, TLOC], BF16) for i in range(ND)]
    st = p.sb("st", [128, 8 * NT], F32)
    lnscr = p.sb("lnscr", [128, D], F32)
    ident = p.sb("identt", [128, 128], BF16)
    wsl = [p.sb(f"wsl{i}", [128, ND * 512], BF16) for i in range(2)]
    wgk1b = p.sb("wgk1b", [128, ND * 16], BF16)
    wgk2b = p.sb("wgk2b", [16, KD], BF16)
    lowfm = p.sb("lowfm", [16, TLOC], BF16)
    bgkbc = p.sb("bgkbc", [128, KD], F32)
    bgkcol = p.sb("bgkcol", [128, 8], F32)
    ev = [p.sb(f"ev{i}", [128, 512], BF16) for i in range(8)]
    scrA = [p.sb(f"scrA{i}", [128, 512], F32) for i in range(4)]
    scrC = [p.sb(f"scrC{i}", [128, 512], F32) for i in range(4)]

    pst = [p.ps(f"pst{i}", [128, 1024], BF16) for i in range(2)]
    pmm = [p.ps(f"pmm{i}", [128, 512], F32) for i in range(6)]

    ld = p.sem("ld")
    wsm = p.sem("wsm")
    sv = p.sem("sv")
    vv = p.sem("vv")
    tsm = p.sem("tsm")
    esm = p.sem("esm")
    osm = p.sem("osm")
    xsm = p.sem("xsm")

    s = Sched(p)
    p.enable_consts(s, ld)
    s.wait("scalar", ld, 16)
    xdone = []
    for i in range(NT):
        xdone.append(s.dma("sync", xt[i][:], x_in[i * 128:(i + 1) * 128, :], ld))
    s.dma("sync", bgkbc[:], bgk_bc, ld)
    s.dma("sync", bgkcol[:], bgk_col, ld)
    iddone = s.dma("sync", ident[:], ident_in, ld)

    s.dma("gpsimd", wgk1b[:].rearrange("p (c n) -> p c n", n=16), wgk1.rearrange("(c p) n -> p c n", p=128), wsm)
    s.dma("gpsimd", wgk2b[:], wgk2, wsm)
    s.dma("gpsimd", wsl[0][:].rearrange("p (c n) -> p c n", n=512), wq[:, 0:512].rearrange("(c p) n -> p c n", p=128), wsm)
    s.dma("gpsimd", wsl[1][:].rearrange("p (c n) -> p c n", n=512), wq[:, 512:1024].rearrange("(c p) n -> p c n", p=128), wsm)

    ln_done = []
    for i in range(NT):
        s.wait("scalar", ld, xdone[i])
        emit_ln(s, p, xt[i][:], lnt[i][:], st[:, 8 * i:8 * i + 8], sv, vv, D,
                lnscr[:])
        ln_done.append(vv.n)

    s.wait("tensor", ld, iddone)
    tp = MMPipe(s, pst, tsm, esm, xsm)
    for i in range(NT):
        s.wait("tensor", vv, ln_done[i])
        for dc in range(ND):
            def mmfn(h, b, i=i, dc=dc):
                return h.transpose(b[:, 0:128],
                                   lnt[i][:, dc * 128:(dc + 1) * 128], ident[:])
            def evfn(h, b, i=i, dc=dc):
                return h.tensor_copy(lnfm[dc][:, i * 128:(i + 1) * 128],
                                     b[:, 0:128])
            tp.job([mmfn], [("vector", evfn, True)])
    s.build()

    s = Sched(p)
    mp = MMPipe(s, pmm, tsm, esm, xsm)
    bm = BufMgr(s)
    stj = [0]

    def simple_store(dst_ap, pool, dtype_cast="copy"):
        j = stj[0]
        stj[0] += 1
        buf = pool[j % len(pool)]
        bm.guard("vector", buf, osm)
        def ev_fn(h, b, buf=buf):
            return h.tensor_copy(buf[:], b[:])
        def after(last_x, buf=buf, dst=dst_ap):
            s.op("sync", lambda h, d=dst, bf=buf: h.dma_start(out=d, in_=bf[:]),
                 waits=[(xsm, last_x)], incs=[(osm, 16)])
            bm.mark(buf, osm.inc(16))
        return [("vector", ev_fn, True)], after

    def run_roleB(slab_buf, wait_w, dst, nch0):
        first = True
        for nch in range(4):
            for th in range(2):
                mms = []
                for dc in range(ND):
                    def f(h, b, dc=dc, nch=nch, th=th, sb_=slab_buf):
                        return h.matmul(
                            b[:],
                            sb_[:, dc * 512 + nch * 128:dc * 512 + nch * 128 + 128],
                            lnfm[dc][:, th * 512:th * 512 + 512],
                            start=(dc == 0), stop=(dc == ND - 1))
                    mms.append(f)
                steps, after = simple_store(
                    dst[(nch0 + nch) * 128:(nch0 + nch + 1) * 128,
                        th * 512:th * 512 + 512], ev)
                mp.job(mms, steps, after,
                       extra_tensor_waits=[(wsm, wait_w)] if first else ())
                first = False

    def run_roleA(slab_buf, wait_w, dst, col0, ncols):
        first = True
        for tt in range(NT):
            mms = []
            for dc in range(ND):
                def f(h, b, dc=dc, tt=tt, sb_=slab_buf):
                    return h.matmul(
                        b[:], lnfm[dc][:, tt * 128:(tt + 1) * 128],
                        sb_[:, dc * ncols:dc * ncols + ncols],
                        start=(dc == 0), stop=(dc == ND - 1))
                mms.append(f)
            steps, after = simple_store(
                dst[tt * 128:(tt + 1) * 128, col0:col0 + ncols], ev)
            mp.job(mms, steps, after,
                   extra_tensor_waits=[(wsm, wait_w)] if first else ())
            first = False

    run_roleB(wsl[0], 16 * 3, o_qfm, 0)
    t_q0 = tsm.n
    run_roleB(wsl[1], 16 * 4, o_qfm, 4)
    t_q1 = tsm.n

    s.wait("gpsimd", tsm, t_q0)
    w_k0 = s.dma("gpsimd", wsl[0][:].rearrange("p (c n) -> p c n", n=512),
                 wk[:, 0:512].rearrange("(c p) n -> p c n", p=128), wsm)
    run_roleB(wsl[0], w_k0, o_kfm, 0)
    run_roleA(wsl[0], w_k0, o_ktok, 0, 512)
    t_k0 = tsm.n
    s.wait("gpsimd", tsm, t_q1)
    w_k1 = s.dma("gpsimd", wsl[1][:].rearrange("p (c n) -> p c n", n=512),
                 wk[:, 512:1024].rearrange("(c p) n -> p c n", p=128), wsm)
    run_roleB(wsl[1], w_k1, o_kfm, 4)
    run_roleA(wsl[1], w_k1, o_ktok, 512, 512)
    prev_t = [t_k0, tsm.n]
    for phase, w_ap, dst in ((0, wv, o_v), (1, wg, o_g)):
        for si in range(4):
            buf = wsl[si % 2]
            s.wait("gpsimd", tsm, prev_t[si % 2])
            w_i = s.dma("gpsimd", buf[:].rearrange("p (c n) -> p c n", n=512),
                        w_ap[:, si * 512:(si + 1) * 512]
                        .rearrange("(c p) n -> p c n", p=128), wsm)
            run_roleA(buf, w_i, dst, si * 512, 512)
            prev_t[si % 2] = tsm.n

    for th in range(2):
        mms = []
        for dc in range(ND):
            def f(h, b, dc=dc, th=th):
                return h.matmul(b[0:16, :], wgk1b[:, dc * 16:dc * 16 + 16],
                                lnfm[dc][:, th * 512:th * 512 + 512],
                                start=(dc == 0), stop=(dc == ND - 1))
            mms.append(f)
        def evl(h, b, th=th):
            return h.tensor_copy(lowfm[0:16, th * 512:th * 512 + 512],
                                 b[0:16, :])
        mp.job(mms, [("vector", evl, True)],
               extra_tensor_waits=[(wsm, 32)] if th == 0 else ())

    def gk_steps(a, c, u_maker):
        return [
            u_maker,
            ("scalar", lambda h, b: h.activation(c[:], a[:], AF.Abs), False),
            ("scalar", lambda h, b: h.activation(c[:], c[:], AF.Exp, scale=-1.0),
             False),
            ("scalar", lambda h, b: h.activation(c[:], c[:], AF.Ln, bias=1.0),
             False),
            ("vector", lambda h, b: h.tensor_scalar(a[:], a[:], 0.0, 1.0 / 16,
                                                    ALU.min, ALU.mult), False),
            ("vector", lambda h, b: h.scalar_tensor_tensor(a[:], c[:],
                                                           -1.0 / 16, a[:],
                                                           ALU.mult, ALU.add),
             False),
        ]

    gk_j = [0]

    def gk_job(mms, u_maker, dst_ap):
        j = gk_j[0]
        gk_j[0] += 1
        a = scrA[j % 4]
        c = scrC[j % 4]
        bm.guard("vector", a, osm)
        bm.guard("scalar", a, osm)
        steps = gk_steps(a, c, u_maker(a))
        def after(last_x, a=a, dst=dst_ap):
            s.op("sync", lambda h: h.dma_start(out=dst, in_=a[:]),
                 waits=[(xsm, last_x)], incs=[(osm, 16)])
            bm.mark(a, osm.inc(16))
        mp.job(mms, steps, after)

    for tt in range(NT):
        for nsl in range(2):
            mms = [lambda h, b, tt=tt, nsl=nsl: h.matmul(
                b[:], lowfm[0:16, tt * 128:(tt + 1) * 128],
                wgk2b[0:16, nsl * 512:nsl * 512 + 512], start=True, stop=True)]
            def um(a, nsl=nsl):
                return ("vector",
                        lambda h, b, a=a, nsl=nsl: h.tensor_add(
                            a[:], b[:], bgkbc[:, nsl * 512:nsl * 512 + 512]),
                        True)
            gk_job(mms, um,
                   o_gktok[tt * 128:(tt + 1) * 128, nsl * 512:nsl * 512 + 512])

    for nch in range(8):
        for th in range(2):
            mms = [lambda h, b, nch=nch, th=th: h.matmul(
                b[:], wgk2b[0:16, nch * 128:(nch + 1) * 128],
                lowfm[0:16, th * 512:th * 512 + 512], start=True, stop=True)]
            def um(a, nch=nch):
                return ("vector",
                        lambda h, b, a=a, nch=nch: h.tensor_scalar(
                            a[:], b[:], bgkcol[:, nch:nch + 1], None, ALU.add),
                        True)
            gk_job(mms, um,
                   o_gkfm[nch * 128:(nch + 1) * 128, th * 512:th * 512 + 512])

    s.wait("sync", osm, osm.n)
    s.build()
    p.close()
    return p


def bcast_free(ap, n):
    return bass.AP(ap.tensor, ap.offset, list(ap.ap) + [[0, n]])



def build_A2():
    p = Prog()
    nc = p.nc
    i_qfm = p.din("q_fm", [KD, TLOC], BF16)
    i_kfm = p.din("k_fm", [KD, TLOC], BF16)
    i_ktok = p.din("k_tok", [TLOC, KD], BF16)
    i_v = p.din("v_tok", [TLOC, VD], BF16)
    i_g = p.din("g_tok", [TLOC, VD], BF16)
    i_gkfm = p.din("gk_fm", [KD, TLOC], F32)
    i_gktok = p.din("gk_tok", [TLOC, KD], F32)
    i_wo = p.din("wo", [VD * 1, D], F32)
    i_xres = p.din("x_res", [TLOC, D], F32)
    i_normw = p.din("normw_bc", [128, DV], F32)
    i_pairsel = p.din("pairsel", [64, 8], F32)
    i_tri = p.din("tri64", [CHK, CHK], F32)
    i_mask = p.din("mask64", [CHK, CHK], BF16)
    i_rst = p.din("rst2048", [64, 2048], F32)
    i_rst16 = p.din("rst16", [64, 15], F32)
    i_ident = p.din("ident", [128, 128], BF16)

    o_x1 = p.dout("x1", [TLOC, D], F32)
    o_ln1fm = p.dout("ln1_fm", [D, TLOC], BF16)

    cc_in = nc.dram_tensor("cc_in", [64, 16 * DV], F32)
    cc_out = nc.dram_tensor("cc_out", [64 * NCORES, 16 * DV], F32)

    obf = p.sb("obf", [128, 8 * VD], BF16)
    S = p.sb("S", [64, 16 * DV], F32)
    Sbf = p.sb("Sbf", [64, 16 * DV], BF16)
    Sin = p.sb("Sin", [64, 16 * DV], F32)
    Sinbf = p.sb("Sinbf", [64, 16 * DV], BF16)
    gend = p.sb("gend", [64, 16 * NCHK], F32)
    egnd = p.sb("egnd", [64, 16 * NCHK], F32)
    gstb = p.sb("gstb", [64, 16 * NCHK], F32)
    tri = p.sb("tri", [CHK, CHK], F32)
    msk = p.sb("msk", [CHK, CHK], BF16)
    rst = p.sb("rst", [64, 2048], F32)
    rst16 = p.sb("rst16_t", [64, 15], F32)
    psel = p.sb("psel", [64, 8], F32)
    normw = p.sb("normw", [128, DV], F32)
    ident = p.sb("identt", [128, 128], BF16)
    agb = p.sb("agb", [64, 16 * DV], F32)

    qq8 = p.sb("qq8", [64, 16 * 128], BF16)
    kq8 = p.sb("kq8", [64, 16 * 128], BF16)
    gkq8 = p.sb("gkq8", [64, 16 * 128], F32)
    Gq8 = p.sb("Gq8", [64, 16 * 128], F32)
    qts = p.sb("qts", [64, 16 * 128], BF16)
    kts = p.sb("kts", [64, 16 * 128], BF16)
    ktc = p.sb("ktc", [64, KD], BF16)
    vc = p.sb("vc", [64, VD], BF16)
    gktc = p.sb("gktc", [64, KD], F32)
    egmt = p.sb("egmt", [64, KD], F32)
    ktil = p.sb("ktil", [64, KD], BF16)
    atb = [p.sb(f"atb{i}", [64, 64], BF16) for i in range(4)]
    qhb = [p.sb(f"qhb{i}", [64, 64], BF16) for i in range(4)]

    gt = p.sb("gt", [128, VD], BF16)
    sgt = p.sb("sgt", [128, D], F32)
    sqt = p.sb("sqt", [128, D], F32)
    ms = p.sb("ms", [128, 16 * 2], F32)
    xrt = p.sb("xrt", [128, D], F32)
    aot = p.sb("aot", [128, D], F32)
    x1t = p.sb("x1t", [128, D], F32)
    ln1t = p.sb("ln1t", [128, D], BF16)
    ogfm = p.sb("ogfm", [128, ND * 128], BF16)
    wosl = p.sb("wosl0", [128, ND * 512], BF16)
    st8 = p.sb("st8", [128, 16], F32)
    lnao = sgt
    lnscr = sqt

    psA = [p.ps(f"psA{i}", [64, 512], F32) for i in range(2)]
    psO = [p.ps(f"psO{i}", [128, 512], F32) for i in range(2)]
    psS2 = [p.ps(f"psS2{i}", [64, 512], F32) for i in range(2)]
    pstr = p.ps("pstr", [128, 1024], BF16)
    pmm = [p.ps("pmm0", [128, 512], F32)]

    ld = p.sem("ld")
    qld = p.sem("qld")
    sv = p.sem("sv")
    vv = p.sem("vv")
    tsm = p.sem("tsm")
    xsm = p.sem("xsm")
    osm = p.sem("osm")
    ccs = p.sem("ccs")

    s = Sched(p)
    p.enable_consts(s, ld)
    for dst, src_ in ((tri[:], i_tri), (msk[:], i_mask), (rst[:], i_rst),
                      (rst16[:], i_rst16), (psel[:], i_pairsel),
                      (normw[:], i_normw), (ident[:], i_ident)):
        s.dma("sync", dst, src_, ld)
    s.op("vector", lambda h: h.memset(S[:], 0.0))
    s.op("vector", lambda h: h.memset(Sbf[:], 0.0))
    s.op("vector", lambda h: h.memset(Sin[:], 0.0))
    s.wait("sync", ld, ld.n)
    s.build()

    obf3 = obf[:].rearrange("p (c n) -> p c n", n=VD)

    def emit_qkprep(s, e8, with_q=True, with_k=True):
        sl0 = slice(0, 16 * 128)
        c0 = e8 * 128
        wq_ = s.dma("sync", qq8[:].rearrange("p (a t) -> p a t", t=128),
                    i_qfm[:, c0:c0 + 128].rearrange("(a p) t -> p a t", p=64),
                    qld) if with_q else None
        wk_ = s.dma("sync", kq8[:].rearrange("p (a t) -> p a t", t=128),
                    i_kfm[:, c0:c0 + 128].rearrange("(a p) t -> p a t", p=64),
                    qld) if with_k else None
        wg_ = s.dma("sync", gkq8[:].rearrange("p (a t) -> p a t", t=128),
                    i_gkfm[:, c0:c0 + 128].rearrange("(a p) t -> p a t", p=64),
                    qld)
        s.wait("vector", qld, wg_)
        s.op("vector", lambda h: h.tensor_tensor_scan(
            Gq8[:], rst[:], gkq8[:], 0.0, ALU.mult, ALU.add), incs=[(vv, 1)])
        tG = vv.inc()
        s.op("scalar", lambda h: h.activation(gkq8[:], Gq8[:], AF.Exp),
             waits=[(vv, tG)], incs=[(sv, 1)])
        t1 = sv.inc()
        if with_q:
            s.wait("vector", qld, wq_)
            s.op("vector", lambda h: h.scalar_tensor_tensor(
                qts[:], qq8[:], 0.125, gkq8[:], ALU.mult, ALU.mult),
                waits=[(sv, t1)], incs=[(vv, 1)])
        else:
            s.op("vector", lambda h: h.engine_nop(), waits=[(sv, t1)],
                 incs=[(vv, 1)])
        tq = vv.inc()
        if with_k:
            s.op("scalar", lambda h: h.activation(gkq8[:], Gq8[:], AF.Exp,
                                                  scale=-1.0),
                 waits=[(vv, tq)], incs=[(sv, 1)])
            t2 = sv.inc()
            s.wait("vector", qld, wk_)
            s.op("vector", lambda h: h.tensor_mul(kts[:], kq8[:], gkq8[:]),
                 waits=[(sv, t2)], incs=[(vv, 1)])
            vv.inc()
        return vv.n, tG

    for e8 in range(8):
        s = Sched(p)
        c0 = e8 * 128
        prep_v, tG = emit_qkprep(s, e8)
        gsl = Gq8[:]
        gv = bass.AP(gsl.tensor, gsl.offset + 63,
                     [[gsl.ap[0][0], 64], [128, 16], [64, 2]])
        gd = bass.AP(gend[:].tensor, gend[:].offset + 2 * e8,
                     [[gend[:].ap[0][0], 64], [NCHK, 16], [1, 2]])
        ed = bass.AP(egnd[:].tensor, egnd[:].offset + 2 * e8,
                     [[egnd[:].ap[0][0], 64], [NCHK, 16], [1, 2]])
        s.op("vector", lambda h, gv=gv, gd=gd: h.tensor_copy(gd, gv),
             waits=[(vv, prep_v)], incs=[(vv, 1)])
        tgd = vv.inc()
        s.op("scalar", lambda h, gd=gd, ed=ed: h.activation(ed, gd, AF.Exp),
             waits=[(vv, tgd)], incs=[(sv, 1)])
        tegn = sv.inc()
        for e in ("tensor", "vector"):
            s.wait(e, sv, sv.n)
            s.wait(e, vv, vv.n)

        hist = {}
        for cl in range(2):
            cg = 2 * e8 + cl
            rp = cl * 64
            if cl == 1:
                s.wait("sync", tsm, tsm.n)
                s.wait("sync", vv, vv.n)
            wk1 = s.dma("sync", ktc[:],
                        i_ktok[c0 + cl * 64:c0 + cl * 64 + 64, :], qld)
            wv1 = s.dma("sync", vc[:],
                        i_v[c0 + cl * 64:c0 + cl * 64 + 64, :], qld)
            wg1 = s.dma("sync", gktc[:],
                        i_gktok[c0 + cl * 64:c0 + cl * 64 + 64, :], qld)
            s.wait("tensor", qld, wg1)
            for half in range(2):
                s.op("tensor", lambda h, half=half: h.matmul(
                    psS2[0][0:64, :], tri[:],
                    gktc[:, half * 512:half * 512 + 512],
                    start=True, stop=True), incs=[(tsm, 1)])
                tj = tsm.inc()
                s.op("scalar", lambda h, half=half: h.activation(
                    egmt[:, half * 512:half * 512 + 512], psS2[0][0:64, :],
                    AF.Exp, scale=-1.0), waits=[(tsm, tj)], incs=[(sv, 1)])
                tk = sv.inc()
                s.wait("tensor", sv, tk)
            s.wait("vector", qld, wk1)
            s.op("vector", lambda h: h.tensor_mul(ktil[:], ktc[:], egmt[:]),
                 waits=[(sv, sv.n)], incs=[(vv, 1)])
            kt_ready = vv.inc()
            s.wait("tensor", vv, kt_ready)
            s.wait("tensor", qld, wv1)

            for hh in range(16):
                j = cg * 16 + hh
                qsl = qts[0:64, hh * 128 + cl * 64:hh * 128 + cl * 64 + 64]
                ksl = kts[0:64, hh * 128 + cl * 64:hh * 128 + cl * 64 + 64]
                vsl = vc[0:64, hh * DV:hh * DV + DV]
                bA, bO, bS = psA[hh % 2], psO[hh % 2], psS2[hh % 2]
                wA = [(vv, hist[('m', j - 2)])] if ('m', j - 2) in hist else []
                s.op("tensor", lambda h, b=bA, ksl=ksl, qsl=qsl: h.matmul(
                    b[0:64, 0:64], ksl, qsl, start=True, stop=True),
                    waits=wA, incs=[(tsm, 1)])
                t_at = tsm.inc()
                s.op("vector", lambda h, b=bA, j=j: h.tensor_mul(
                    atb[j % 4][:], b[0:64, 0:64], msk[:]),
                    waits=[(tsm, t_at)], incs=[(vv, 1)])
                hist[('m', j)] = t_msk = vv.inc()
                wO = [(vv, t_msk)]
                if ('oe', j - 2) in hist:
                    wO.append((vv, hist[('oe', j - 2)]))
                s.op("tensor", lambda h, b=bO, j=j, vsl=vsl, rp=rp: h.matmul(
                    b[rp:rp + 64, 0:DV], atb[j % 4][:], vsl, start=True,
                    stop=False, tile_position=(0, rp)), waits=wO)
                s.op("tensor", lambda h, b=bO, qsl=qsl, hh=hh, rp=rp: h.matmul(
                    b[rp:rp + 64, 0:DV], qsl, Sbf[:, hh * DV:hh * DV + DV],
                    start=False, stop=True, tile_position=(0, rp)),
                    incs=[(tsm, 1)])
                t_o = tsm.inc()
                wS = []
                if ('su', j - 2) in hist:
                    wS.append((vv, hist[('su', j - 2)]))
                if cl == 0 and hh < 2:
                    wS.append((sv, sv.n))
                s.op("tensor", lambda h, b=bS, hh=hh, vsl=vsl: h.matmul(
                    b[0:64, 0:DV], ktil[:, hh * 64:hh * 64 + 64], vsl,
                    start=True, stop=True), waits=wS, incs=[(tsm, 1)])
                t_s = tsm.inc()
                oc = obf3[rp:rp + 64, cg // 2, hh * DV:hh * DV + DV]
                s.op("vector", lambda h, b=bO, oc=oc, rp=rp: h.tensor_copy(
                    oc, b[rp:rp + 64, 0:DV]), waits=[(tsm, t_o)],
                    incs=[(vv, 1)])
                hist[('oe', j)] = vv.inc()
                egcol = egnd[0:64, hh * NCHK + cg:hh * NCHK + cg + 1]
                s.op("vector", lambda h, b=bS, hh=hh: h.tensor_add(
                    S[:, hh * DV:hh * DV + DV], S[:, hh * DV:hh * DV + DV],
                    b[0:64, 0:DV]), waits=[(tsm, t_s)], incs=[(vv, 1)])
                t_u1 = vv.inc()
                s.op("vector", lambda h, hh=hh, egcol=egcol: h.tensor_scalar(
                    S[:, hh * DV:hh * DV + DV], S[:, hh * DV:hh * DV + DV],
                    egcol, None, ALU.mult), waits=[(vv, t_u1)], incs=[(vv, 1)])
                t_u2 = vv.inc()
                s.op("vector", lambda h, hh=hh: h.tensor_copy(
                    Sbf[:, hh * DV:hh * DV + DV],
                    S[:, hh * DV:hh * DV + DV]), waits=[(vv, t_u2)],
                    incs=[(vv, 1)])
                hist[('su', j)] = vv.inc()
            s.wait("tensor", vv, vv.n)
        s.build()

    s = Sched(p)
    w1 = s.dma("gpsimd", cc_in.ap(), S[:], ccs)
    s.op("gpsimd", lambda h: h.collective_compute(
        "AllGather", ALU.bypass, replica_groups=[list(range(NCORES))],
        ins=[cc_in.ap().opt()], outs=[cc_out.ap().opt()]),
        waits=[(ccs, w1)], incs=[(ccs, 1)])
    wcc = ccs.inc()
    s.wait("sync", ccs, wcc)
    for r in range(NCORES):
        wr = s.dma("sync", agb[:], cc_out.ap()[r * 64:(r + 1) * 64, :], ccs)
        s.wait("vector", ccs, wr)
        s.op("vector", lambda h, r=r: h.scalar_tensor_tensor(
            Sin[:], agb[:], psel[:, r:r + 1], Sin[:], ALU.mult, ALU.add),
            incs=[(vv, 1)])
        wv2 = vv.inc()
        if r < NCORES - 1:
            s.wait("sync", vv, wv2)
    s.op("vector", lambda h: h.tensor_copy(Sinbf[:], Sin[:]),
         waits=[(vv, vv.n)], incs=[(vv, 1)])
    vv.inc()
    for hhh in range(16):
        b0 = hhh * NCHK
        s.op("vector", lambda h, b0=b0: h.memset(gstb[:, b0:b0 + 1], 0.0),
             incs=[(vv, 1)])
        t0 = vv.inc()
        s.op("vector", lambda h, b0=b0: h.tensor_tensor_scan(
            gstb[:, b0 + 1:b0 + NCHK], rst16[:], gend[:, b0:b0 + NCHK - 1],
            0.0, ALU.mult, ALU.add), waits=[(vv, t0)], incs=[(vv, 1)])
        vv.inc()
    s.op("scalar", lambda h: h.activation(gstb[:], gstb[:], AF.Exp),
         waits=[(vv, vv.n)], incs=[(sv, 1)])
    sv.inc()
    s.build()

    for e8 in range(8):
        s = Sched(p)
        if e8 == 0:
            for e in ("vector", "tensor"):
                s.wait(e, sv, sv.n)
        prep_v, _ = emit_qkprep(s, e8, with_q=True, with_k=False)
        s.wait("tensor", vv, prep_v)
        hist = {}
        for cl in range(2):
            cg = 2 * e8 + cl
            rp = cl * 64
            for hh in range(16):
                j = cg * 16 + hh
                wq_ = []
                if ('cm', j - 4) in hist:
                    wq_.append((tsm, hist[('cm', j - 4)]))
                s.op("vector", lambda h, j=j, hh=hh, cl=cl, cg=cg: h.tensor_scalar(
                    qhb[j % 4][:],
                    qts[0:64, hh * 128 + cl * 64:hh * 128 + cl * 64 + 64],
                    gstb[0:64, hh * NCHK + cg:hh * NCHK + cg + 1], None,
                    ALU.mult), waits=wq_, incs=[(vv, 1)])
                t_q = vv.inc()
                wmm = [(vv, t_q)]
                if ('ca', j - 2) in hist:
                    wmm.append((vv, hist[('ca', j - 2)]))
                s.op("tensor", lambda h, j=j, hh=hh, rp=rp: h.matmul(
                    psO[j % 2][rp:rp + 64, 0:DV], qhb[j % 4][:],
                    Sinbf[:, hh * DV:hh * DV + DV], start=True, stop=True,
                    tile_position=(0, rp)), waits=wmm, incs=[(tsm, 1)])
                hist[('cm', j)] = t_m = tsm.inc()
                oc = obf3[rp:rp + 64, cg // 2, hh * DV:hh * DV + DV]
                s.op("vector", lambda h, j=j, oc=oc, rp=rp: h.tensor_add(
                    oc, oc, psO[j % 2][rp:rp + 64, 0:DV]),
                    waits=[(tsm, t_m)], incs=[(vv, 1)])
                hist[('ca', j)] = vv.inc()
        s.build()

    s = Sched(p)
    for e in ("vector", "tensor", "scalar"):
        s.wait(e, vv, vv.n)
    mp = MMPipe(s, pmm, tsm, None, xsm)
    tp_ = MMPipe(s, [pstr], tsm, None, xsm)
    wo_w = [0]
    inv128 = 1.0 / DV
    for tt in range(NT):
        wg_ = s.dma("sync", gt[:], i_g[tt * 128:(tt + 1) * 128, :], qld)
        wx_ = s.dma("sync", xrt[:], i_xres[tt * 128:(tt + 1) * 128, :], qld)
        s.wait("scalar", qld, wg_)
        otile = obf[:, tt * VD * 8 // 8:tt * VD + VD] if False else \
            obf[:, tt * VD:(tt + 1) * VD]
        s.op("scalar", lambda h, otile=otile: h.activation(
            sqt[:, 0:VD], otile, AF.Square), incs=[(sv, 1)])
        t0 = sv.inc()
        s.op("vector", lambda h: h.tensor_reduce(
            ms[:, 0:16], sqt[:, 0:VD].rearrange("p (a b) -> p a b", b=DV),
            AX.X, ALU.add), waits=[(sv, t0)], incs=[(vv, 1)])
        t1 = vv.inc()
        s.op("scalar", lambda h: h.activation(
            ms[:, 16:32], ms[:, 0:16], AF.Ln, bias=p.cc(EPS),
            scale=p.cc(inv128)), waits=[(vv, t1)], incs=[(sv, 1)])
        t2 = sv.inc()
        s.op("scalar", lambda h: h.activation(
            ms[:, 0:16], ms[:, 16:32], AF.Exp, scale=p.cc(-0.5)),
            waits=[(sv, t2)], incs=[(sv, 1)])
        t3 = sv.inc()
        s.op("scalar", lambda h: h.activation(sgt[:, 0:VD], gt[:], AF.Silu),
             incs=[(sv, 1)])
        t4 = sv.inc()
        rstd_bc = bass.AP(ms[:, 0:16].tensor, ms[:, 0:16].offset,
                          [[ms[:, 0:16].ap[0][0], 128], [1, 16], [0, DV]])
        s.op("vector", lambda h, otile=otile, rb=rstd_bc: h.tensor_mul(
            sqt[:, 0:VD].rearrange("p (a b) -> p a b", b=DV),
            otile.rearrange("p (a b) -> p a b", b=DV), rb),
            waits=[(sv, t3)], incs=[(vv, 1)])
        t5 = vv.inc()
        nw_bc = bass.AP(normw[:].tensor, normw[:].offset,
                        [[normw[:].ap[0][0], 128], [0, 16], [1, DV]])
        s.op("vector", lambda h, nb=nw_bc: h.tensor_mul(
            sqt[:, 0:VD].rearrange("p (a b) -> p a b", b=DV),
            sqt[:, 0:VD].rearrange("p (a b) -> p a b", b=DV), nb),
            waits=[(vv, t5)], incs=[(vv, 1)])
        t6 = vv.inc()
        s.op("vector", lambda h: h.tensor_mul(sgt[:, 0:VD], sqt[:, 0:VD],
                                              sgt[:, 0:VD]),
             waits=[(vv, t6), (sv, t4)], incs=[(vv, 1)])
        t7 = vv.inc()
        s.op("vector", lambda h: h.tensor_copy(gt[:], sgt[:, 0:VD]),
             waits=[(vv, t7)], incs=[(vv, 1)])
        t8 = vv.inc()
        s.wait("tensor", vv, t8)
        for dc in range(ND):
            def tfn(h, b, dc=dc):
                return h.transpose(b[:, 0:128], gt[:, dc * 128:(dc + 1) * 128],
                                   ident[:])
            def efn(h, b, dc=dc):
                return h.tensor_copy(ogfm[:, dc * 128:(dc + 1) * 128],
                                     b[:, 0:128])
            tp_.job([tfn], [("vector", efn, True)])
        for nsl in range(4):
            if wo_w[0]:
                s.wait("gpsimd", tsm, wo_w[0])
            wld = s.dma("gpsimd", wosl[:].rearrange("p (c n) -> p c n", n=512),
                        i_wo[:, nsl * 512:(nsl + 1) * 512].rearrange(
                            "(c p) n -> p c n", p=128), ld)
            mms = []
            for dc in range(ND):
                def f(h, b, dc=dc):
                    return h.matmul(b[:], ogfm[:, dc * 128:(dc + 1) * 128],
                                    wosl[:, dc * 512:dc * 512 + 512],
                                    start=(dc == 0), stop=(dc == ND - 1))
                mms.append(f)
            def evf(h, b, nsl=nsl):
                return h.tensor_copy(aot[:, nsl * 512:nsl * 512 + 512], b[:])
            mp.job(mms, [("vector", evf, True)],
                   extra_tensor_waits=[(ld, wld)])
            wo_w[0] = tsm.n
        s.wait("scalar", xsm, xsm.n)
        s.wait("vector", xsm, xsm.n)
        s.wait("vector", qld, wx_)
        emit_ln(s, p, aot[:], lnao[:, 0:D], st8[:, 0:8], sv, vv, D, lnscr[:])
        s.op("vector", lambda h: h.tensor_add(x1t[:], xrt[:], lnao[:, 0:D]),
             waits=[(vv, vv.n)], incs=[(vv, 1)])
        tx1 = vv.inc()
        s.wait("sync", vv, tx1)
        s.dma("sync", o_x1[tt * 128:(tt + 1) * 128, :], x1t[:], osm)
        s.wait("scalar", vv, tx1)
        emit_ln(s, p, x1t[:], ln1t[:], st8[:, 8:16], sv, vv, D, lnscr[:])
        s.wait("tensor", vv, vv.n)
        for dc in range(ND):
            def tfn(h, b, dc=dc):
                return h.transpose(b[:, 0:128], ln1t[:, dc * 128:(dc + 1) * 128],
                                   ident[:])
            def efn(h, b, dc=dc):
                return h.tensor_copy(ogfm[:, dc * 128:(dc + 1) * 128],
                                     b[:, 0:128])
            tp_.job([tfn], [("vector", efn, True)])
        s.wait("sync", xsm, xsm.n)
        s.dma("sync", o_ln1fm[:, tt * 128:(tt + 1) * 128].rearrange(
            "(c p) t -> p c t", p=128),
            ogfm[:].rearrange("p (c t) -> p c t", t=128), osm)
        s.wait("vector", osm, osm.n)
        s.wait("scalar", osm, osm.n)
        s.wait("vector", tsm, tsm.n)
    s.wait("sync", osm, osm.n)
    s.build()
    p.close()
    return p



def build_F1(stop=3):
    import os
    stop = int(os.environ.get("F1_STOP", stop))
    p = Prog()
    nc = p.nc
    TB = 2048
    i_ln = p.din("ln1_fm", [D, TB], BF16)
    i_gwt = p.din("gate_WT", [D, 8], F32)
    i_w1r = p.din("W1r", [6 * D, 1408], F32)
    i_w2r = p.din("W2r", [6 * 1408, D], F32)
    i_b1t = p.din("b1T", [1408, 8], F32)
    i_b2 = p.din("b2rows", [6, D], F32)
    i_sel2 = p.din("sel2", [1, 2], F32)
    i_sel8 = p.din("sel8", [1, 8], F32)
    i_iotap = p.din("iota1e9", [1, 8], F32)
    o_out = p.dout("out_slot", [TB, D], F32)
    o_row = p.dout("row_dbg", [1, 48], F32)
    o_m8 = p.dout("m8_dbg", [1, 8], F32)
    o_idx = p.dout("idx_dbg", [128, 16], U32)
    o_idx2 = p.dout("idx2_dbg", [128, 16], U32)
    o_irep = p.dout("irep_dbg", [128, 1], U32)

    lnb = p.sb("lnb", [128, ND * 1024], BF16)
    gwt = p.sb("gwt", [128, ND * 8], BF16)
    w1b = p.sb("w1b", [128, ND * 1408], BF16)
    w2b = p.sb("w2b", [128, 11 * D], BF16)
    hq = p.sb("hq", [128, 11 * 1024], BF16)
    w1raw = [p.sb(f"w1raw{i}", [128, 1408], F32) for i in range(2)]
    w2raw = [p.sb(f"w2raw{i}", [128, D], F32) for i in range(2)]
    b1t = p.sb("b1t", [128, 11 * 8], F32)
    b1c = p.sb("b1c", [128, 11], F32)
    b2rep = p.sb("b2rep", [128, D], F32)
    sums = p.sb("sums", [128, 16], F32)
    sumsb = p.sb("sumsb", [128, 16], BF16)
    row = p.sb("row", [1, 48], F32)
    row8 = p.sb("row8", [1, 8], F32)
    m8 = p.sb("m8", [1, 8], F32)
    selr = p.sb("selr", [1, 8], F32)
    sel2r = p.sb("sel2r", [1, 2], F32)
    iotar = p.sb("iotar", [1, 8], F32)
    onesc = p.sb("onesc", [1, 128], F32)
    mywc = p.sb("mywc", [128, 1], F32)
    basec = p.sb("basec", [128, 1], F32)
    oh6bc = p.sb("oh6bc", [128, 8], F32)
    iom = p.sb("iom", [128, 16], F32)
    idxf = p.sb("idxf", [128, 16], F32)
    idxu = p.sb("idxu", [128, 16], U32)
    idx2u = p.sb("idx2u", [128, 16], U32)
    idxrep = p.sb("idxrep", [128, 1], U32)
    ev = [p.sb(f"ev{i}", [128, 512], F32) for i in range(8)]
    scr = p.sb("scrF", [128, 2048], F32)

    pmm = [p.ps(f"pmm{i}", [128, 512], F32) for i in range(6)]
    psm = p.ps("psm", [128, 512], F32)

    ld = p.sem("ld")
    gsm = p.sem("gsm")
    sv = p.sem("sv")
    vv = p.sem("vv")
    tsm = p.sem("tsm")
    xsm = p.sem("xsm")
    osm = p.sem("osm")

    s = Sched(p)
    p.enable_consts(s, ld)
    s.dma("gpsimd", gwt[:].rearrange("p (c n) -> p c n", n=8),
          i_gwt.rearrange("(c p) n -> p c n", p=128), ld)
    s.dma("sync", b1t[:].rearrange("p (c n) -> p c n", n=8),
          i_b1t[0:1408, :].rearrange("(c p) n -> p c n", p=128), ld)
    s.dma("sync", selr[:], i_sel8, ld)
    s.dma("sync", sel2r[:], i_sel2, ld)
    s.dma("sync", iotar[:], i_iotap, ld)
    wln = s.dma("sync", lnb[:].rearrange("p (c n) -> p c n", n=1024),
                i_ln[:, 0:1024].rearrange("(c p) n -> p c n", p=128), ld)
    s.op("vector", lambda h: h.memset(onesc[:], 1.0))
    s.op("gpsimd", lambda h: h.iota(iom[:], [[128, 16]], base=0,
                                    channel_multiplier=1,
                                    allow_small_or_imprecise_dtypes=True))
    s.wait("scalar", ld, wln)
    for dc in range(ND):
        s.op("scalar", lambda h, dc=dc: h.activation(
            scr[:, 0:1024], lnb[:, dc * 1024:(dc + 1) * 1024], AF.Copy,
            accum_out=sums[:, dc:dc + 1]),
            incs=[(sv, 1)] if dc == ND - 1 else ())
    t_s1 = sv.inc()
    for dc in range(ND):
        if dc > 0:
            s.wait("sync", sv, sv.n)
        w2_ = s.dma("sync", hq[:, 0:1024],
                    i_ln[dc * 128:(dc + 1) * 128, 1024:2048], ld)
        s.wait("scalar", ld, w2_)
        s.op("scalar", lambda h, dc=dc: h.activation(
            scr[:, 1024:2048], hq[:, 0:1024], AF.Copy,
            accum_out=idxf[:, dc:dc + 1]),
            incs=[(sv, 1)])
        sv.inc()
    s.wait("vector", sv, sv.n)
    s.op("vector", lambda h: h.tensor_add(sums[:], sums[:], idxf[:]),
         incs=[(vv, 1)])
    t_sum = vv.inc()
    s.op("vector", lambda h: h.tensor_copy(sumsb[:], sums[:]),
         waits=[(vv, t_sum)], incs=[(vv, 1)])
    t_sb = vv.inc()
    s.wait("tensor", vv, t_sb)
    s.wait("tensor", ld, 16 * 2)
    for dc in range(ND):
        s.op("tensor", lambda h, dc=dc: h.matmul(
            psm[0:1, 0:8], sumsb[:, dc:dc + 1], gwt[:, dc * 8:dc * 8 + 8],
            start=(dc == 0), stop=(dc == ND - 1)),
            incs=[(tsm, 1)] if dc == ND - 1 else ())
    t_mm = tsm.inc()
    def rop(fn, wait_prev=True):
        s.op("vector", fn, waits=[(vv, vv.n)], incs=[(vv, 1)])
        vv.inc()
    s.op("vector", lambda h: h.tensor_scalar(row[:, 0:8], psm[0:1, 0:8],
                                             1.0 / TB, None, ALU.mult),
         waits=[(tsm, t_mm)], incs=[(vv, 1)])
    vv.inc()
    rop(lambda h: h.scalar_tensor_tensor(row[:, 0:8], selr[:], -1e30,
                                         row[:, 0:8], ALU.mult, ALU.add))
    rop(lambda h: h.reduce_max(row[:, 16:17], row[:, 0:8], AX.X, ALU.max)
        if False else h.reduce_max(row[:, 16:17], row[:, 0:8], AX.X))
    rop(lambda h: h.tensor_scalar(row[:, 8:16], row[:, 0:8], row[:, 16:17],
                                  None, ALU.subtract))
    s.op("scalar", lambda h: h.activation(row[:, 8:16], row[:, 8:16], AF.Exp),
         waits=[(vv, vv.n)], incs=[(sv, 1)])
    sv.inc()
    s.wait("vector", sv, sv.n)
    rop(lambda h: h.tensor_reduce(row[:, 17:18], row[:, 8:16], AX.X, ALU.add))
    rop(lambda h: h.reciprocal(row[:, 18:19], row[:, 17:18]))
    rop(lambda h: h.tensor_scalar(row[:, 8:16], row[:, 8:16], row[:, 18:19],
                                  None, ALU.mult))
    rop(lambda h: h.max(m8[:], row[:, 8:16]))
    rop(lambda h: h.tensor_mul(row[:, 20:22], m8[:, 0:2], sel2r[:]))
    rop(lambda h: h.tensor_reduce(row[:, 22:23], row[:, 20:22], AX.X, ALU.add))
    rop(lambda h: h.tensor_reduce(row[:, 23:24], m8[:, 0:2], AX.X, ALU.add))
    rop(lambda h: h.reciprocal(row[:, 24:25], row[:, 23:24]))
    rop(lambda h: h.tensor_mul(row[:, 25:26], row[:, 22:23], row[:, 24:25]))
    rop(lambda h: h.tensor_scalar(row8[:], row[:, 8:16], m8[:, 0:1], None,
                                  ALU.is_ge))
    rop(lambda h: h.scalar_tensor_tensor(row8[:], row8[:], -1000.0, iotar[:],
                                         ALU.mult, ALU.add))
    rop(lambda h: h.tensor_reduce(row[:, 26:27], row8[:], AX.X, ALU.min))
    rop(lambda h: h.tensor_scalar(row8[:], iotar[:], row[:, 26:27], None,
                                  ALU.subtract))
    rop(lambda h: h.tensor_scalar(row8[:], row8[:], 1000.0, None, ALU.is_equal))
    rop(lambda h: h.scalar_tensor_tensor(row[:, 28:36], row8[:], -2.0,
                                         row[:, 8:16], ALU.mult, ALU.add))
    rop(lambda h: h.tensor_scalar(row[:, 28:36], row[:, 28:36], m8[:, 1:2],
                                  None, ALU.is_ge))
    rop(lambda h: h.scalar_tensor_tensor(row[:, 28:36], row[:, 28:36], -1000.0,
                                         iotar[:], ALU.mult, ALU.add))
    rop(lambda h: h.tensor_reduce(row[:, 27:28], row[:, 28:36], AX.X, ALU.min))
    rop(lambda h: h.tensor_mul(row[:, 36:37], row[:, 26:27], sel2r[:, 0:1]))
    rop(lambda h: h.scalar_tensor_tensor(row[:, 37:38], row[:, 27:28],
                                         1.0, row[:, 36:37], ALU.mult,
                                         ALU.add) if False else
        h.scalar_tensor_tensor(row[:, 37:38], sel2r[:, 1:2], 1.0,
                               row[:, 36:37], ALU.mult, ALU.add))
    rop(lambda h: h.tensor_mul(row[:, 38:39], row[:, 27:28], sel2r[:, 1:2]))
    rop(lambda h: h.tensor_add(row[:, 39:40], row[:, 36:37], row[:, 38:39]))
    rop(lambda h: h.tensor_scalar(row[:, 40:48], iotar[:], 1000.0, None,
                                  ALU.subtract))
    rop(lambda h: h.tensor_scalar(row[:, 40:48], row[:, 40:48],
                                  row[:, 39:40], None, ALU.is_equal))
    t_gate = vv.n

    s.op("vector", lambda h: h.tensor_scalar(row[:, 41 - 41 + 28:29],
                                             row[:, 39:40], float(D), None,
                                             ALU.mult),
         waits=[(vv, t_gate)], incs=[(vv, 1)])
    t_base = vv.inc()
    s.wait("tensor", vv, t_base)
    bc_jobs = [(mywc, row[:, 25:26], 1), (basec, row[:, 28:29], 1),
               (oh6bc, row[:, 40:48], 8)]
    for bi, (dst, srcr, ncol) in enumerate(bc_jobs):
        s.op("tensor", lambda h, srcr=srcr, ncol=ncol: h.matmul(
            psm[0:128, 8:8 + ncol], onesc[:], srcr, start=True, stop=True),
            incs=[(tsm, 1)])
        tb = tsm.inc()
        s.op("vector", lambda h, dst=dst, ncol=ncol: h.tensor_copy(
            dst[:, 0:ncol], psm[0:128, 8:8 + ncol]), waits=[(tsm, tb)],
            incs=[(vv, 1)])
        tv = vv.inc()
        s.wait("tensor", vv, tv)
    s.op("vector", lambda h: h.tensor_scalar(idxf[:], iom[:], basec[:, 0:1],
                                             None, ALU.add),
         waits=[(vv, vv.n)], incs=[(vv, 1)])
    t_i = vv.inc()
    s.op("vector", lambda h: h.tensor_copy(idxu[:], idxf[:]),
         waits=[(vv, t_i)], incs=[(vv, 1)])
    t_iu = vv.inc()
    s.op("vector", lambda h: h.tensor_scalar(idxf[:], row[:, 39:40],
                                             1408.0, None, ALU.mult),
         waits=[(vv, vv.n)], incs=[(vv, 1)]) if False else None
    s.op("vector", lambda h: h.tensor_scalar(
        row[:, 29:30], row[:, 39:40], 1408.0, None, ALU.mult),
        waits=[(vv, t_iu)], incs=[(vv, 1)])
    t_b2 = vv.inc()
    s.op("tensor", lambda h: h.matmul(psm[0:128, 8:9], onesc[:],
                                      row[:, 29:30], start=True, stop=True),
         waits=[(vv, t_b2)], incs=[(tsm, 1)])
    t_mm2 = tsm.inc()
    s.op("vector", lambda h: h.tensor_copy(basec[:], psm[0:128, 8:9]),
         waits=[(tsm, t_mm2)], incs=[(vv, 1)])
    t_b2c = vv.inc()
    s.op("vector", lambda h: h.tensor_scalar(idxf[:], iom[:], basec[:, 0:1],
                                             None, ALU.add),
         waits=[(vv, t_b2c)], incs=[(vv, 1)])
    t_i2 = vv.inc()
    s.op("vector", lambda h: h.tensor_copy(idx2u[:], idxf[:]),
         waits=[(vv, t_i2)], incs=[(vv, 1)])
    t_iu2 = vv.inc()
    for hc in range(11):
        s.op("vector", lambda h, hc=hc: h.tensor_mul(
            scr[:, 0:8], b1t[:, hc * 8:hc * 8 + 8], oh6bc[:]),
            waits=[(vv, vv.n)], incs=[(vv, 1)])
        ta = vv.inc()
        s.op("vector", lambda h, hc=hc: h.tensor_reduce(
            b1c[:, hc:hc + 1], scr[:, 0:8], AX.X, ALU.add),
            waits=[(vv, ta)], incs=[(vv, 1)])
        vv.inc()
    s.op("vector", lambda h: h.tensor_copy(idxrep[:],
                                           basec[:, 0:1]),
         waits=[(vv, t_b2c)], incs=[(vv, 1)]) if False else None
    s.op("tensor", lambda h: h.matmul(psm[0:128, 8:9], onesc[:],
                                      row[:, 39:40], start=True, stop=True),
         waits=[(vv, vv.n)], incs=[(tsm, 1)])
    t_mm3 = tsm.inc()
    s.op("vector", lambda h: h.tensor_copy(basec[:], psm[0:128, 8:9]),
         waits=[(tsm, t_mm3)], incs=[(vv, 1)])
    t_me = vv.inc()
    s.op("vector", lambda h: h.tensor_copy(idxrep[:], basec[:]),
         waits=[(vv, t_me)], incs=[(vv, 1)])
    t_irep = vv.inc()
    s.wait("gpsimd", vv, max(t_iu, t_iu2, t_irep))
    s.wait("sync", vv, vv.n)
    s.dma("sync", o_row, row[:], osm)
    s.dma("sync", o_m8, m8[:], osm)
    s.dma("sync", o_idx, idxu[:], osm)
    s.dma("sync", o_idx2, idx2u[:], osm)
    s.dma("sync", o_irep, idxrep[:], osm)
    s.wait("sync", osm, osm.n)
    if stop == 1:
        s.build()
        p.close()
        return p
    wb2 = s.dma("gpsimd", b2rep[:], None, gsm) if False else None
    s.op("gpsimd", lambda h: h.indirect_dma_start(
        out=b2rep[:], out_offset=None, in_=i_b2,
        in_offset=IndirectOffsetOnAxis(ap=idxrep[:, 0:1], axis=0)),
        incs=[(gsm, 16)])
    s.wait("gpsimd", gsm, gsm.inc(16))
    s.build()

    s = Sched(p)
    for i in range(ND):
        bi = i % 2
        if i >= 2:
            s.wait("gpsimd", vv, gcast[i - 2])
        s.op("gpsimd", lambda h, i=i, bi=bi: h.indirect_dma_start(
            out=w1raw[bi][:], out_offset=None, in_=i_w1r,
            in_offset=IndirectOffsetOnAxis(ap=idxu[:, i:i + 1], axis=0)),
            incs=[(gsm, 16)])
        wg = gsm.inc(16)
        s.wait("gpsimd", gsm, wg)
        s.wait("vector", gsm, wg)
        s.op("vector", lambda h, i=i, bi=bi: h.tensor_copy(
            w1b[:, i * 1408:(i + 1) * 1408], w1raw[bi][:]), incs=[(vv, 1)])
        if i == 0:
            gcast = {}
        gcast[i] = vv.inc()
    for i in range(11):
        bi = i % 2
        if i >= 2:
            s.wait("gpsimd", vv, gcast2[i - 2])
        s.op("gpsimd", lambda h, i=i, bi=bi: h.indirect_dma_start(
            out=w2raw[bi][:], out_offset=None, in_=i_w2r,
            in_offset=IndirectOffsetOnAxis(ap=idx2u[:, i:i + 1], axis=0)),
            incs=[(gsm, 16)])
        wg = gsm.inc(16)
        s.wait("gpsimd", gsm, wg)
        s.wait("vector", gsm, wg)
        s.op("vector", lambda h, i=i, bi=bi: h.tensor_copy(
            w2b[:, i * D:(i + 1) * D], w2raw[bi][:]), incs=[(vv, 1)])
        if i == 0:
            gcast2 = {}
        gcast2[i] = vv.inc()
    s.wait("vector", vv, vv.n)
    s.build()
    if stop == 2:
        p.close()
        return p

    s = Sched(p)
    s.op("vector", lambda h: h.tensor_scalar(b2rep[:], b2rep[:],
                                             mywc[:, 0:1], None, ALU.mult),
         incs=[(vv, 1)])
    vv.inc()
    s.build()

    for half in range(2):
        s = Sched(p)
        if half == 1:
            wlh = s.dma("sync", lnb[:].rearrange("p (c n) -> p c n", n=1024),
                        i_ln[:, 1024:2048].rearrange("(c p) n -> p c n",
                                                     p=128), ld)
            for e in ("tensor",):
                s.wait(e, ld, wlh)
        mp = MMPipe(s, pmm, tsm, None, xsm)
        bm = BufMgr(s)
        stj = [0]
        for hc in range(11):
            for th in range(2):
                mms = []
                for dc in range(ND):
                    def f(h, b, dc=dc, hc=hc, th=th):
                        return h.matmul(
                            b[:], w1b[:, dc * 1408 + hc * 128:
                                      dc * 1408 + hc * 128 + 128],
                            lnb[:, dc * 1024 + th * 512:
                                dc * 1024 + th * 512 + 512],
                            start=(dc == 0), stop=(dc == ND - 1))
                    mms.append(f)
                def evf(h, b, hc=hc, th=th):
                    return h.activation(
                        hq[:, hc * 1024 + th * 512:hc * 1024 + th * 512 + 512],
                        b[:], AF.Gelu, bias=b1c[:, hc:hc + 1])
                mp.job(mms, [("scalar", evf, True)])
        for tt in range(8):
            for osl in range(4):
                mms = []
                for hc in range(11):
                    def f(h, b, hc=hc, tt=tt, osl=osl):
                        return h.matmul(
                            b[:], hq[:, hc * 1024 + tt * 128:
                                     hc * 1024 + tt * 128 + 128],
                            w2b[:, hc * D + osl * 512:hc * D + osl * 512 + 512],
                            start=(hc == 0), stop=(hc == 10))
                    mms.append(f)
                j = stj[0]
                stj[0] += 1
                buf = ev[j % 8]
                bm.guard("vector", buf, osm)
                def evf(h, b, buf=buf, osl=osl):
                    return h.scalar_tensor_tensor(
                        buf[:], b[:], mywc[:, 0:1],
                        b2rep[:, osl * 512:osl * 512 + 512],
                        ALU.mult, ALU.add)
                def after(lx, buf=buf, tt=tt, osl=osl, half=half):
                    s.op("sync", lambda h: h.dma_start(
                        out=o_out[half * 1024 + tt * 128:
                                  half * 1024 + (tt + 1) * 128,
                                  osl * 512:osl * 512 + 512], in_=buf[:]),
                        waits=[(xsm, lx)], incs=[(osm, 16)])
                    bm.mark(buf, osm.inc(16))
                mp.job(mms, [("vector", evf, True)], after)
        s.build()
    p.close()
    return p





def build_F2M1():
    p = Prog()
    nc = p.nc
    i_s0 = p.din("slot0", [TLOC, D], F32)
    i_s1 = p.din("slot1", [TLOC, D], F32)
    i_x1 = p.din("x1", [TLOC, D], F32)
    i_wint = p.din("W_inT", [D, 2 * DIN], F32)
    i_ident = p.din("ident", [128, 128], BF16)
    o_x2 = p.dout("x2", [TLOC, D], F32)
    o_xz = p.dout("xz_fm", [2 * DIN, TLOC], BF16)

    t0 = p.sb("t0", [128, D], F32)
    t1 = p.sb("t1", [128, D], F32)
    x2t = p.sb("x2t", [128, D], F32)
    ln2t = p.sb("ln2t", [128, D], BF16)
    lnscr = p.sb("lnscr", [128, D], F32)
    st8 = p.sb("st8", [128, 16], F32)
    ident = p.sb("identt", [128, 128], BF16)
    ln2fm = p.sb("ln2fm", [128, ND * TLOC], BF16)
    wsl = [p.sb(f"wsl{i}", [128, ND * 512], BF16) for i in range(2)]
    ev = [p.sb(f"ev{i}", [128, 512], BF16) for i in range(8)]

    pstr = p.ps("pstr", [128, 1024], BF16)
    pmm = [p.ps(f"pmm{i}", [128, 512], F32) for i in range(6)]

    ld = p.sem("ld")
    sv = p.sem("sv")
    vv = p.sem("vv")
    tsm = p.sem("tsm")
    xsm = p.sem("xsm")
    osm = p.sem("osm")

    s = Sched(p)
    p.enable_consts(s, ld)
    wid = s.dma("sync", ident[:], i_ident, ld)
    s.wait("tensor", ld, wid)
    s.wait("scalar", ld, 16)
    tp_ = MMPipe(s, [pstr], tsm, None, xsm)
    for tt in range(NT):
        sl = slice(tt * 128, (tt + 1) * 128)
        w0 = s.dma("sync", t0[:], i_s0[sl, :], ld)
        w1 = s.dma("sync", t1[:], i_s1[sl, :], ld)
        wx = s.dma("sync", x2t[:], i_x1[sl, :], ld)
        s.wait("vector", ld, w1)
        s.op("vector", lambda h: h.tensor_add(t0[:], t0[:], t1[:]),
             incs=[(vv, 1)])
        ta = vv.inc()
        s.wait("scalar", vv, ta)
        emit_ln(s, p, t0[:], t1[:], st8[:, 0:8], sv, vv, D, lnscr[:])
        s.wait("vector", ld, wx)
        s.op("vector", lambda h: h.tensor_add(x2t[:], x2t[:], t1[:]),
             waits=[(vv, vv.n)], incs=[(vv, 1)])
        tx2 = vv.inc()
        s.wait("sync", vv, tx2)
        s.dma("sync", o_x2[sl, :], x2t[:], osm)
        s.wait("scalar", vv, tx2)
        emit_ln(s, p, x2t[:], ln2t[:], st8[:, 8:16], sv, vv, D, lnscr[:])
        s.wait("tensor", vv, vv.n)
        for dc in range(ND):
            def tfn(h, b, dc=dc):
                return h.transpose(b[:, 0:128], ln2t[:, dc * 128:(dc + 1) * 128],
                                   ident[:])
            def efn(h, b, dc=dc, tt=tt):
                return h.tensor_copy(
                    ln2fm[:, dc * TLOC + tt * 128:dc * TLOC + (tt + 1) * 128],
                    b[:, 0:128])
            tp_.job([tfn], [("vector", efn, True)])
        s.wait("sync", vv, vv.n)
        s.wait("sync", xsm, xsm.n)
    s.build()

    s = Sched(p)
    mp = MMPipe(s, pmm, tsm, None, xsm)
    bm = BufMgr(s)
    stj = [0]
    wprev = [0, 0]
    for slab in range(16):
        bi = slab % 2
        if wprev[bi]:
            s.wait("gpsimd", tsm, wprev[bi])
        wld = s.dma("gpsimd", wsl[bi][:].rearrange("p (c n) -> p c n", n=512),
                    i_wint[:, slab * 512:(slab + 1) * 512].rearrange(
                        "(c p) n -> p c n", p=128), ld)
        first = True
        for nch in range(4):
            for th in range(2):
                mms = []
                for dc in range(ND):
                    def f(h, b, dc=dc, nch=nch, th=th, bi=bi):
                        return h.matmul(
                            b[:],
                            wsl[bi][:, dc * 512 + nch * 128:dc * 512 + nch * 128 + 128],
                            ln2fm[:, dc * TLOC + th * 512:dc * TLOC + th * 512 + 512],
                            start=(dc == 0), stop=(dc == ND - 1))
                    mms.append(f)
                j = stj[0]
                stj[0] += 1
                buf = ev[j % 8]
                bm.guard("vector", buf, osm)
                def evf(h, b, buf=buf):
                    return h.tensor_copy(buf[:], b[:])
                def after(lx, buf=buf, slab=slab, nch=nch, th=th):
                    s.op("sync", lambda h: h.dma_start(
                        out=o_xz[slab * 512 + nch * 128:slab * 512 + (nch + 1) * 128,
                                 th * 512:th * 512 + 512], in_=buf[:]),
                        waits=[(xsm, lx)], incs=[(osm, 16)])
                    bm.mark(buf, osm.inc(16))
                mp.job(mms, [("vector", evf, True)], after,
                       extra_tensor_waits=[(ld, wld)] if first else ())
                first = False
        wprev[bi] = tsm.n
    s.wait("sync", osm, osm.n)
    s.build()
    p.close()
    return p



def build_M2():
    p = Prog()
    nc = p.nc
    i_xh = p.din("xh_fm", [DIN, TLOC + 3], BF16)
    i_cw = p.din("conv_w", [128, 32 * 4], F32)
    i_cb = p.din("conv_b", [128, 32], F32)
    i_xpt = p.din("x_projT", [DIN, 160], F32)
    i_dpt = p.din("dt_projT", [128, DIN], F32)
    i_dpb2 = p.din("dpb2", [128, 32], F32)
    o_xc = p.dout("xc_fm", [DIN, TLOC], F32)
    o_dt = p.dout("dt_fm", [DIN, TLOC], F32)
    o_bc = p.dout("bc_rows", [32, TLOC], F32)

    xht = p.sb("xht", [128, TLOC + 3], BF16)
    cw = p.sb("cw", [128, 32 * 4], F32)
    cb = p.sb("cb", [128, 32], F32)
    acc = p.sb("acc", [128, TLOC], F32)
    xcf = p.sb("xcf", [128, TLOC], F32)
    xcb = p.sb("xcb", [128, 32 * TLOC], BF16)
    xpt = p.sb("xpt", [128, 32 * 160], BF16)
    dpt = p.sb("dpt", [128, DIN], BF16)
    dpb = p.sb("dpb", [128, 32], F32)
    dtlow = p.sb("dtlow", [128, TLOC], BF16)
    u_ = p.sb("u_", [128, 512], F32)
    uc_ = p.sb("uc_", [128, 512], F32)
    ev = [p.sb(f"ev{i}", [128, 512], F32) for i in range(4)]

    pmm = [p.ps(f"pmm{i}", [128, 512], F32) for i in range(6)]

    ld = p.sem("ld")
    sv = p.sem("sv")
    vv = p.sem("vv")
    tsm = p.sem("tsm")
    xsm = p.sem("xsm")
    osm = p.sem("osm")

    s = Sched(p)
    p.enable_consts(s, ld)
    s.dma("sync", cw[:], i_cw, ld)
    wcb = s.dma("sync", cb[:], i_cb, ld)
    s.dma("gpsimd", xpt[:].rearrange("p (c n) -> p c n", n=160),
          i_xpt.rearrange("(c p) n -> p c n", p=128), ld)
    wdp = s.dma("gpsimd", dpt[:], i_dpt, ld)
    s.dma("sync", dpb[:], i_dpb2, ld)
    s.wait("scalar", ld, ld.n)
    s.wait("vector", ld, ld.n)
    for dc in range(32):
        wx = s.dma("sync", xht[:], i_xh[dc * 128:(dc + 1) * 128, :], ld)
        s.wait("vector", ld, wx)
        s.op("vector", lambda h, dc=dc: h.tensor_scalar(
            acc[:], xht[:, 0:TLOC], cw[:, 4 * dc:4 * dc + 1], None, ALU.mult),
            incs=[(vv, 1)])
        tv = vv.inc()
        for j in range(1, 4):
            s.op("vector", lambda h, dc=dc, j=j: h.scalar_tensor_tensor(
                acc[:], xht[:, j:j + TLOC], cw[:, 4 * dc + j:4 * dc + j + 1],
                acc[:], ALU.mult, ALU.add), waits=[(vv, tv)], incs=[(vv, 1)])
            tv = vv.inc()
        s.op("scalar", lambda h, dc=dc: h.activation(
            xcf[:], acc[:], AF.Silu, bias=cb[:, dc:dc + 1]),
            waits=[(vv, tv)], incs=[(sv, 1)])
        ts_ = sv.inc()
        s.op("vector", lambda h, dc=dc: h.tensor_copy(
            xcb[:, dc * TLOC:(dc + 1) * TLOC], xcf[:]), waits=[(sv, ts_)],
            incs=[(vv, 1)])
        tv2 = vv.inc()
        s.wait("sync", sv, ts_)
        s.dma("sync", o_xc[dc * 128:(dc + 1) * 128, :], xcf[:], osm)
        s.wait("sync", vv, tv)
        s.wait("scalar", osm, osm.n)
    s.build()

    s = Sched(p)
    mp = MMPipe(s, pmm, tsm, None, xsm)
    for nch in range(2):
        ncols = 128 if nch == 0 else 32
        for th in range(2):
            mms = []
            for dc in range(32):
                def f(h, b, dc=dc, nch=nch, th=th, ncols=ncols):
                    return h.matmul(
                        b[0:ncols, :],
                        xpt[:, dc * 160 + nch * 128:dc * 160 + nch * 128 + ncols],
                        xcb[:, dc * TLOC + th * 512:dc * TLOC + th * 512 + 512],
                        start=(dc == 0), stop=(dc == 31))
                mms.append(f)
            if nch == 0:
                def evf(h, b, th=th):
                    return h.tensor_copy(dtlow[:, th * 512:th * 512 + 512],
                                         b[0:128, :])
            else:
                def evf(h, b, th=th):
                    return h.tensor_copy(ev[th][:, 0:512][0:32, :], b[0:32, :])
            def after(lx, nch=nch, th=th):
                if nch == 1:
                    s.op("sync", lambda h, th=th: h.dma_start(
                        out=o_bc[:, th * 512:th * 512 + 512],
                        in_=ev[th][0:32, 0:512]),
                        waits=[(xsm, lx)], incs=[(osm, 16)])
                    osm.inc(16)
            mp.job(mms, [("vector", evf, True)], after)
    for dc in range(32):
        for th in range(2):
            mms = [lambda h, b, dc=dc, th=th: h.matmul(
                b[:], dpt[:, dc * 128:(dc + 1) * 128],
                dtlow[:, th * 512:th * 512 + 512], start=True, stop=True)]
            steps = [
                ("vector", lambda h, b, dc=dc: h.tensor_scalar(
                    u_[:], b[:], dpb[:, dc:dc + 1], None, ALU.add), True),
                ("vector", lambda h, b: h.tensor_scalar(
                    uc_[:], u_[:], 70.0, None, ALU.min), False),
                ("scalar", lambda h, b: h.activation(uc_[:], uc_[:], AF.Exp),
                 False),
                ("scalar", lambda h, b: h.activation(uc_[:], uc_[:], AF.Ln,
                                                     bias=1.0), False),
                ("vector", lambda h, b, dc=dc: h.tensor_max(
                    u_[:], uc_[:], u_[:]), False),
            ]
            def after(lx, dc=dc, th=th):
                s.op("sync", lambda h, dc=dc, th=th: h.dma_start(
                    out=o_dt[dc * 128:(dc + 1) * 128, th * 512:th * 512 + 512],
                    in_=u_[:]), waits=[(xsm, lx)], incs=[(osm, 16)])
                osm.inc(16)
                s.wait("vector", osm, osm.n)
            mp.job(mms, steps, after)
    s.wait("sync", osm, osm.n)
    s.build()
    p.close()
    return p



def build_M3():
    p = Prog()
    nc = p.nc
    TF = 2048
    CK = 256
    NCK = TF // CK
    i_dt = p.din("dt", [2048, TF], F32)
    i_xc = p.din("xc", [2048, TF], F32)
    i_z = p.din("z_fm", [2048, TF], BF16)
    i_B = p.din("Brows", [16, TF], F32)
    i_C = p.din("Crows", [16, TF], F32)
    i_al = p.din("A_log", [128, 16 * 16], F32)
    i_D = p.din("Dcol", [128, 16], F32)
    i_wot = p.din("WoutT_sl", [2048, D], F32)
    o_part = p.dout("part", [TF, D], F32)

    dtq = p.sb("dtq", [128, 16 * CK], F32)
    xcq = p.sb("xcq", [128, 16 * CK], F32)
    dxq = p.sb("dxq", [128, 16 * CK], F32)
    zq = p.sb("zq", [128, 16 * CK], BF16)
    yq = p.sb("yq", [128, 16 * CK], F32)
    yzb = p.sb("yzb", [128, 16 * CK], BF16)
    Bbc = p.sb("Bbc", [128, 16 * CK], BF16)
    Cbc = p.sb("Cbc", [128, 16 * CK], BF16)
    Brow = p.sb("Brow", [1, 16 * CK], F32)
    Crow = p.sb("Crow", [1, 16 * CK], F32)
    onesc = p.sb("onesc", [1, 128], F32)
    acol = p.sb("acol", [128, 16 * 16], F32)
    aq = p.sb("aq", [128, CK], F32)
    bq = p.sb("bq", [128, CK], F32)
    hq = p.sb("hq", [128, CK], F32)
    Hc = p.sb("Hc", [128, 16 * 16], F32)
    Dc = p.sb("Dc", [128, 16], F32)
    wsl = p.sb("wsl0", [128, 16 * 512], BF16)
    ev = [p.sb(f"ev{i}", [128, 512], F32) for i in range(6)]

    pmm = [p.ps(f"pmm{i}", [128, 512], F32) for i in range(6)]

    ld = p.sem("ld")
    sv = p.sem("sv")
    vv = p.sem("vv")
    tsm = p.sem("tsm")
    xsm = p.sem("xsm")
    osm = p.sem("osm")

    s = Sched(p)
    p.enable_consts(s, ld)
    wal = s.dma("sync", acol[:], i_al, ld)
    s.dma("sync", Dc[:], i_D, ld)
    s.op("vector", lambda h: h.memset(onesc[:], 1.0))
    s.op("vector", lambda h: h.memset(Hc[:], 0.0))
    s.wait("scalar", ld, wal)
    s.op("scalar", lambda h: h.activation(acol[:], acol[:], AF.Exp),
         incs=[(sv, 1)])
    ts0 = sv.inc()
    s.op("vector", lambda h: h.tensor_scalar(acol[:], acol[:], -1.0, None,
                                             ALU.mult), waits=[(sv, ts0)],
         incs=[(vv, 1)])
    vv.inc()
    s.build()

    for ck in range(NCK):
        s = Sched(p)
        c0 = ck * CK
        wdt = s.dma("sync", dtq[:].rearrange("p (c n) -> p c n", n=CK),
                    i_dt[:, c0:c0 + CK].rearrange("(c p) n -> p c n", p=128), ld)
        wxc = s.dma("sync", xcq[:].rearrange("p (c n) -> p c n", n=CK),
                    i_xc[:, c0:c0 + CK].rearrange("(c p) n -> p c n", p=128), ld)
        wz = s.dma("sync", zq[:].rearrange("p (c n) -> p c n", n=CK),
                   i_z[:, c0:c0 + CK].rearrange("(c p) n -> p c n", p=128), ld)
        wB = s.dma("sync", Brow[:].rearrange("p (a n) -> p a n", n=CK),
                   i_B[:, c0:c0 + CK].rearrange("(p a) n -> p a n", p=1), ld)
        wC = s.dma("sync", Crow[:].rearrange("p (a n) -> p a n", n=CK),
                   i_C[:, c0:c0 + CK].rearrange("(p a) n -> p a n", p=1), ld)
        s.wait("tensor", ld, wC)
        mpb = MMPipe(s, pmm[0:3], tsm, None, xsm)
        for sx in range(16):
            for dst, rowt in ((Bbc, Brow), (Cbc, Crow)):
                def f(h, b, sx=sx, rowt=rowt):
                    return h.matmul(b[0:128, 0:CK], onesc[:],
                                    rowt[0:1, sx * CK:(sx + 1) * CK],
                                    start=True, stop=True)
                def e(h, b, sx=sx, dst=dst):
                    return h.tensor_copy(dst[:, sx * CK:(sx + 1) * CK],
                                         b[0:128, 0:CK])
                mpb.job([f], [("vector", e, True)])
        s.wait("vector", ld, wxc)
        s.op("vector", lambda h: h.tensor_mul(dxq[:], dtq[:], xcq[:]),
             waits=[(vv, vv.n)], incs=[(vv, 1)])
        vv.inc()
        s.wait("scalar", ld, wdt)
        s.wait("scalar", vv, vv.n)
        s.wait("vector", xsm, xsm.n)
        for dtl in range(16):
            sl = slice(dtl * CK, (dtl + 1) * CK)
            for sx in range(16):
                ssl = slice(sx * CK, (sx + 1) * CK)
                hc = Hc[:, dtl * 16 + sx:dtl * 16 + sx + 1]
                s.op("scalar", lambda h, sl=sl, dtl=dtl, sx=sx: h.activation(
                    aq[:], dtq[:, sl], AF.Exp,
                    scale=acol[:, dtl * 16 + sx:dtl * 16 + sx + 1]),
                    waits=[(vv, vv.n)], incs=[(sv, 1)])
                t_a = sv.inc()
                s.op("vector", lambda h, sl=sl, ssl=ssl: h.tensor_mul(
                    bq[:], dxq[:, sl], Bbc[:, ssl]),
                    waits=[(vv, vv.n)], incs=[(vv, 1)])
                t_b = vv.inc()
                s.op("vector", lambda h, hc=hc: h.tensor_tensor_scan(
                    hq[:], aq[:], bq[:], hc, ALU.mult, ALU.add),
                    waits=[(vv, t_b), (sv, t_a)], incs=[(vv, 1)])
                t_h = vv.inc()
                s.op("vector", lambda h, hc=hc: h.tensor_copy(
                    hq[:, CK - 1:CK], hq[:, CK - 1:CK]) if False else
                    h.tensor_copy(hc, hq[:, CK - 1:CK]),
                    waits=[(vv, t_h)], incs=[(vv, 1)])
                t_hc = vv.inc()
                if sx == 0:
                    s.op("vector", lambda h, sl=sl, ssl=ssl: h.tensor_mul(
                        yq[:, sl], hq[:], Cbc[:, ssl]),
                        waits=[(vv, t_hc)], incs=[(vv, 1)])
                    vv.inc()
                else:
                    s.op("vector", lambda h, ssl=ssl: h.tensor_mul(
                        bq[:], hq[:], Cbc[:, ssl]),
                        waits=[(vv, t_hc)], incs=[(vv, 1)])
                    tm = vv.inc()
                    s.op("vector", lambda h, sl=sl: h.tensor_add(
                        yq[:, sl], yq[:, sl], bq[:]), waits=[(vv, tm)],
                        incs=[(vv, 1)])
                    vv.inc()
            s.op("vector", lambda h, sl=sl, dtl=dtl: h.scalar_tensor_tensor(
                yq[:, sl], xcq[:, sl], Dc[:, dtl:dtl + 1], yq[:, sl],
                ALU.mult, ALU.add), waits=[(vv, vv.n)], incs=[(vv, 1)])
            td = vv.inc()
            s.wait("vector", ld, wz)
            s.op("vector", lambda h, sl=sl: h.tensor_mul(
                yzb[:, sl], yq[:, sl], zq[:, sl]), waits=[(vv, td)],
                incs=[(vv, 1)])
            vv.inc()
        s.wait("tensor", vv, vv.n)
        mp = MMPipe(s, pmm, tsm, None, xsm)
        bm = BufMgr(s)
        stj = [0]
        wpr = [0]
        for osl in range(4):
            if wpr[0]:
                s.wait("gpsimd", tsm, wpr[0])
            wld = s.dma("gpsimd", wsl[:].rearrange("p (c n) -> p c n", n=512),
                        i_wot[:, osl * 512:(osl + 1) * 512].rearrange(
                            "(c p) n -> p c n", p=128), ld)
            first = True
            for ts128 in range(2):
                mms = []
                for dtl in range(16):
                    def f(h, b, dtl=dtl, ts128=ts128):
                        return h.matmul(
                            b[:], yzb[:, dtl * CK + ts128 * 128:
                                      dtl * CK + ts128 * 128 + 128],
                            wsl[:, dtl * 512:dtl * 512 + 512],
                            start=(dtl == 0), stop=(dtl == 15))
                    mms.append(f)
                j = stj[0]
                stj[0] += 1
                buf = ev[j % 6]
                bm.guard("vector", buf, osm)
                def evf(h, b, buf=buf):
                    return h.tensor_copy(buf[:], b[:])
                def after(lx, buf=buf, osl=osl, ts128=ts128, c0=c0):
                    s.op("sync", lambda h: h.dma_start(
                        out=o_part[c0 + ts128 * 128:c0 + (ts128 + 1) * 128,
                                   osl * 512:osl * 512 + 512], in_=buf[:]),
                        waits=[(xsm, lx)], incs=[(osm, 16)])
                    bm.mark(buf, osm.inc(16))
                mp.job(mms, [("vector", evf, True)], after,
                       extra_tensor_waits=[(ld, wld)] if first else ())
                first = False
            wpr[0] = tsm.n
        s.wait("sync", vv, vv.n)
        s.wait("sync", sv, sv.n)
        s.wait("sync", tsm, tsm.n)
        if ck == NCK - 1:
            s.wait("sync", osm, osm.n)
        s.build()
    p.close()
    return p



def build_M5():
    p = Prog()
    nc = p.nc
    i_p0 = p.din("part0", [TLOC, D], F32)
    i_p1 = p.din("part1", [TLOC, D], F32)
    i_x2 = p.din("x2", [TLOC, D], F32)
    i_ident = p.din("ident", [128, 128], BF16)
    o_x3 = p.dout("x3", [TLOC, D], F32)
    o_ln3 = p.dout("ln3_tok", [TLOC, D], BF16)
    o_ln3fm = p.dout("ln3_fm", [D, TLOC], BF16)

    t0 = p.sb("t0", [128, D], F32)
    t1 = p.sb("t1", [128, D], F32)
    x3t = p.sb("x3t", [128, D], F32)
    ln3t = p.sb("ln3t", [128, D], BF16)
    lnscr = p.sb("lnscr", [128, D], F32)
    st8 = p.sb("st8", [128, 16], F32)
    ident = p.sb("identt", [128, 128], BF16)
    ogfm = p.sb("ogfm", [128, ND * 128], BF16)
    pstr = p.ps("pstr", [128, 1024], BF16)

    ld = p.sem("ld")
    sv = p.sem("sv")
    vv = p.sem("vv")
    tsm = p.sem("tsm")
    xsm = p.sem("xsm")
    osm = p.sem("osm")

    s = Sched(p)
    p.enable_consts(s, ld)
    wid = s.dma("sync", ident[:], i_ident, ld)
    s.wait("tensor", ld, wid)
    s.wait("scalar", ld, 16)
    tp_ = MMPipe(s, [pstr], tsm, None, xsm)
    for tt in range(NT):
        sl = slice(tt * 128, (tt + 1) * 128)
        w0 = s.dma("sync", t0[:], i_p0[sl, :], ld)
        w1 = s.dma("sync", t1[:], i_p1[sl, :], ld)
        wx = s.dma("sync", x3t[:], i_x2[sl, :], ld)
        s.wait("vector", ld, w1)
        s.op("vector", lambda h: h.tensor_add(t0[:], t0[:], t1[:]),
             incs=[(vv, 1)])
        ta = vv.inc()
        s.wait("scalar", vv, ta)
        emit_ln(s, p, t0[:], t1[:], st8[:, 0:8], sv, vv, D, lnscr[:])
        s.wait("vector", ld, wx)
        s.op("vector", lambda h: h.tensor_add(x3t[:], x3t[:], t1[:]),
             waits=[(vv, vv.n)], incs=[(vv, 1)])
        tx3 = vv.inc()
        s.wait("sync", vv, tx3)
        s.dma("sync", o_x3[sl, :], x3t[:], osm)
        s.wait("scalar", vv, tx3)
        emit_ln(s, p, x3t[:], ln3t[:], st8[:, 8:16], sv, vv, D, lnscr[:])
        tln = vv.n
        s.wait("sync", vv, tln)
        s.dma("sync", o_ln3[sl, :], ln3t[:], osm)
        s.wait("tensor", vv, tln)
        for dc in range(ND):
            def tfn(h, b, dc=dc):
                return h.transpose(b[:, 0:128], ln3t[:, dc * 128:(dc + 1) * 128],
                                   ident[:])
            def efn(h, b, dc=dc):
                return h.tensor_copy(ogfm[:, dc * 128:(dc + 1) * 128],
                                     b[:, 0:128])
            tp_.job([tfn], [("vector", efn, True)])
        s.wait("sync", xsm, xsm.n)
        s.dma("sync", o_ln3fm[:, sl].rearrange("(c p) t -> p c t", p=128),
              ogfm[:].rearrange("p (c t) -> p c t", t=128), osm)
        s.wait("sync", osm, osm.n)
        s.wait("vector", osm, osm.n)
    s.build()
    p.close()
    return p



def build_G1():
    p = Prog()
    nc = p.nc
    i_ln3fm = p.din("ln3_fm", [D, TLOC], BF16)
    i_wg = p.din("w_gate", [D, 64], F32)
    i_wsu = p.din("w_shared_up", [D, 1024], F32)
    i_wsd = p.din("w_shared_down", [512, D], F32)
    i_iota64 = p.din("iota64_bc", [128, 64], F32)
    o_sh = p.dout("sharedout", [TLOC, D], F32)
    o_m = p.dout("mmat", [TLOC, 64], F32)
    o_widx = p.dout("widx", [TLOC, 8], F32)
    o_wn = p.dout("wnorm", [TLOC, 8], F32)
    o_sc = p.dout("sc_dbg", [TLOC, 64], F32)

    lnfm = p.sb("lnfm", [128, ND * TLOC], BF16)
    wgb = p.sb("wgb", [128, ND * 64], BF16)
    wsub = p.sb("wsub", [128, ND * 1024], BF16)
    wsdb = p.sb("wsdb", [128, 4 * D], BF16)
    sufm = p.sb("sufm", [128, 8 * TLOC], F32)
    hsh = p.sb("hsh", [128, 4 * TLOC], BF16)
    sc = p.sb("sc", [128, 64], F32)
    scw = p.sb("scw", [128, 64], F32)
    m8t = p.sb("m8t", [128, 8], F32)
    macc = p.sb("macc", [128, 64], F32)
    widxt = p.sb("widxt", [128, 8], F32)
    wnt = p.sb("wnt", [128, 8], F32)
    io64 = p.sb("io64", [128, 64], F32)
    r1 = p.sb("r1", [128, 64], F32)
    r2 = p.sb("r2", [128, 1], F32)
    ev = [p.sb(f"ev{i}", [128, 512], F32) for i in range(6)]

    pmm = [p.ps(f"pmm{i}", [128, 512], F32) for i in range(6)]

    ld = p.sem("ld")
    sv = p.sem("sv")
    vv = p.sem("vv")
    tsm = p.sem("tsm")
    xsm = p.sem("xsm")
    osm = p.sem("osm")

    s = Sched(p)
    p.enable_consts(s, ld)
    s.dma("gpsimd", wgb[:].rearrange("p (c n) -> p c n", n=64),
          i_wg.rearrange("(c p) n -> p c n", p=128), ld)
    s.dma("gpsimd", wsub[:].rearrange("p (c n) -> p c n", n=1024),
          i_wsu.rearrange("(c p) n -> p c n", p=128), ld)
    s.dma("gpsimd", wsdb[:].rearrange("p (c n) -> p c n", n=D),
          i_wsd.rearrange("(c p) n -> p c n", p=128), ld)
    wio = s.dma("sync", io64[:], i_iota64, ld)
    wln = s.dma("sync", lnfm[:].rearrange("p (c n) -> p c n", n=TLOC),
                i_ln3fm.rearrange("(c p) n -> p c n", p=128), ld)
    s.wait("tensor", ld, ld.n)
    s.wait("vector", ld, ld.n)
    s.wait("scalar", ld, 16)

    mp = MMPipe(s, pmm, tsm, None, xsm)
    bm = BufMgr(s)
    for nch in range(8):
        for th in range(2):
            mms = []
            for dc in range(ND):
                def f(h, b, dc=dc, nch=nch, th=th):
                    return h.matmul(
                        b[:], wsub[:, dc * 1024 + nch * 128:
                                   dc * 1024 + nch * 128 + 128],
                        lnfm[:, dc * TLOC + th * 512:dc * TLOC + th * 512 + 512],
                        start=(dc == 0), stop=(dc == ND - 1))
                mms.append(f)
            def evf(h, b, nch=nch, th=th):
                return h.tensor_copy(
                    sufm[:, nch * TLOC + th * 512:nch * TLOC + th * 512 + 512],
                    b[:])
            mp.job(mms, [("vector", evf, True)])
    s.wait("scalar", xsm, xsm.n)
    for u in range(4):
        s.op("scalar", lambda h, u=u: h.activation(
            sufm[:, (4 + u) * TLOC:(5 + u) * TLOC],
            sufm[:, (4 + u) * TLOC:(5 + u) * TLOC], AF.Silu), incs=[(sv, 1)])
        tsx = sv.inc()
        s.op("vector", lambda h, u=u: h.tensor_mul(
            hsh[:, u * TLOC:(u + 1) * TLOC], sufm[:, (4 + u) * TLOC:(5 + u) * TLOC],
            sufm[:, u * TLOC:(u + 1) * TLOC]), waits=[(sv, tsx)],
            incs=[(vv, 1)])
        vv.inc()
    s.wait("tensor", vv, vv.n)
    stj = [0]
    for tt in range(NT):
        for osl in range(4):
            mms = []
            for hc in range(4):
                def f(h, b, hc=hc, tt=tt, osl=osl):
                    return h.matmul(
                        b[:], hsh[:, hc * TLOC + tt * 128:hc * TLOC + tt * 128 + 128],
                        wsdb[:, hc * D + osl * 512:hc * D + osl * 512 + 512],
                        start=(hc == 0), stop=(hc == 3))
                mms.append(f)
            j = stj[0]
            stj[0] += 1
            buf = ev[j % 6]
            bm.guard("vector", buf, osm)
            def evf(h, b, buf=buf):
                return h.tensor_copy(buf[:], b[:])
            def after(lx, buf=buf, tt=tt, osl=osl):
                s.op("sync", lambda h: h.dma_start(
                    out=o_sh[tt * 128:(tt + 1) * 128, osl * 512:osl * 512 + 512],
                    in_=buf[:]), waits=[(xsm, lx)], incs=[(osm, 16)])
                bm.mark(buf, osm.inc(16))
            mp.job(mms, [("vector", evf, True)], after)
    sc_guard = [0, 0]
    for tt in range(NT):
        mms = []
        for dc in range(ND):
            def f(h, b, dc=dc, tt=tt):
                return h.matmul(
                    b[0:128, 0:64], wgb[:, dc * 64:(dc + 1) * 64],
                    lnfm[:, dc * TLOC + tt * 128:dc * TLOC + tt * 128 + 128],
                    start=(dc == 0), stop=(dc == ND - 1))
            mms.append(f)
        mms = []
        for dc in range(ND):
            def f(h, b, dc=dc, tt=tt):
                return h.matmul(
                    b[0:128, 0:64],
                    lnfm[:, dc * TLOC + tt * 128:dc * TLOC + tt * 128 + 128],
                    wgb[:, dc * 64:(dc + 1) * 64],
                    start=(dc == 0), stop=(dc == ND - 1))
            mms.append(f)
        steps = [("scalar",
                  lambda h, b: h.activation(sc[:], b[0:128, 0:64], AF.Sigmoid),
                  True)]
        etw = []
        if sc_guard[0]:
            etw = [(vv, sc_guard[0]), (osm, sc_guard[1])]
        xv = mp.job(mms, steps, extra_tensor_waits=etw)
        s.wait("sync", xsm, xv)
        s.dma("sync", o_sc[tt * 128:(tt + 1) * 128, :], sc[:], osm)
        s.op("vector", lambda h: h.tensor_copy(scw[:], sc[:]),
             waits=[(xsm, xv)], incs=[(vv, 1)])
        tw = vv.inc()
        sc_guard = [tw, osm.n]
        s.op("vector", lambda h: h.memset(macc[:], 0.0), incs=[(vv, 1)])
        tm0 = vv.inc()
        s.op("vector", lambda h: h.max(m8t[:], scw[:]),
             waits=[(vv, tw)], incs=[(vv, 1)])
        tmx = vv.inc()
        for k in range(6):
            s.op("vector", lambda h, k=k: h.tensor_scalar(
                r1[:], scw[:], m8t[:, k:k + 1], None, ALU.is_ge),
                waits=[(vv, vv.n)], incs=[(vv, 1)])
            ta = vv.inc()
            s.op("vector", lambda h: h.scalar_tensor_tensor(
                r1[:], r1[:], -1000.0, io64[:], ALU.mult, ALU.add),
                waits=[(vv, ta)], incs=[(vv, 1)])
            tb = vv.inc()
            s.op("vector", lambda h, k=k: h.tensor_reduce(
                widxt[:, k:k + 1], r1[:], AX.X, ALU.min),
                waits=[(vv, tb)], incs=[(vv, 1)])
            tc = vv.inc()
            s.op("vector", lambda h, k=k: h.tensor_scalar(
                r1[:], io64[:], widxt[:, k:k + 1], None, ALU.subtract),
                waits=[(vv, tc)], incs=[(vv, 1)])
            td = vv.inc()
            s.op("vector", lambda h: h.tensor_scalar(
                r1[:], r1[:], 1000.0, None, ALU.is_equal),
                waits=[(vv, td)], incs=[(vv, 1)])
            te = vv.inc()
            s.op("vector", lambda h: h.tensor_add(macc[:], macc[:], r1[:]),
                 waits=[(vv, te), (vv, tm0)], incs=[(vv, 1)])
            tf_ = vv.inc()
            s.op("vector", lambda h: h.scalar_tensor_tensor(
                scw[:], r1[:], -2.0, scw[:], ALU.mult, ALU.add),
                waits=[(vv, tf_)], incs=[(vv, 1)])
            vv.inc()
        s.op("vector", lambda h: h.tensor_reduce(r2[:], m8t[:, 0:6], AX.X,
                                                 ALU.add),
             waits=[(vv, vv.n)], incs=[(vv, 1)])
        tg = vv.inc()
        s.op("vector", lambda h: h.reciprocal(r2[:], r2[:]),
             waits=[(vv, tg)], incs=[(vv, 1)])
        th_ = vv.inc()
        s.op("vector", lambda h: h.tensor_scalar(wnt[:], m8t[:], r2[:, 0:1],
                                                 None, ALU.mult),
             waits=[(vv, th_)], incs=[(vv, 1)])
        ti_ = vv.inc()
        s.wait("sync", vv, ti_)
        sl = slice(tt * 128, (tt + 1) * 128)
        s.dma("sync", o_m[sl, :], macc[:], osm)
        s.dma("sync", o_widx[sl, :], widxt[:], osm)
        s.dma("sync", o_wn[sl, :], wnt[:], osm)
        s.wait("vector", osm, osm.n)
    s.wait("sync", osm, osm.n)
    s.build()
    p.close()
    return p



def build_G2():
    p = Prog()
    nc = p.nc
    NTOK = 8192
    NTI = 64
    i_ln3 = p.din("ln3_all", [NTOK, D], BF16)
    i_m = p.din("m_my", [NTOK, 8], F32)
    i_wup = p.din("wup_sl", [8 * D, 512], F32)
    i_wdn = p.din("wdn_sl", [8 * 256, D], F32)
    i_tri = p.din("tri128x", [128, 128], F32)
    i_off8 = p.din("off8", [128, 8], F32)
    i_ident = p.din("ident", [128, 128], BF16)
    o_outr = p.dout("out_r", [8 * CAP, D], BF16)
    o_pos = p.dout("pos_full", [NTOK, 8], F32)

    x_disp = nc.dram_tensor("x_disp", [8 * CAP + 128, D], BF16)
    bas_d = nc.dram_tensor("bas_d", [NTI, 8], F32)

    m_all = p.sb("m_all", [128, NTI * 8], F32)
    trashc = p.sb("trashc", [128, 1], F32)
    pos_all = p.sb("pos_all", [128, NTI * 8], F32)
    didx = p.sb("didx", [128, NTI * 8], U32)
    ddf = p.sb("ddf", [128, NTI * 8], F32)
    tri = p.sb("tri", [128, 128], F32)
    off8 = p.sb("off8t", [128, 8], F32)
    onescol = p.sb("onescol", [128, 1], F32)
    onesrow = p.sb("onesrow", [1, 128], F32)
    psC_sb = p.sb("psC_sb", [8, NTI], F32)
    basrow = p.sb("basrow", [1, NTI * 8], F32)
    ident = p.sb("identt", [128, 128], BF16)
    identf = p.sb("identtf", [128, 128], F32)
    lnt = [p.sb(f"lnt{i}", [128, D], BF16) for i in range(2)]
    xdfm = p.sb("xdfm", [128, ND * CAP], BF16)
    wup = p.sb("wup", [128, ND * 512], BF16)
    wdn = p.sb("wdn", [128, 2 * D], BF16)
    hfm = p.sb("hfm", [128, 4 * CAP], F32)
    hr = p.sb("hr", [128, 2 * CAP], BF16)
    ev = [p.sb(f"ev{i}", [128, 512], BF16) for i in range(6)]

    pmm = [p.ps(f"pmm{i}", [128, 512], F32) for i in range(4)]
    pst = p.ps("pstG", [128, 1024], BF16)
    pstf = p.ps("pstGf", [128, 512], F32)

    ld = p.sem("ld")
    gsm = p.sem("gsm")
    sv = p.sem("sv")
    vv = p.sem("vv")
    tsm = p.sem("tsm")
    xsm = p.sem("xsm")
    osm = p.sem("osm")

    s = Sched(p)
    p.enable_consts(s, ld)
    s.dma("sync", tri[:], i_tri, ld)
    s.dma("sync", off8[:], i_off8, ld)
    s.dma("sync", ident[:], i_ident, ld)
    wm = s.dma("sync", m_all[:].rearrange("p (a n) -> p a n", n=8),
               i_m.rearrange("(a p) n -> p a n", p=128), ld)
    s.op("gpsimd", lambda h: h.iota(trashc[:], [[1, 1]], base=8 * CAP,
                                    channel_multiplier=1,
                                    allow_small_or_imprecise_dtypes=True),
         incs=[(vv, 1)])
    vv.inc()
    s.op("vector", lambda h: h.memset(onescol[:], 1.0))
    s.op("vector", lambda h: h.memset(onesrow[:], 1.0), incs=[(vv, 1)])
    tone = vv.inc()
    s.wait("tensor", ld, wm)
    s.wait("tensor", vv, tone)
    s.op("vector", lambda h: h.tensor_copy(identf[:], ident[:]),
         incs=[(vv, 1)])
    tidf = vv.inc()
    for ti in range(NTI):
        s.op("tensor", lambda h, ti=ti: h.matmul(
            pmm[3][0:8, ti:ti + 1], m_all[:, ti * 8:(ti + 1) * 8], onescol[:],
            start=True, stop=True),
            incs=[(tsm, 1)] if ti == NTI - 1 else ())
    t_cs = tsm.inc()
    s.op("vector", lambda h: h.tensor_copy(psC_sb[:], pmm[3][0:8, 0:NTI]),
         waits=[(tsm, t_cs)], incs=[(vv, 1)])
    t_e = vv.inc()
    s.op("vector", lambda h: h.memset(ddf[0:8, 0:1], 0.0),
         waits=[(vv, t_e)], incs=[(vv, 1)])
    t0_ = vv.inc()
    s.op("vector", lambda h: h.memset(ddf[0:8, NTI:NTI + NTI], 1.0),
         incs=[(vv, 1)])
    t1_ = vv.inc()
    s.op("vector", lambda h: h.tensor_tensor_scan(
        ddf[0:8, 1:NTI], ddf[0:8, NTI:NTI + NTI - 1], psC_sb[:, 0:NTI - 1],
        0.0, ALU.mult, ALU.add), waits=[(vv, t1_), (vv, t0_)], incs=[(vv, 1)])
    t2_ = vv.inc()
    s.wait("sync", vv, t2_)
    wb = s.dma("sync", bass.AP(bas_d.ap().tensor, 0, [[1, 8], [8, NTI]]),
               ddf[0:8, 0:NTI], ld, allow_slow_non_contiguous=True)
    s.wait("sync", ld, wb)
    wb2 = s.dma("sync", basrow[:].rearrange("p (a n) -> p a n", n=8),
                bas_d.ap().rearrange("(p a) n -> p a n", p=1), ld)
    s.wait("tensor", ld, wb2)
    for ti in range(NTI):
        bank = pmm[ti % 3]
        if ti >= 3:
            s.wait("tensor", vv, posev[ti - 3])
        s.op("tensor", lambda h, ti=ti, bank=bank: h.matmul(
            bank[0:128, 0:8], tri[:], m_all[:, ti * 8:(ti + 1) * 8],
            start=True, stop=False))
        s.op("tensor", lambda h, ti=ti, bank=bank: h.matmul(
            bank[0:128, 0:8], onesrow[:], basrow[:, ti * 8:(ti + 1) * 8],
            start=False, stop=True), incs=[(tsm, 1)])
        tmm = tsm.inc()
        s.op("vector", lambda h, ti=ti, bank=bank: h.tensor_copy(
            pos_all[:, ti * 8:(ti + 1) * 8], bank[0:128, 0:8]),
            waits=[(tsm, tmm)], incs=[(vv, 1)])
        if ti == 0:
            posev = {}
        posev[ti] = vv.inc()
    s.wait("sync", vv, posev[NTI - 1])
    wps = s.dma("sync", o_pos.rearrange("(a p) n -> p a n", p=128),
                pos_all[:].rearrange("p (a n) -> p a n", n=8), osm)
    s.wait("vector", osm, wps)
    s.wait("vector", vv, posev[NTI - 1])
    for ti in range(NTI):
        slt = slice(ti * 8, (ti + 1) * 8)
        s.op("vector", lambda h, slt=slt: h.tensor_scalar(
            ddf[:, slt], pos_all[:, slt], float(CAP), None, ALU.is_lt),
            incs=[(vv, 1)])
        ta = vv.inc()
        s.op("vector", lambda h, slt=slt: h.tensor_mul(
            ddf[:, slt], ddf[:, slt], m_all[:, slt]),
            waits=[(vv, ta)], incs=[(vv, 1)])
        tb = vv.inc()
        s.op("vector", lambda h, slt=slt: h.tensor_add(
            pos_all[:, slt], pos_all[:, slt], off8[:]),
            waits=[(vv, tb)], incs=[(vv, 1)])
        tc = vv.inc()
        s.op("vector", lambda h, slt=slt: h.tensor_scalar(
            pos_all[:, slt], pos_all[:, slt], trashc[:, 0:1], None,
            ALU.subtract), waits=[(vv, tc)], incs=[(vv, 1)])
        td = vv.inc()
        s.op("vector", lambda h, slt=slt: h.tensor_mul(
            ddf[:, slt], ddf[:, slt], pos_all[:, slt]),
            waits=[(vv, td)], incs=[(vv, 1)])
        te = vv.inc()
        s.op("vector", lambda h, slt=slt: h.tensor_scalar(
            ddf[:, slt], ddf[:, slt], trashc[:, 0:1], None, ALU.add),
            waits=[(vv, te)], incs=[(vv, 1)])
        tf_ = vv.inc()
        s.op("vector", lambda h, slt=slt: h.tensor_copy(didx[:, slt],
                                                        ddf[:, slt]),
            waits=[(vv, tf_)], incs=[(vv, 1)])
        vv.inc()
    s.build()

    lw = {}
    for tg in range(16):
        s = Sched(p)
        for tl in range(4):
            ti = tg * 4 + tl
            bi = ti % 2
            if tl >= 2:
                s.wait("sync", gsm, lw[ti - 2])
            wt = s.dma("sync", lnt[bi][:], i_ln3[ti * 128:(ti + 1) * 128, :],
                       ld)
            s.wait("gpsimd", ld, wt)
            for e in range(8):
                s.op("gpsimd", lambda h, ti=ti, e=e, bi=bi: h.indirect_dma_start(
                    out=x_disp.ap(), out_offset=IndirectOffsetOnAxis(
                        ap=didx[:, ti * 8 + e:ti * 8 + e + 1], axis=0),
                    in_=lnt[bi][:], in_offset=None),
                    incs=[(gsm, 16)])
                lw[ti] = gsm.inc(16)
                s.wait("gpsimd", gsm, lw[ti])
        if tg == 15:
            s.wait("gpsimd", gsm, gsm.n)
        s.build()

    for e in range(8):
        s = Sched(p)
        wu = s.dma("gpsimd", wup[:].rearrange("p (c n) -> p c n", n=512),
                   i_wup[e * D:(e + 1) * D, :].rearrange("(c p) n -> p c n",
                                                         p=128), ld)
        wd = s.dma("gpsimd", wdn[:].rearrange("p (c n) -> p c n", n=D),
                   i_wdn[e * 256:(e + 1) * 256, :].rearrange(
                       "(c p) n -> p c n", p=128), ld)
        tp_ = MMPipe(s, [pst], tsm, None, xsm)
        for st in range(8):
            bi = st % 2
            wt = s.dma("sync", lnt[bi][:],
                       x_disp.ap()[e * CAP + st * 128:e * CAP + (st + 1) * 128, :],
                       ld)
            s.wait("tensor", ld, wt)
            for dc in range(ND):
                def tfn(h, b, dc=dc, bi=bi):
                    return h.transpose(b[:, 0:128],
                                       lnt[bi][:, dc * 128:(dc + 1) * 128],
                                       ident[:])
                def efn(h, b, dc=dc, st=st):
                    return h.tensor_copy(
                        xdfm[:, dc * CAP + st * 128:dc * CAP + st * 128 + 128],
                        b[:, 0:128])
                tp_.job([tfn], [("vector", efn, True)])
            if st >= 1:
                pass
            s.wait("sync", tsm, tsm.n)
        mp = MMPipe(s, pmm, tsm, None, xsm)
        s.wait("tensor", ld, wu)
        s.wait("vector", xsm, xsm.n)
        for uc in range(4):
            for sh in range(2):
                mms = []
                for dc in range(ND):
                    def f(h, b, dc=dc, uc=uc, sh=sh):
                        return h.matmul(
                            b[:], wup[:, dc * 512 + uc * 128:dc * 512 + uc * 128 + 128],
                            xdfm[:, dc * CAP + sh * 512:dc * CAP + sh * 512 + 512],
                            start=(dc == 0), stop=(dc == ND - 1))
                    mms.append(f)
                def evf(h, b, uc=uc, sh=sh):
                    return h.tensor_copy(
                        hfm[:, uc * CAP + sh * 512:uc * CAP + sh * 512 + 512],
                        b[:])
                mp.job(mms, [("vector", evf, True)])
        s.wait("scalar", xsm, xsm.n)
        for u in range(2):
            s.op("scalar", lambda h, u=u: h.activation(
                hfm[:, u * CAP:(u + 1) * CAP], hfm[:, u * CAP:(u + 1) * CAP],
                AF.Silu), incs=[(sv, 1)])
            tsx = sv.inc()
            s.op("vector", lambda h, u=u: h.tensor_mul(
                hr[:, u * CAP:(u + 1) * CAP], hfm[:, u * CAP:(u + 1) * CAP],
                hfm[:, (2 + u) * CAP:(3 + u) * CAP]), waits=[(sv, tsx)],
                incs=[(vv, 1)])
            vv.inc()
        s.wait("tensor", vv, vv.n)
        s.wait("tensor", ld, wd)
        bm = BufMgr(s)
        stj = [0]
        for st in range(8):
            for osl in range(4):
                mms = []
                for hc in range(2):
                    def f(h, b, hc=hc, st=st, osl=osl):
                        return h.matmul(
                            b[:], hr[:, hc * CAP + st * 128:hc * CAP + st * 128 + 128],
                            wdn[:, hc * D + osl * 512:hc * D + osl * 512 + 512],
                            start=(hc == 0), stop=(hc == 1))
                    mms.append(f)
                j = stj[0]
                stj[0] += 1
                buf = ev[j % 6]
                bm.guard("vector", buf, osm)
                def evf(h, b, buf=buf):
                    return h.tensor_copy(buf[:], b[:])
                def after(lx, buf=buf, st=st, osl=osl, e=e):
                    s.op("sync", lambda h: h.dma_start(
                        out=o_outr[e * CAP + st * 128:e * CAP + (st + 1) * 128,
                                   osl * 512:osl * 512 + 512], in_=buf[:]),
                        waits=[(xsm, lx)], incs=[(osm, 16)])
                    bm.mark(buf, osm.inc(16))
                mp.job(mms, [("vector", evf, True)], after)
        if e == 7:
            s.wait("sync", osm, osm.n)
        s.build()
    p.close()
    return p




def build_G3():
    p = Prog()
    nc = p.nc
    i_outr = p.din("outr_all", [E64 * CAP, D], BF16)
    i_pos = p.din("pos_my", [TLOC, 64], F32)
    i_widx = p.din("widx", [TLOC, 8], F32)
    i_wn = p.din("wnorm", [TLOC, 8], F32)
    i_sh = p.din("sharedout", [TLOC, D], F32)
    i_x3 = p.din("x3", [TLOC, D], F32)
    i_io64 = p.din("iota64p", [128, 64], F32)
    o_x4 = p.dout("x4", [TLOC, D], F32)

    post = p.sb("post", [128, 64], F32)
    wit = p.sb("wit", [128, 8], F32)
    wnt = p.sb("wnt", [128, 8], F32)
    io64 = p.sb("io64", [128, 64], F32)
    r1 = p.sb("r1", [128, 64], F32)
    dstc = p.sb("dstc", [128, 8], F32)
    pvc = p.sb("pvc", [128, 8], F32)
    vvc = p.sb("vvc", [128, 8], F32)
    wtl = p.sb("wtl", [128, 8], F32)
    dstu = p.sb("dstu", [128, 8], U32)
    gb = [p.sb(f"gb{i}", [128, D], BF16) for i in range(2)]
    yac = p.sb("yac", [128, D], F32)
    sht = p.sb("sht", [128, D], F32)
    x4t = p.sb("x4t", [128, D], F32)
    lnt_ = p.sb("lnt_", [128, D], F32)
    st8 = p.sb("st8", [128, 8], F32)
    lnscr = p.sb("lnscr", [128, D], F32)

    ld = p.sem("ld")
    gsm = p.sem("gsm")
    sv = p.sem("sv")
    vv = p.sem("vv")
    osm = p.sem("osm")

    nblk = 0
    for tt in range(NT):
        s = Sched(p)
        if tt == 0:
            p.enable_consts(s, ld)
            wio = s.dma("sync", io64[:], i_io64, ld)
            s.op("vector", lambda h: h.memset(gb[0][:], 0.0))
            s.op("vector", lambda h: h.memset(gb[1][:], 0.0))
            s.wait("scalar", ld, 16)
        sl = slice(tt * 128, (tt + 1) * 128)
        wp = s.dma("sync", post[:], i_pos[sl, :], ld)
        wi = s.dma("sync", wit[:], i_widx[sl, :], ld)
        wn = s.dma("sync", wnt[:], i_wn[sl, :], ld)
        wsh = s.dma("sync", sht[:], i_sh[sl, :], ld)
        wx3 = s.dma("sync", x4t[:], i_x3[sl, :], ld)
        s.wait("vector", ld, ld.n)
        for k in range(TOPK6):
            s.op("vector", lambda h, k=k: h.tensor_scalar(
                r1[:], io64[:], wit[:, k:k + 1], None, ALU.is_equal),
                waits=[(vv, vv.n)], incs=[(vv, 1)])
            ta = vv.inc()
            s.op("vector", lambda h: h.scalar_tensor_tensor(
                r1[:], io64[:], 1024.0, r1[:], ALU.mult, ALU.mult),
                waits=[(vv, ta)], incs=[(vv, 1)])
            tb = vv.inc()
            s.op("vector", lambda h, k=k: h.tensor_scalar(
                lnscr[:, 0:64], io64[:], wit[:, k:k + 1], None, ALU.is_equal),
                waits=[(vv, tb)], incs=[(vv, 1)])
            tc = vv.inc()
            s.op("vector", lambda h: h.tensor_mul(
                lnscr[:, 0:64], lnscr[:, 0:64], post[:]),
                waits=[(vv, tc)], incs=[(vv, 1)])
            td = vv.inc()
            s.op("vector", lambda h: h.tensor_add(r1[:], r1[:],
                                                  lnscr[:, 0:64]),
                 waits=[(vv, td)], incs=[(vv, 1)])
            te = vv.inc()
            s.op("vector", lambda h, k=k: h.tensor_reduce(
                dstc[:, k:k + 1], r1[:], AX.X, ALU.add),
                waits=[(vv, te)], incs=[(vv, 1)])
            tf_ = vv.inc()
            s.op("vector", lambda h, k=k: h.scalar_tensor_tensor(
                pvc[:, k:k + 1], wit[:, k:k + 1], -1024.0, dstc[:, k:k + 1],
                ALU.mult, ALU.add), waits=[(vv, tf_)], incs=[(vv, 1)])
            tg = vv.inc()
            s.op("vector", lambda h, k=k: h.tensor_scalar(
                vvc[:, k:k + 1], pvc[:, k:k + 1], float(CAP), None, ALU.is_lt),
                waits=[(vv, tg)], incs=[(vv, 1)])
            th_ = vv.inc()
            s.op("vector", lambda h, k=k: h.tensor_mul(
                wtl[:, k:k + 1], wnt[:, k:k + 1], vvc[:, k:k + 1]),
                waits=[(vv, th_)], incs=[(vv, 1)])
            ti_ = vv.inc()
            s.op("vector", lambda h, k=k: h.tensor_mul(
                dstc[:, k:k + 1], dstc[:, k:k + 1], vvc[:, k:k + 1]),
                waits=[(vv, ti_)], incs=[(vv, 1)])
            tj = vv.inc()
            s.op("vector", lambda h, k=k: h.tensor_copy(
                dstu[:, k:k + 1], dstc[:, k:k + 1]),
                waits=[(vv, tj)], incs=[(vv, 1)])
            vv.inc()
        for k in range(TOPK6):
            bi = k % 2
            s.wait("gpsimd", vv, vv.n)
            if k >= 2:
                s.wait("gpsimd", vv, gdone[k - 2])
            s.op("gpsimd", lambda h, k=k, bi=bi: h.indirect_dma_start(
                out=gb[bi][:], out_offset=None, in_=i_outr,
                in_offset=IndirectOffsetOnAxis(ap=dstu[:, k:k + 1], axis=0)),
                incs=[(gsm, 16)])
            wg = gsm.inc(16)
            s.wait("gpsimd", gsm, wg)
            s.wait("vector", gsm, wg)
            if k == 0:
                s.op("vector", lambda h, bi=bi: h.tensor_scalar(
                    yac[:], gb[bi][:], wtl[:, 0:1], None, ALU.mult),
                    incs=[(vv, 1)])
            else:
                s.op("vector", lambda h, k=k, bi=bi: h.scalar_tensor_tensor(
                    yac[:], gb[bi][:], wtl[:, k:k + 1], yac[:],
                    ALU.mult, ALU.add), incs=[(vv, 1)])
            if k == 0:
                gdone = {}
            gdone[k] = vv.inc()
        s.op("vector", lambda h: h.tensor_add(yac[:], yac[:], sht[:]),
             waits=[(vv, vv.n)], incs=[(vv, 1)])
        ty = vv.inc()
        s.wait("scalar", vv, ty)
        emit_ln(s, p, yac[:], lnt_[:], st8[:, 0:8], sv, vv, D, lnscr[:])
        s.op("vector", lambda h: h.tensor_add(x4t[:], x4t[:], lnt_[:]),
             waits=[(vv, vv.n)], incs=[(vv, 1)])
        tx4 = vv.inc()
        s.wait("sync", vv, tx4)
        s.dma("sync", o_x4[sl, :], x4t[:], osm)
        s.wait("sync", osm, osm.n)
        s.build()
    p.close()
    return p



_PROGS = {}


def _prog(name, builder):
    if name not in _PROGS:
        _PROGS[name] = builder()
    return _PROGS[name]


def _run(p, in_maps):
    res = run_bass_kernel_spmd(p.nc, in_maps, core_ids=list(range(NCORES)))
    return res.results


def kernel(**inp):
    x = _f32(inp["L0_attn_x"])
    ident = np.eye(128).astype(BFNP)
    tri64 = np.triu(np.ones((CHK, CHK), np.float32))
    mask64 = np.triu(np.ones((CHK, CHK), np.float32)).astype(BFNP)
    rst2048 = np.ones((64, 2048), np.float32)
    rst2048[:, ::64] = 0.0
    rst16 = np.ones((64, 15), np.float32)
    iota1000_8 = (np.arange(8, dtype=np.float32) + 1000.0)[None, :]
    io64bc = np.tile(np.arange(64, dtype=np.float32)[None, :] + 1000.0,
                     (128, 1))
    io64p = np.tile(np.arange(64, dtype=np.float32)[None, :], (128, 1))
    tri128x = np.triu(np.ones((128, 128), np.float32), 1)
    off8 = np.tile((np.arange(8, dtype=np.float32) * CAP)[None, :], (128, 1))
    padsel = np.array([[0, 0, 0, 0, 0, 0, 1, 1]], np.float32)

    pA1 = _prog("A1", build_A1)
    bgk2 = _f32(inp["L0_attn_bgk2"])
    mapsA1 = []
    for c in range(NCORES):
        b, hh = c // 2, c % 2
        mapsA1.append({
            "x": _f32(x[b, hh * TLOC:(hh + 1) * TLOC]),
            "wq": _f32(inp["L0_attn_Wq"]), "wk": _f32(inp["L0_attn_Wk"]),
            "wv": _f32(inp["L0_attn_Wv"]), "wg": _f32(inp["L0_attn_Wg"]),
            "wgk1": _f32(inp["L0_attn_Wgk1"]),
            "wgk2": _f32(inp["L0_attn_Wgk2"]),
            "bgk_bc": np.tile(bgk2[None, :], (128, 1)),
            "bgk_col": np.ascontiguousarray(bgk2.reshape(8, 128).T),
            "ident": ident, "cconst": pA1.const_host(),
        })
    rA1 = _run(pA1, mapsA1)

    pA2 = _prog("A2", build_A2)
    normw_bc = np.tile(_f32(inp["L0_attn_norm_w"])[None, :], (128, 1))
    mapsA2 = []
    for c in range(NCORES):
        b, hh = c // 2, c % 2
        psel = np.zeros((64, 8), np.float32)
        if hh == 1:
            psel[:, c - 1] = 1.0
        r = rA1[c]
        mapsA2.append({
            "q_fm": r["q_fm"], "k_fm": r["k_fm"], "k_tok": r["k_tok"],
            "v_tok": r["v_tok"], "g_tok": r["g_tok"], "gk_fm": r["gk_fm"],
            "gk_tok": r["gk_tok"], "wo": _f32(inp["L0_attn_Wo"]),
            "x_res": _f32(x[b, hh * TLOC:(hh + 1) * TLOC]),
            "normw_bc": normw_bc, "pairsel": psel, "tri64": tri64,
            "mask64": mask64, "rst2048": rst2048, "rst16": rst16,
            "ident": ident, "cconst": pA2.const_host(),
        })
    rA2 = _run(pA2, mapsA2)

    pF1 = _prog("F1", build_F1)
    gwt = np.zeros((D, 8), np.float32)
    gwt[:, :6] = np.ascontiguousarray(inp["L0_ffn_gate_W"].T)
    w1r = _f32(np.asarray(inp["L0_ffn_W1"]).reshape(6 * D, 1408))
    w2r = _f32(np.asarray(inp["L0_ffn_W2"]).reshape(6 * 1408, D))
    b1t = np.zeros((1408, 8), np.float32)
    b1t[:, :6] = np.ascontiguousarray(inp["L0_ffn_b1"].T)
    b2r = _f32(inp["L0_ffn_b2"])
    ln1c = {b: np.concatenate([np.asarray(rA2[2 * b]["ln1_fm"]),
                               np.asarray(rA2[2 * b + 1]["ln1_fm"])], axis=1)
            for b in range(4)}
    mapsF1 = []
    for c in range(NCORES):
        b, k = c // 2, c % 2
        sel2 = np.zeros((1, 2), np.float32)
        sel2[0, k] = 1.0
        mapsF1.append({
            "ln1_fm": ln1c[b], "gate_WT": gwt, "W1r": w1r, "W2r": w2r,
            "b1T": b1t, "b2rows": b2r, "sel2": sel2, "sel8": padsel,
            "iota1e9": iota1000_8, "cconst": pF1.const_host(),
        })
    rF1 = _run(pF1, mapsF1)

    pF2 = _prog("F2M1", build_F2M1)
    wint = np.ascontiguousarray(_f32(inp["L1_attn_in_proj_w"]).T)
    mapsF2 = []
    for c in range(NCORES):
        b, hh = c // 2, c % 2
        sl = slice(hh * TLOC, (hh + 1) * TLOC)
        mapsF2.append({
            "slot0": np.asarray(rF1[2 * b]["out_slot"])[sl],
            "slot1": np.asarray(rF1[2 * b + 1]["out_slot"])[sl],
            "x1": np.asarray(rA2[c]["x1"]),
            "W_inT": wint, "ident": ident, "cconst": pF2.const_host(),
        })
    rF2 = _run(pF2, mapsF2)

    pM2 = _prog("M2", build_M2)
    cw = _f32(inp["L1_attn_conv_w"]).reshape(32, 128, 4).transpose(1, 0, 2)\
        .reshape(128, 128)
    cb = np.ascontiguousarray(_f32(inp["L1_attn_conv_b"]).reshape(32, 128).T)
    xpt = np.ascontiguousarray(_f32(inp["L1_attn_x_proj_w"]).T)
    dpt = np.ascontiguousarray(_f32(inp["L1_attn_dt_proj_w"]).T)
    dpb2 = np.ascontiguousarray(
        (2.0 * _f32(inp["L1_attn_dt_proj_b"])).reshape(32, 128).T)
    mapsM2 = []
    for c in range(NCORES):
        b, hh = c // 2, c % 2
        xz = np.asarray(rF2[c]["xz_fm"])
        xh = xz[0:DIN]
        if hh == 0:
            halo = np.zeros((DIN, 3), BFNP)
        else:
            halo = np.asarray(rF2[c - 1]["xz_fm"])[0:DIN, TLOC - 3:TLOC]
        mapsM2.append({
            "xh_fm": np.concatenate([halo, xh], axis=1),
            "conv_w": cw, "conv_b": cb, "x_projT": xpt, "dt_projT": dpt,
            "dpb2": dpb2, "cconst": pM2.const_host(),
        })
    rM2 = _run(pM2, mapsM2)

    pM3 = _prog("M3", build_M3)
    al = _f32(inp["L1_attn_A_log"])
    Dv = _f32(inp["L1_attn_D"])
    wout = np.ascontiguousarray(_f32(inp["L1_attn_out_proj_w"]).T)
    mapsM3 = []
    for c in range(NCORES):
        b, dg = c // 2, c % 2
        rsl = slice(2048 * dg, 2048 * (dg + 1))
        dt_c = np.concatenate([np.asarray(rM2[2 * b]["dt_fm"])[rsl],
                               np.asarray(rM2[2 * b + 1]["dt_fm"])[rsl]], 1)
        xc_c = np.concatenate([np.asarray(rM2[2 * b]["xc_fm"])[rsl],
                               np.asarray(rM2[2 * b + 1]["xc_fm"])[rsl]], 1)
        z_c = np.concatenate(
            [np.asarray(rF2[2 * b]["xz_fm"])[DIN + 2048 * dg:DIN + 2048 * (dg + 1)],
             np.asarray(rF2[2 * b + 1]["xz_fm"])[DIN + 2048 * dg:DIN + 2048 * (dg + 1)]],
            1)
        B_c = np.concatenate([np.asarray(rM2[2 * b]["bc_rows"])[0:16],
                              np.asarray(rM2[2 * b + 1]["bc_rows"])[0:16]], 1)
        C_c = np.concatenate([np.asarray(rM2[2 * b]["bc_rows"])[16:32],
                              np.asarray(rM2[2 * b + 1]["bc_rows"])[16:32]], 1)
        al_sl = al[rsl].reshape(16, 128, 16).transpose(1, 0, 2).reshape(128, 256)
        D_sl = np.ascontiguousarray(Dv[rsl].reshape(16, 128).T)
        mapsM3.append({
            "dt": dt_c, "xc": xc_c, "z_fm": z_c, "Brows": B_c, "Crows": C_c,
            "A_log": np.ascontiguousarray(al_sl), "Dcol": D_sl,
            "WoutT_sl": np.ascontiguousarray(wout[rsl]),
            "cconst": pM3.const_host(),
        })
    rM3 = _run(pM3, mapsM3)

    pM5 = _prog("M5", build_M5)
    mapsM5 = []
    for c in range(NCORES):
        b, hh = c // 2, c % 2
        sl = slice(hh * TLOC, (hh + 1) * TLOC)
        mapsM5.append({
            "part0": np.asarray(rM3[2 * b]["part"])[sl],
            "part1": np.asarray(rM3[2 * b + 1]["part"])[sl],
            "x2": np.asarray(rF2[c]["x2"]),
            "ident": ident, "cconst": pM5.const_host(),
        })
    rM5 = _run(pM5, mapsM5)

    pG1 = _prog("G1", build_G1)
    mapsG1 = []
    for c in range(NCORES):
        mapsG1.append({
            "ln3_fm": np.asarray(rM5[c]["ln3_fm"]),
            "w_gate": _f32(inp["L1_ffn_w_gate"]),
            "w_shared_up": _f32(inp["L1_ffn_w_shared_up"]),
            "w_shared_down": _f32(inp["L1_ffn_w_shared_down"]),
            "iota64_bc": io64bc, "cconst": pG1.const_host(),
        })
    rG1 = _run(pG1, mapsG1)

    pG2 = _prog("G2", build_G2)
    ln3_all = np.concatenate([np.asarray(rM5[c]["ln3_tok"])
                              for c in range(NCORES)], axis=0)
    m_full = np.concatenate([np.asarray(rG1[c]["mmat"])
                             for c in range(NCORES)], axis=0)
    wup = _f32(np.asarray(inp["L1_ffn_w_up"]))
    wdn = _f32(np.asarray(inp["L1_ffn_w_down"]))
    mapsG2 = []
    for c in range(NCORES):
        mapsG2.append({
            "ln3_all": ln3_all,
            "m_my": np.ascontiguousarray(m_full[:, 8 * c:8 * c + 8]),
            "wup_sl": np.ascontiguousarray(
                wup[8 * c:8 * c + 8].reshape(8 * D, 512)),
            "wdn_sl": np.ascontiguousarray(
                wdn[8 * c:8 * c + 8].reshape(8 * 256, D)),
            "tri128x": tri128x, "off8": off8, "ident": ident,
            "cconst": pG2.const_host(),
        })
    rG2 = _run(pG2, mapsG2)

    pG3 = _prog("G3", build_G3)
    outr_all = np.concatenate([np.asarray(rG2[c]["out_r"])
                               for c in range(NCORES)], axis=0)
    pos_full = np.concatenate([np.asarray(rG2[c]["pos_full"])
                               for c in range(NCORES)], axis=1)
    mapsG3 = []
    for c in range(NCORES):
        sl = slice(c * TLOC, (c + 1) * TLOC)
        mapsG3.append({
            "outr_all": outr_all,
            "pos_my": np.ascontiguousarray(pos_full[sl]),
            "widx": np.asarray(rG1[c]["widx"]),
            "wnorm": np.asarray(rG1[c]["wnorm"]),
            "sharedout": np.asarray(rG1[c]["sharedout"]),
            "x3": np.asarray(rM5[c]["x3"]),
            "iota64p": io64p, "cconst": pG3.const_host(),
        })
    rG3 = _run(pG3, mapsG3)

    out = np.empty((4, T, D), np.float32)
    for c in range(NCORES):
        b, hh = c // 2, c % 2
        out[b, hh * TLOC:(hh + 1) * TLOC] = np.asarray(rG3[c]["x4"])
    return out


# version 05 (final)
# speedup vs baseline: 1.3658x; 1.1781x over previous
from contextlib import ExitStack

import numpy as np
import ml_dtypes
import concourse.bass as bass
import concourse.mybir as mybir
from concourse.bass import IndirectOffsetOnAxis
from concourse.bass_utils import run_bass_kernel_spmd

F32 = mybir.dt.float32
BF16 = mybir.dt.bfloat16
U32 = mybir.dt.uint32
AF = mybir.ActivationFunctionType
ALU = mybir.AluOpType
AX = mybir.AxisListType

BFNP = ml_dtypes.bfloat16

NCORES = 8
B, T, D = 4, 2048, 2048
TLOC = 1024
NT = TLOC // 128
ND = D // 128
EPS = 1e-5

H, DK, DV = 16, 64, 128
KD, VD = 1024, 2048
CHK = 64
NCHK = TLOC // CHK

DIN, DST, DCONV, DTR = 4096, 16, 4, 128
E64, CAP, TOPK6 = 64, 1024, 6

ENGS = ("sync", "gpsimd", "scalar", "vector", "tensor")


def _bf(x):
    return np.ascontiguousarray(np.asarray(x)).astype(BFNP)


def _f32(x):
    return np.ascontiguousarray(np.asarray(x), dtype=np.float32)


class SemC:

    def __init__(self, handle):
        self.h = handle
        self.n = 0

    def inc(self, k=1):
        self.n += k
        return self.n


class Prog:
    def __init__(self):
        self.nc = bass.Bass("TRN2", target_bir_lowering=False)
        self.stk = ExitStack()

    def sb(self, name, shape, dt):
        return self.stk.enter_context(self.nc.sbuf_tensor(name, shape, dt))

    def ps(self, name, shape, dt=F32):
        return self.stk.enter_context(self.nc.psum_tensor(name, shape, dt))

    def sem(self, name):
        return SemC(self.stk.enter_context(self.nc.semaphore(name)))

    def din(self, name, shape, dt):
        return self.nc.dram_tensor(name, shape, dt, kind="ExternalInput").ap()

    def dout(self, name, shape, dt):
        return self.nc.dram_tensor(name, shape, dt, kind="ExternalOutput").ap()

    def enable_consts(self, s, sem):
        self.cin = self.din("cconst", [128, 16], F32)
        self.ct = self.sb("cconst_t", [128, 16], F32)
        self.cmap = {}
        s.dma("sync", self.ct[:], self.cin, sem)

    def cc(self, val):
        if val not in self.cmap:
            assert len(self.cmap) < 16
            self.cmap[val] = len(self.cmap)
        return self.ct[:, self.cmap[val]:self.cmap[val] + 1]

    def const_host(self):
        v = np.zeros(16, np.float32)
        for k, i in self.cmap.items():
            v[i] = k
        return np.tile(v[None, :], (128, 1))

    def close(self):
        self.stk.close()


class Sched:

    def __init__(self, p):
        self.p = p
        self.ops = {e: [] for e in ENGS}

    def op(self, eng, fn, waits=(), incs=()):
        self.ops[eng].append((tuple(waits), fn, tuple(incs)))

    def wait(self, eng, sem, val):
        self.ops[eng].append((((sem, val),), None, ()))

    def dma(self, eng, out, in_, sem, **kw):
        tot = sem.inc(16)
        self.op(eng, lambda h, o=out, i=in_, kw=kw: h.dma_start(out=o, in_=i, **kw),
                incs=[(sem, 16)])
        self.wait(eng, sem, tot)
        return tot

    def build(self):
        nc = self.p.nc
        ops = self.ops
        with nc.Block() as blk:
            def mk(eng_ops):
                def body(h):
                    for waits, fn, incs in eng_ops:
                        for sem, val in waits:
                            h.wait_ge(sem.h, val)
                        if fn is None:
                            continue
                        ins = fn(h)
                        for sem, k in incs:
                            ins = ins.then_inc(sem.h, k)
                return body
            for e in ENGS:
                if ops[e]:
                    getattr(blk, e)(mk(ops[e]))


def emit_ln(s, p, xt, ot, stc, sv, vv, n_free, scr_full, ot2=None):
    inv = 1.0 / n_free
    s.op("scalar", lambda h: h.activation(scr_full, xt, AF.Copy,
                                          accum_out=stc[:, 0:1]),
         incs=[(sv, 1)])
    t00 = sv.inc()
    s.op("scalar", lambda h: h.activation(scr_full, xt, AF.Square,
                                          accum_out=stc[:, 1:2]),
         waits=[(sv, t00)], incs=[(sv, 1)])
    t0 = sv.inc()
    s.op("vector", lambda h: h.tensor_scalar(stc[:, 2:3], stc[:, 0:1], inv,
                                             None, ALU.mult),
         waits=[(sv, t0)], incs=[(vv, 1)])
    t1 = vv.inc()
    s.op("vector", lambda h: h.tensor_scalar(stc[:, 5:6], stc[:, 1:2], inv,
                                             None, ALU.mult),
         incs=[(vv, 1)])
    t2 = vv.inc()
    s.op("vector", lambda h: h.scalar_tensor_tensor(stc[:, 6:7], stc[:, 2:3],
                                                    -1.0, stc[:, 2:3],
                                                    ALU.mult, ALU.mult),
         waits=[(vv, t1)], incs=[(vv, 1)])
    t3 = vv.inc()
    s.op("vector", lambda h: h.tensor_add(stc[:, 4:5], stc[:, 5:6], stc[:, 6:7]),
         waits=[(vv, t3)], incs=[(vv, 1)])
    t4 = vv.inc()
    s.op("scalar", lambda h: h.activation(stc[:, 5:6], stc[:, 4:5], AF.Ln,
                                          bias=p.cc(EPS)),
         waits=[(vv, t4)], incs=[(sv, 1)])
    t5 = sv.inc()
    s.op("scalar", lambda h: h.activation(stc[:, 3:4], stc[:, 5:6], AF.Exp,
                                          scale=p.cc(-0.5)),
         waits=[(sv, t5)], incs=[(sv, 1)])
    t6 = sv.inc()
    s.op("vector", lambda h: h.tensor_scalar(ot, xt, stc[:, 2:3], stc[:, 3:4],
                                             ALU.subtract, ALU.mult),
         waits=[(sv, t6)], incs=[(vv, 1)])
    vv.inc()
    if ot2 is not None:
        s.op("vector", lambda h: h.tensor_scalar(ot2, xt, stc[:, 2:3],
                                                 stc[:, 3:4],
                                                 ALU.subtract, ALU.mult),
             incs=[(vv, 1)])
        vv.inc()


class MMPipe:

    def __init__(self, s, banks, tsm, esm, xsm):
        self.s = s
        self.banks = banks
        self.nb = len(banks)
        self.tsm = tsm
        self.xsm = xsm
        self.bank_free = [0] * self.nb
        self.j = 0

    def job(self, mms, steps, after=None, extra_tensor_waits=()):
        s, j = self.s, self.j
        bi = j % self.nb
        bank = self.banks[bi]
        waits = list(extra_tensor_waits)
        if self.bank_free[bi]:
            waits.append((self.xsm, self.bank_free[bi]))
        for i, fn in enumerate(mms):
            inc = [(self.tsm, 1)] if i == len(mms) - 1 else ()
            s.op("tensor", (lambda h, f=fn, b=bank: f(h, b)),
                 waits=waits if i == 0 else (), incs=inc)
        tj = self.tsm.inc()
        last_rp = max(i for i, st_ in enumerate(steps) if st_[2])
        gate = (self.tsm, tj)
        xv = None
        for i, (eng, fn, rp) in enumerate(steps):
            s.op(eng, (lambda h, f=fn, b=bank: f(h, b)),
                 waits=[gate], incs=[(self.xsm, 1)])
            xv = self.xsm.inc()
            gate = (self.xsm, xv)
            if i == last_rp:
                self.bank_free[bi] = xv
        self.j += 1
        if after is not None:
            after(xv)
        return xv


class BufMgr:

    def __init__(self, s):
        self.s = s
        self.last = {}

    def guard(self, eng, buf, sem):
        c = self.last.get(id(buf))
        if c:
            self.s.wait(eng, sem, c)

    def mark(self, buf, cnt):
        self.last[id(buf)] = cnt



def build_A1():
    p = Prog()
    nc = p.nc
    x_in = p.din("x", [TLOC, D], F32)
    wq = p.din("wq", [D, KD], F32)
    wk = p.din("wk", [D, KD], F32)
    wv = p.din("wv", [D, VD], F32)
    wg = p.din("wg", [D, VD], F32)
    wgk1 = p.din("wgk1", [D, 16], F32)
    wgk2 = p.din("wgk2", [16, KD], F32)
    bgk_bc = p.din("bgk_bc", [128, KD], F32)
    bgk_col = p.din("bgk_col", [128, 8], F32)
    ident_in = p.din("ident", [128, 128], BF16)

    o_qfm = p.dout("q_fm", [KD, TLOC], BF16)
    o_kfm = p.dout("k_fm", [KD, TLOC], BF16)
    o_ktok = p.dout("k_tok", [TLOC, KD], BF16)
    o_v = p.dout("v_tok", [TLOC, VD], BF16)
    o_g = p.dout("g_tok", [TLOC, VD], BF16)
    o_gkfm = p.dout("gk_fm", [KD, TLOC], F32)
    o_gktok = p.dout("gk_tok", [TLOC, KD], F32)

    xt = [p.sb(f"xt{i}", [128, D], F32) for i in range(NT)]
    lnt = [p.sb(f"lnt{i}", [128, D], BF16) for i in range(NT)]
    lnfm = [p.sb(f"lnfm{i}", [128, TLOC], BF16) for i in range(ND)]
    st = p.sb("st", [128, 8 * NT], F32)
    lnscr = p.sb("lnscr", [128, D], F32)
    ident = p.sb("identt", [128, 128], BF16)
    wsl = [p.sb(f"wsl{i}", [128, ND * 512], BF16) for i in range(2)]
    wgk1b = p.sb("wgk1b", [128, ND * 16], BF16)
    wgk2b = p.sb("wgk2b", [16, KD], BF16)
    lowfm = p.sb("lowfm", [16, TLOC], BF16)
    bgkbc = p.sb("bgkbc", [128, KD], F32)
    bgkcol = p.sb("bgkcol", [128, 8], F32)
    ev = [p.sb(f"ev{i}", [128, 512], BF16) for i in range(8)]
    scrA = [p.sb(f"scrA{i}", [128, 512], F32) for i in range(4)]
    scrC = [p.sb(f"scrC{i}", [128, 512], F32) for i in range(4)]

    pst = [p.ps(f"pst{i}", [128, 1024], BF16) for i in range(2)]
    pmm = [p.ps(f"pmm{i}", [128, 512], F32) for i in range(6)]

    ld = p.sem("ld")
    wsm = p.sem("wsm")
    sv = p.sem("sv")
    vv = p.sem("vv")
    tsm = p.sem("tsm")
    esm = p.sem("esm")
    osm = p.sem("osm")
    xsm = p.sem("xsm")

    s = Sched(p)
    p.enable_consts(s, ld)
    s.wait("scalar", ld, 16)
    xdone = []
    for i in range(NT):
        xdone.append(s.dma("sync", xt[i][:], x_in[i * 128:(i + 1) * 128, :], ld))
    s.dma("sync", bgkbc[:], bgk_bc, ld)
    s.dma("sync", bgkcol[:], bgk_col, ld)
    iddone = s.dma("sync", ident[:], ident_in, ld)

    s.dma("gpsimd", wgk1b[:].rearrange("p (c n) -> p c n", n=16), wgk1.rearrange("(c p) n -> p c n", p=128), wsm)
    s.dma("gpsimd", wgk2b[:], wgk2, wsm)
    s.dma("gpsimd", wsl[0][:].rearrange("p (c n) -> p c n", n=512), wq[:, 0:512].rearrange("(c p) n -> p c n", p=128), wsm)
    s.dma("gpsimd", wsl[1][:].rearrange("p (c n) -> p c n", n=512), wq[:, 512:1024].rearrange("(c p) n -> p c n", p=128), wsm)

    ln_done = []
    for i in range(NT):
        s.wait("scalar", ld, xdone[i])
        emit_ln(s, p, xt[i][:], lnt[i][:], st[:, 8 * i:8 * i + 8], sv, vv, D,
                lnscr[:])
        ln_done.append(vv.n)

    s.wait("tensor", ld, iddone)
    tp = MMPipe(s, pst, tsm, esm, xsm)
    for i in range(NT):
        s.wait("tensor", vv, ln_done[i])
        for dc in range(ND):
            def mmfn(h, b, i=i, dc=dc):
                return h.transpose(b[:, 0:128],
                                   lnt[i][:, dc * 128:(dc + 1) * 128], ident[:])
            def evfn(h, b, i=i, dc=dc):
                return h.tensor_copy(lnfm[dc][:, i * 128:(i + 1) * 128],
                                     b[:, 0:128])
            tp.job([mmfn], [("vector", evfn, True)])
    s.build()

    s = Sched(p)
    mp = MMPipe(s, pmm, tsm, esm, xsm)
    bm = BufMgr(s)
    stj = [0]

    def simple_store(dst_ap, pool, dtype_cast="copy"):
        j = stj[0]
        stj[0] += 1
        buf = pool[j % len(pool)]
        bm.guard("vector", buf, osm)
        def ev_fn(h, b, buf=buf):
            return h.tensor_copy(buf[:], b[:])
        def after(last_x, buf=buf, dst=dst_ap):
            s.op("sync", lambda h, d=dst, bf=buf: h.dma_start(out=d, in_=bf[:]),
                 waits=[(xsm, last_x)], incs=[(osm, 16)])
            bm.mark(buf, osm.inc(16))
        return [("vector", ev_fn, True)], after

    def run_roleB(slab_buf, wait_w, dst, nch0):
        first = True
        for nch in range(4):
            for th in range(2):
                mms = []
                for dc in range(ND):
                    def f(h, b, dc=dc, nch=nch, th=th, sb_=slab_buf):
                        return h.matmul(
                            b[:],
                            sb_[:, dc * 512 + nch * 128:dc * 512 + nch * 128 + 128],
                            lnfm[dc][:, th * 512:th * 512 + 512],
                            start=(dc == 0), stop=(dc == ND - 1))
                    mms.append(f)
                steps, after = simple_store(
                    dst[(nch0 + nch) * 128:(nch0 + nch + 1) * 128,
                        th * 512:th * 512 + 512], ev)
                mp.job(mms, steps, after,
                       extra_tensor_waits=[(wsm, wait_w)] if first else ())
                first = False

    def run_roleA(slab_buf, wait_w, dst, col0, ncols):
        first = True
        for tt in range(NT):
            mms = []
            for dc in range(ND):
                def f(h, b, dc=dc, tt=tt, sb_=slab_buf):
                    return h.matmul(
                        b[:], lnfm[dc][:, tt * 128:(tt + 1) * 128],
                        sb_[:, dc * ncols:dc * ncols + ncols],
                        start=(dc == 0), stop=(dc == ND - 1))
                mms.append(f)
            steps, after = simple_store(
                dst[tt * 128:(tt + 1) * 128, col0:col0 + ncols], ev)
            mp.job(mms, steps, after,
                   extra_tensor_waits=[(wsm, wait_w)] if first else ())
            first = False

    run_roleB(wsl[0], 16 * 3, o_qfm, 0)
    t_q0 = tsm.n
    run_roleB(wsl[1], 16 * 4, o_qfm, 4)
    t_q1 = tsm.n

    s.wait("gpsimd", tsm, t_q0)
    w_k0 = s.dma("gpsimd", wsl[0][:].rearrange("p (c n) -> p c n", n=512),
                 wk[:, 0:512].rearrange("(c p) n -> p c n", p=128), wsm)
    run_roleB(wsl[0], w_k0, o_kfm, 0)
    run_roleA(wsl[0], w_k0, o_ktok, 0, 512)
    t_k0 = tsm.n
    s.wait("gpsimd", tsm, t_q1)
    w_k1 = s.dma("gpsimd", wsl[1][:].rearrange("p (c n) -> p c n", n=512),
                 wk[:, 512:1024].rearrange("(c p) n -> p c n", p=128), wsm)
    run_roleB(wsl[1], w_k1, o_kfm, 4)
    run_roleA(wsl[1], w_k1, o_ktok, 512, 512)
    prev_t = [t_k0, tsm.n]
    for phase, w_ap, dst in ((0, wv, o_v), (1, wg, o_g)):
        for si in range(4):
            buf = wsl[si % 2]
            s.wait("gpsimd", tsm, prev_t[si % 2])
            w_i = s.dma("gpsimd", buf[:].rearrange("p (c n) -> p c n", n=512),
                        w_ap[:, si * 512:(si + 1) * 512]
                        .rearrange("(c p) n -> p c n", p=128), wsm)
            run_roleA(buf, w_i, dst, si * 512, 512)
            prev_t[si % 2] = tsm.n

    for th in range(2):
        mms = []
        for dc in range(ND):
            def f(h, b, dc=dc, th=th):
                return h.matmul(b[0:16, :], wgk1b[:, dc * 16:dc * 16 + 16],
                                lnfm[dc][:, th * 512:th * 512 + 512],
                                start=(dc == 0), stop=(dc == ND - 1))
            mms.append(f)
        def evl(h, b, th=th):
            return h.tensor_copy(lowfm[0:16, th * 512:th * 512 + 512],
                                 b[0:16, :])
        mp.job(mms, [("vector", evl, True)],
               extra_tensor_waits=[(wsm, 32)] if th == 0 else ())

    def gk_steps(a, c, u_maker):
        return [
            u_maker,
            ("scalar", lambda h, b: h.activation(c[:], a[:], AF.Abs), False),
            ("scalar", lambda h, b: h.activation(c[:], c[:], AF.Exp, scale=-1.0),
             False),
            ("scalar", lambda h, b: h.activation(c[:], c[:], AF.Ln, bias=1.0),
             False),
            ("vector", lambda h, b: h.tensor_scalar(a[:], a[:], 0.0, 1.0 / 16,
                                                    ALU.min, ALU.mult), False),
            ("vector", lambda h, b: h.scalar_tensor_tensor(a[:], c[:],
                                                           -1.0 / 16, a[:],
                                                           ALU.mult, ALU.add),
             False),
        ]

    gk_j = [0]

    def gk_job(mms, u_maker, dst_ap):
        j = gk_j[0]
        gk_j[0] += 1
        a = scrA[j % 4]
        c = scrC[j % 4]
        bm.guard("vector", a, osm)
        bm.guard("scalar", a, osm)
        steps = gk_steps(a, c, u_maker(a))
        def after(last_x, a=a, dst=dst_ap):
            s.op("sync", lambda h: h.dma_start(out=dst, in_=a[:]),
                 waits=[(xsm, last_x)], incs=[(osm, 16)])
            bm.mark(a, osm.inc(16))
        mp.job(mms, steps, after)

    for tt in range(NT):
        for nsl in range(2):
            mms = [lambda h, b, tt=tt, nsl=nsl: h.matmul(
                b[:], lowfm[0:16, tt * 128:(tt + 1) * 128],
                wgk2b[0:16, nsl * 512:nsl * 512 + 512], start=True, stop=True)]
            def um(a, nsl=nsl):
                return ("vector",
                        lambda h, b, a=a, nsl=nsl: h.tensor_add(
                            a[:], b[:], bgkbc[:, nsl * 512:nsl * 512 + 512]),
                        True)
            gk_job(mms, um,
                   o_gktok[tt * 128:(tt + 1) * 128, nsl * 512:nsl * 512 + 512])

    for nch in range(8):
        for th in range(2):
            mms = [lambda h, b, nch=nch, th=th: h.matmul(
                b[:], wgk2b[0:16, nch * 128:(nch + 1) * 128],
                lowfm[0:16, th * 512:th * 512 + 512], start=True, stop=True)]
            def um(a, nch=nch):
                return ("vector",
                        lambda h, b, a=a, nch=nch: h.tensor_scalar(
                            a[:], b[:], bgkcol[:, nch:nch + 1], None, ALU.add),
                        True)
            gk_job(mms, um,
                   o_gkfm[nch * 128:(nch + 1) * 128, th * 512:th * 512 + 512])

    s.wait("sync", osm, osm.n)
    s.build()
    p.close()
    return p


def bcast_free(ap, n):
    return bass.AP(ap.tensor, ap.offset, list(ap.ap) + [[0, n]])



def build_A2():
    p = Prog()
    nc = p.nc
    i_qfm = p.din("q_fm", [KD, TLOC], BF16)
    i_kfm = p.din("k_fm", [KD, TLOC], BF16)
    i_ktok = p.din("k_tok", [TLOC, KD], BF16)
    i_v = p.din("v_tok", [TLOC, VD], BF16)
    i_g = p.din("g_tok", [TLOC, VD], BF16)
    i_gkfm = p.din("gk_fm", [KD, TLOC], F32)
    i_gktok = p.din("gk_tok", [TLOC, KD], F32)
    i_wo = p.din("wo", [VD * 1, D], F32)
    i_xres = p.din("x_res", [TLOC, D], F32)
    i_normw = p.din("normw_bc", [128, DV], F32)
    i_pairsel = p.din("pairsel", [64, 8], F32)
    i_tri = p.din("tri64", [CHK, CHK], F32)
    i_mask = p.din("mask64", [CHK, CHK], BF16)
    i_rst = p.din("rst2048", [64, 2048], F32)
    i_rst16 = p.din("rst16", [64, 15], F32)
    i_ident = p.din("ident", [128, 128], BF16)

    o_x1 = p.dout("x1", [TLOC, D], F32)
    o_ln1fm = p.dout("ln1_fm", [D, TLOC], BF16)

    cc_in = nc.dram_tensor("cc_in", [64, 16 * DV], F32)
    cc_out = nc.dram_tensor("cc_out", [64 * NCORES, 16 * DV], F32)

    obf = p.sb("obf", [128, 8 * VD], BF16)
    S = p.sb("S", [64, 16 * DV], F32)
    Sbf = p.sb("Sbf", [64, 16 * DV], BF16)
    Sin = p.sb("Sin", [64, 16 * DV], F32)
    Sinbf = p.sb("Sinbf", [64, 16 * DV], BF16)
    gend = p.sb("gend", [64, 16 * NCHK], F32)
    egnd = p.sb("egnd", [64, 16 * NCHK], F32)
    gstb = p.sb("gstb", [64, 16 * NCHK], F32)
    tri = p.sb("tri", [CHK, CHK], F32)
    msk = p.sb("msk", [CHK, CHK], BF16)
    rst = p.sb("rst", [64, 2048], F32)
    rst16 = p.sb("rst16_t", [64, 15], F32)
    psel = p.sb("psel", [64, 8], F32)
    normw = p.sb("normw", [128, DV], F32)
    ident = p.sb("identt", [128, 128], BF16)
    agb = p.sb("agb", [64, 16 * DV], F32)

    qq8 = p.sb("qq8", [64, 16 * 128], BF16)
    kq8 = p.sb("kq8", [64, 16 * 128], BF16)
    gkq8 = p.sb("gkq8", [64, 16 * 128], F32)
    Gq8 = p.sb("Gq8", [64, 16 * 128], F32)
    qts = p.sb("qts", [64, 16 * 128], BF16)
    kts = p.sb("kts", [64, 16 * 128], BF16)
    ktc = p.sb("ktc", [64, KD], BF16)
    vc = p.sb("vc", [64, VD], BF16)
    gktc = p.sb("gktc", [64, KD], F32)
    egmt = p.sb("egmt", [64, KD], F32)
    ktil = p.sb("ktil", [64, KD], BF16)
    atb = [p.sb(f"atb{i}", [64, 64], BF16) for i in range(4)]
    qhb = [p.sb(f"qhb{i}", [64, 64], BF16) for i in range(4)]

    gt = p.sb("gt", [128, VD], BF16)
    sgt = p.sb("sgt", [128, D], F32)
    sqt = p.sb("sqt", [128, D], F32)
    ms = p.sb("ms", [128, 16 * 2], F32)
    xrt = p.sb("xrt", [128, D], F32)
    aot = p.sb("aot", [128, D], F32)
    x1t = p.sb("x1t", [128, D], F32)
    ln1t = p.sb("ln1t", [128, D], BF16)
    ogfm = p.sb("ogfm", [128, ND * 128], BF16)
    wosl = p.sb("wosl0", [128, ND * 512], BF16)
    st8 = p.sb("st8", [128, 16], F32)
    lnao = sgt
    lnscr = sqt

    psA = [p.ps(f"psA{i}", [64, 512], F32) for i in range(2)]
    psO = [p.ps(f"psO{i}", [128, 512], F32) for i in range(2)]
    psS2 = [p.ps(f"psS2{i}", [64, 512], F32) for i in range(2)]
    pstr = p.ps("pstr", [128, 1024], BF16)
    pmm = [p.ps("pmm0", [128, 512], F32)]

    ld = p.sem("ld")
    qld = p.sem("qld")
    sv = p.sem("sv")
    vv = p.sem("vv")
    tsm = p.sem("tsm")
    xsm = p.sem("xsm")
    osm = p.sem("osm")
    ccs = p.sem("ccs")

    s = Sched(p)
    p.enable_consts(s, ld)
    for dst, src_ in ((tri[:], i_tri), (msk[:], i_mask), (rst[:], i_rst),
                      (rst16[:], i_rst16), (psel[:], i_pairsel),
                      (normw[:], i_normw), (ident[:], i_ident)):
        s.dma("sync", dst, src_, ld)
    s.op("vector", lambda h: h.memset(S[:], 0.0))
    s.op("vector", lambda h: h.memset(Sbf[:], 0.0))
    s.op("vector", lambda h: h.memset(Sin[:], 0.0))
    s.wait("sync", ld, ld.n)
    s.build()

    obf3 = obf[:].rearrange("p (c n) -> p c n", n=VD)

    def emit_qkprep(s, e8, with_q=True, with_k=True):
        sl0 = slice(0, 16 * 128)
        c0 = e8 * 128
        wq_ = s.dma("sync", qq8[:].rearrange("p (a t) -> p a t", t=128),
                    i_qfm[:, c0:c0 + 128].rearrange("(a p) t -> p a t", p=64),
                    qld) if with_q else None
        wk_ = s.dma("sync", kq8[:].rearrange("p (a t) -> p a t", t=128),
                    i_kfm[:, c0:c0 + 128].rearrange("(a p) t -> p a t", p=64),
                    qld) if with_k else None
        wg_ = s.dma("sync", gkq8[:].rearrange("p (a t) -> p a t", t=128),
                    i_gkfm[:, c0:c0 + 128].rearrange("(a p) t -> p a t", p=64),
                    qld)
        s.wait("vector", qld, wg_)
        s.op("vector", lambda h: h.tensor_tensor_scan(
            Gq8[:], rst[:], gkq8[:], 0.0, ALU.mult, ALU.add), incs=[(vv, 1)])
        tG = vv.inc()
        s.op("scalar", lambda h: h.activation(gkq8[:], Gq8[:], AF.Exp),
             waits=[(vv, tG)], incs=[(sv, 1)])
        t1 = sv.inc()
        if with_q:
            s.wait("vector", qld, wq_)
            s.op("vector", lambda h: h.scalar_tensor_tensor(
                qts[:], qq8[:], 0.125, gkq8[:], ALU.mult, ALU.mult),
                waits=[(sv, t1)], incs=[(vv, 1)])
        else:
            s.op("vector", lambda h: h.engine_nop(), waits=[(sv, t1)],
                 incs=[(vv, 1)])
        tq = vv.inc()
        if with_k:
            s.op("scalar", lambda h: h.activation(gkq8[:], Gq8[:], AF.Exp,
                                                  scale=-1.0),
                 waits=[(vv, tq)], incs=[(sv, 1)])
            t2 = sv.inc()
            s.wait("vector", qld, wk_)
            s.op("vector", lambda h: h.tensor_mul(kts[:], kq8[:], gkq8[:]),
                 waits=[(sv, t2)], incs=[(vv, 1)])
            vv.inc()
        return vv.n, tG

    for e8 in range(8):
        s = Sched(p)
        c0 = e8 * 128
        prep_v, tG = emit_qkprep(s, e8)
        gsl = Gq8[:]
        gv = bass.AP(gsl.tensor, gsl.offset + 63,
                     [[gsl.ap[0][0], 64], [128, 16], [64, 2]])
        gd = bass.AP(gend[:].tensor, gend[:].offset + 2 * e8,
                     [[gend[:].ap[0][0], 64], [NCHK, 16], [1, 2]])
        ed = bass.AP(egnd[:].tensor, egnd[:].offset + 2 * e8,
                     [[egnd[:].ap[0][0], 64], [NCHK, 16], [1, 2]])
        s.op("vector", lambda h, gv=gv, gd=gd: h.tensor_copy(gd, gv),
             waits=[(vv, prep_v)], incs=[(vv, 1)])
        tgd = vv.inc()
        s.op("scalar", lambda h, gd=gd, ed=ed: h.activation(ed, gd, AF.Exp),
             waits=[(vv, tgd)], incs=[(sv, 1)])
        tegn = sv.inc()
        for e in ("tensor", "vector"):
            s.wait(e, sv, sv.n)
            s.wait(e, vv, vv.n)

        hist = {}
        for cl in range(2):
            cg = 2 * e8 + cl
            rp = cl * 64
            if cl == 1:
                s.wait("sync", tsm, tsm.n)
                s.wait("sync", vv, vv.n)
            wk1 = s.dma("sync", ktc[:],
                        i_ktok[c0 + cl * 64:c0 + cl * 64 + 64, :], qld)
            wv1 = s.dma("sync", vc[:],
                        i_v[c0 + cl * 64:c0 + cl * 64 + 64, :], qld)
            wg1 = s.dma("sync", gktc[:],
                        i_gktok[c0 + cl * 64:c0 + cl * 64 + 64, :], qld)
            s.wait("tensor", qld, wg1)
            for half in range(2):
                s.op("tensor", lambda h, half=half: h.matmul(
                    psS2[0][0:64, :], tri[:],
                    gktc[:, half * 512:half * 512 + 512],
                    start=True, stop=True), incs=[(tsm, 1)])
                tj = tsm.inc()
                s.op("scalar", lambda h, half=half: h.activation(
                    egmt[:, half * 512:half * 512 + 512], psS2[0][0:64, :],
                    AF.Exp, scale=-1.0), waits=[(tsm, tj)], incs=[(sv, 1)])
                tk = sv.inc()
                s.wait("tensor", sv, tk)
            s.wait("vector", qld, wk1)
            s.op("vector", lambda h: h.tensor_mul(ktil[:], ktc[:], egmt[:]),
                 waits=[(sv, sv.n)], incs=[(vv, 1)])
            kt_ready = vv.inc()
            s.wait("tensor", vv, kt_ready)
            s.wait("tensor", qld, wv1)

            for hh in range(16):
                j = cg * 16 + hh
                qsl = qts[0:64, hh * 128 + cl * 64:hh * 128 + cl * 64 + 64]
                ksl = kts[0:64, hh * 128 + cl * 64:hh * 128 + cl * 64 + 64]
                vsl = vc[0:64, hh * DV:hh * DV + DV]
                bA, bO, bS = psA[hh % 2], psO[hh % 2], psS2[hh % 2]
                wA = [(vv, hist[('m', j - 2)])] if ('m', j - 2) in hist else []
                s.op("tensor", lambda h, b=bA, ksl=ksl, qsl=qsl: h.matmul(
                    b[0:64, 0:64], ksl, qsl, start=True, stop=True),
                    waits=wA, incs=[(tsm, 1)])
                t_at = tsm.inc()
                s.op("vector", lambda h, b=bA, j=j: h.tensor_mul(
                    atb[j % 4][:], b[0:64, 0:64], msk[:]),
                    waits=[(tsm, t_at)], incs=[(vv, 1)])
                hist[('m', j)] = t_msk = vv.inc()
                wO = [(vv, t_msk)]
                if ('oe', j - 2) in hist:
                    wO.append((vv, hist[('oe', j - 2)]))
                s.op("tensor", lambda h, b=bO, j=j, vsl=vsl, rp=rp: h.matmul(
                    b[rp:rp + 64, 0:DV], atb[j % 4][:], vsl, start=True,
                    stop=False, tile_position=(0, rp)), waits=wO)
                s.op("tensor", lambda h, b=bO, qsl=qsl, hh=hh, rp=rp: h.matmul(
                    b[rp:rp + 64, 0:DV], qsl, Sbf[:, hh * DV:hh * DV + DV],
                    start=False, stop=True, tile_position=(0, rp)),
                    incs=[(tsm, 1)])
                t_o = tsm.inc()
                wS = []
                if ('su', j - 2) in hist:
                    wS.append((vv, hist[('su', j - 2)]))
                if cl == 0 and hh < 2:
                    wS.append((sv, sv.n))
                s.op("tensor", lambda h, b=bS, hh=hh, vsl=vsl: h.matmul(
                    b[0:64, 0:DV], ktil[:, hh * 64:hh * 64 + 64], vsl,
                    start=True, stop=True), waits=wS, incs=[(tsm, 1)])
                t_s = tsm.inc()
                oc = obf3[rp:rp + 64, cg // 2, hh * DV:hh * DV + DV]
                s.op("vector", lambda h, b=bO, oc=oc, rp=rp: h.tensor_copy(
                    oc, b[rp:rp + 64, 0:DV]), waits=[(tsm, t_o)],
                    incs=[(vv, 1)])
                hist[('oe', j)] = vv.inc()
                egcol = egnd[0:64, hh * NCHK + cg:hh * NCHK + cg + 1]
                s.op("vector", lambda h, b=bS, hh=hh: h.tensor_add(
                    S[:, hh * DV:hh * DV + DV], S[:, hh * DV:hh * DV + DV],
                    b[0:64, 0:DV]), waits=[(tsm, t_s)], incs=[(vv, 1)])
                t_u1 = vv.inc()
                s.op("vector", lambda h, hh=hh, egcol=egcol: h.tensor_scalar(
                    S[:, hh * DV:hh * DV + DV], S[:, hh * DV:hh * DV + DV],
                    egcol, None, ALU.mult), waits=[(vv, t_u1)], incs=[(vv, 1)])
                t_u2 = vv.inc()
                s.op("vector", lambda h, hh=hh: h.tensor_copy(
                    Sbf[:, hh * DV:hh * DV + DV],
                    S[:, hh * DV:hh * DV + DV]), waits=[(vv, t_u2)],
                    incs=[(vv, 1)])
                hist[('su', j)] = vv.inc()
            s.wait("tensor", vv, vv.n)
        s.build()

    s = Sched(p)
    w1 = s.dma("gpsimd", cc_in.ap(), S[:], ccs)
    s.op("gpsimd", lambda h: h.collective_compute(
        "AllGather", ALU.bypass, replica_groups=[list(range(NCORES))],
        ins=[cc_in.ap().opt()], outs=[cc_out.ap().opt()]),
        waits=[(ccs, w1)], incs=[(ccs, 1)])
    wcc = ccs.inc()
    s.wait("sync", ccs, wcc)
    for r in range(NCORES):
        wr = s.dma("sync", agb[:], cc_out.ap()[r * 64:(r + 1) * 64, :], ccs)
        s.wait("vector", ccs, wr)
        s.op("vector", lambda h, r=r: h.scalar_tensor_tensor(
            Sin[:], agb[:], psel[:, r:r + 1], Sin[:], ALU.mult, ALU.add),
            incs=[(vv, 1)])
        wv2 = vv.inc()
        if r < NCORES - 1:
            s.wait("sync", vv, wv2)
    s.op("vector", lambda h: h.tensor_copy(Sinbf[:], Sin[:]),
         waits=[(vv, vv.n)], incs=[(vv, 1)])
    vv.inc()
    for hhh in range(16):
        b0 = hhh * NCHK
        s.op("vector", lambda h, b0=b0: h.memset(gstb[:, b0:b0 + 1], 0.0),
             incs=[(vv, 1)])
        t0 = vv.inc()
        s.op("vector", lambda h, b0=b0: h.tensor_tensor_scan(
            gstb[:, b0 + 1:b0 + NCHK], rst16[:], gend[:, b0:b0 + NCHK - 1],
            0.0, ALU.mult, ALU.add), waits=[(vv, t0)], incs=[(vv, 1)])
        vv.inc()
    s.op("scalar", lambda h: h.activation(gstb[:], gstb[:], AF.Exp),
         waits=[(vv, vv.n)], incs=[(sv, 1)])
    sv.inc()
    s.build()

    for e8 in range(8):
        s = Sched(p)
        if e8 == 0:
            for e in ("vector", "tensor"):
                s.wait(e, sv, sv.n)
        prep_v, _ = emit_qkprep(s, e8, with_q=True, with_k=False)
        s.wait("tensor", vv, prep_v)
        hist = {}
        for cl in range(2):
            cg = 2 * e8 + cl
            rp = cl * 64
            for hh in range(16):
                j = cg * 16 + hh
                wq_ = []
                if ('cm', j - 4) in hist:
                    wq_.append((tsm, hist[('cm', j - 4)]))
                s.op("vector", lambda h, j=j, hh=hh, cl=cl, cg=cg: h.tensor_scalar(
                    qhb[j % 4][:],
                    qts[0:64, hh * 128 + cl * 64:hh * 128 + cl * 64 + 64],
                    gstb[0:64, hh * NCHK + cg:hh * NCHK + cg + 1], None,
                    ALU.mult), waits=wq_, incs=[(vv, 1)])
                t_q = vv.inc()
                wmm = [(vv, t_q)]
                if ('ca', j - 2) in hist:
                    wmm.append((vv, hist[('ca', j - 2)]))
                s.op("tensor", lambda h, j=j, hh=hh, rp=rp: h.matmul(
                    psO[j % 2][rp:rp + 64, 0:DV], qhb[j % 4][:],
                    Sinbf[:, hh * DV:hh * DV + DV], start=True, stop=True,
                    tile_position=(0, rp)), waits=wmm, incs=[(tsm, 1)])
                hist[('cm', j)] = t_m = tsm.inc()
                oc = obf3[rp:rp + 64, cg // 2, hh * DV:hh * DV + DV]
                s.op("vector", lambda h, j=j, oc=oc, rp=rp: h.tensor_add(
                    oc, oc, psO[j % 2][rp:rp + 64, 0:DV]),
                    waits=[(tsm, t_m)], incs=[(vv, 1)])
                hist[('ca', j)] = vv.inc()
        s.build()

    s = Sched(p)
    for e in ("vector", "tensor", "scalar"):
        s.wait(e, vv, vv.n)
    mp = MMPipe(s, pmm, tsm, None, xsm)
    tp_ = MMPipe(s, [pstr], tsm, None, xsm)
    wo_w = [0]
    inv128 = 1.0 / DV
    for tt in range(NT):
        wg_ = s.dma("sync", gt[:], i_g[tt * 128:(tt + 1) * 128, :], qld)
        wx_ = s.dma("sync", xrt[:], i_xres[tt * 128:(tt + 1) * 128, :], qld)
        s.wait("scalar", qld, wg_)
        otile = obf[:, tt * VD * 8 // 8:tt * VD + VD] if False else \
            obf[:, tt * VD:(tt + 1) * VD]
        s.op("scalar", lambda h, otile=otile: h.activation(
            sqt[:, 0:VD], otile, AF.Square), incs=[(sv, 1)])
        t0 = sv.inc()
        s.op("vector", lambda h: h.tensor_reduce(
            ms[:, 0:16], sqt[:, 0:VD].rearrange("p (a b) -> p a b", b=DV),
            AX.X, ALU.add), waits=[(sv, t0)], incs=[(vv, 1)])
        t1 = vv.inc()
        s.op("scalar", lambda h: h.activation(
            ms[:, 16:32], ms[:, 0:16], AF.Ln, bias=p.cc(EPS),
            scale=p.cc(inv128)), waits=[(vv, t1)], incs=[(sv, 1)])
        t2 = sv.inc()
        s.op("scalar", lambda h: h.activation(
            ms[:, 0:16], ms[:, 16:32], AF.Exp, scale=p.cc(-0.5)),
            waits=[(sv, t2)], incs=[(sv, 1)])
        t3 = sv.inc()
        s.op("scalar", lambda h: h.activation(sgt[:, 0:VD], gt[:], AF.Silu),
             incs=[(sv, 1)])
        t4 = sv.inc()
        rstd_bc = bass.AP(ms[:, 0:16].tensor, ms[:, 0:16].offset,
                          [[ms[:, 0:16].ap[0][0], 128], [1, 16], [0, DV]])
        s.op("vector", lambda h, otile=otile, rb=rstd_bc: h.tensor_mul(
            sqt[:, 0:VD].rearrange("p (a b) -> p a b", b=DV),
            otile.rearrange("p (a b) -> p a b", b=DV), rb),
            waits=[(sv, t3)], incs=[(vv, 1)])
        t5 = vv.inc()
        nw_bc = bass.AP(normw[:].tensor, normw[:].offset,
                        [[normw[:].ap[0][0], 128], [0, 16], [1, DV]])
        s.op("vector", lambda h, nb=nw_bc: h.tensor_mul(
            sqt[:, 0:VD].rearrange("p (a b) -> p a b", b=DV),
            sqt[:, 0:VD].rearrange("p (a b) -> p a b", b=DV), nb),
            waits=[(vv, t5)], incs=[(vv, 1)])
        t6 = vv.inc()
        s.op("vector", lambda h: h.tensor_mul(sgt[:, 0:VD], sqt[:, 0:VD],
                                              sgt[:, 0:VD]),
             waits=[(vv, t6), (sv, t4)], incs=[(vv, 1)])
        t7 = vv.inc()
        s.op("vector", lambda h: h.tensor_copy(gt[:], sgt[:, 0:VD]),
             waits=[(vv, t7)], incs=[(vv, 1)])
        t8 = vv.inc()
        s.wait("tensor", vv, t8)
        for dc in range(ND):
            def tfn(h, b, dc=dc):
                return h.transpose(b[:, 0:128], gt[:, dc * 128:(dc + 1) * 128],
                                   ident[:])
            def efn(h, b, dc=dc):
                return h.tensor_copy(ogfm[:, dc * 128:(dc + 1) * 128],
                                     b[:, 0:128])
            tp_.job([tfn], [("vector", efn, True)])
        for nsl in range(4):
            if wo_w[0]:
                s.wait("gpsimd", tsm, wo_w[0])
            wld = s.dma("gpsimd", wosl[:].rearrange("p (c n) -> p c n", n=512),
                        i_wo[:, nsl * 512:(nsl + 1) * 512].rearrange(
                            "(c p) n -> p c n", p=128), ld)
            mms = []
            for dc in range(ND):
                def f(h, b, dc=dc):
                    return h.matmul(b[:], ogfm[:, dc * 128:(dc + 1) * 128],
                                    wosl[:, dc * 512:dc * 512 + 512],
                                    start=(dc == 0), stop=(dc == ND - 1))
                mms.append(f)
            def evf(h, b, nsl=nsl):
                return h.tensor_copy(aot[:, nsl * 512:nsl * 512 + 512], b[:])
            mp.job(mms, [("vector", evf, True)],
                   extra_tensor_waits=[(ld, wld)])
            wo_w[0] = tsm.n
        s.wait("scalar", xsm, xsm.n)
        s.wait("vector", xsm, xsm.n)
        s.wait("vector", qld, wx_)
        emit_ln(s, p, aot[:], lnao[:, 0:D], st8[:, 0:8], sv, vv, D, lnscr[:])
        s.op("vector", lambda h: h.tensor_add(x1t[:], xrt[:], lnao[:, 0:D]),
             waits=[(vv, vv.n)], incs=[(vv, 1)])
        tx1 = vv.inc()
        s.wait("sync", vv, tx1)
        s.dma("sync", o_x1[tt * 128:(tt + 1) * 128, :], x1t[:], osm)
        s.wait("scalar", vv, tx1)
        emit_ln(s, p, x1t[:], ln1t[:], st8[:, 8:16], sv, vv, D, lnscr[:])
        s.wait("tensor", vv, vv.n)
        for dc in range(ND):
            def tfn(h, b, dc=dc):
                return h.transpose(b[:, 0:128], ln1t[:, dc * 128:(dc + 1) * 128],
                                   ident[:])
            def efn(h, b, dc=dc):
                return h.tensor_copy(ogfm[:, dc * 128:(dc + 1) * 128],
                                     b[:, 0:128])
            tp_.job([tfn], [("vector", efn, True)])
        s.wait("sync", xsm, xsm.n)
        s.dma("sync", o_ln1fm[:, tt * 128:(tt + 1) * 128].rearrange(
            "(c p) t -> p c t", p=128),
            ogfm[:].rearrange("p (c t) -> p c t", t=128), osm)
        s.wait("vector", osm, osm.n)
        s.wait("scalar", osm, osm.n)
        s.wait("vector", tsm, tsm.n)
    s.wait("sync", osm, osm.n)
    s.build()
    p.close()
    return p



def build_F1(stop=3):
    import os
    stop = int(os.environ.get("F1_STOP", stop))
    p = Prog()
    nc = p.nc
    TB = 2048
    i_ln = p.din("ln1_fm", [D, TB], BF16)
    i_gwt = p.din("gate_WT", [D, 8], F32)
    i_w1r = p.din("W1r", [6 * D, 1408], F32)
    i_w2r = p.din("W2r", [6 * 1408, D], F32)
    i_b1t = p.din("b1T", [1408, 8], F32)
    i_b2 = p.din("b2rows", [6, D], F32)
    i_sel2 = p.din("sel2", [1, 2], F32)
    i_sel8 = p.din("sel8", [1, 8], F32)
    i_iotap = p.din("iota1e9", [1, 8], F32)
    o_out = p.dout("out_slot", [TB, D], F32)
    o_row = p.dout("row_dbg", [1, 48], F32)
    o_m8 = p.dout("m8_dbg", [1, 8], F32)
    o_idx = p.dout("idx_dbg", [128, 16], U32)
    o_idx2 = p.dout("idx2_dbg", [128, 16], U32)
    o_irep = p.dout("irep_dbg", [128, 1], U32)

    lnb = p.sb("lnb", [128, ND * 1024], BF16)
    gwt = p.sb("gwt", [128, ND * 8], BF16)
    w1b = p.sb("w1b", [128, ND * 1408], BF16)
    w2b = p.sb("w2b", [128, 11 * D], BF16)
    hq = p.sb("hq", [128, 11 * 1024], BF16)
    w1raw = [p.sb(f"w1raw{i}", [128, 1408], F32) for i in range(2)]
    w2raw = [p.sb(f"w2raw{i}", [128, D], F32) for i in range(2)]
    b1t = p.sb("b1t", [128, 11 * 8], F32)
    b1c = p.sb("b1c", [128, 11], F32)
    b2rep = p.sb("b2rep", [128, D], F32)
    sums = p.sb("sums", [128, 16], F32)
    sumsb = p.sb("sumsb", [128, 16], BF16)
    row = p.sb("row", [1, 48], F32)
    row8 = p.sb("row8", [1, 8], F32)
    m8 = p.sb("m8", [1, 8], F32)
    selr = p.sb("selr", [1, 8], F32)
    sel2r = p.sb("sel2r", [1, 2], F32)
    iotar = p.sb("iotar", [1, 8], F32)
    onesc = p.sb("onesc", [1, 128], F32)
    mywc = p.sb("mywc", [128, 1], F32)
    basec = p.sb("basec", [128, 1], F32)
    oh6bc = p.sb("oh6bc", [128, 8], F32)
    iom = p.sb("iom", [128, 16], F32)
    idxf = p.sb("idxf", [128, 16], F32)
    idxu = p.sb("idxu", [128, 16], U32)
    idx2u = p.sb("idx2u", [128, 16], U32)
    idxrep = p.sb("idxrep", [128, 1], U32)
    ev = [p.sb(f"ev{i}", [128, 512], F32) for i in range(8)]
    scr = p.sb("scrF", [128, 2048], F32)

    pmm = [p.ps(f"pmm{i}", [128, 512], F32) for i in range(6)]
    psm = p.ps("psm", [128, 512], F32)

    ld = p.sem("ld")
    gsm = p.sem("gsm")
    sv = p.sem("sv")
    vv = p.sem("vv")
    tsm = p.sem("tsm")
    xsm = p.sem("xsm")
    osm = p.sem("osm")

    s = Sched(p)
    p.enable_consts(s, ld)
    s.dma("gpsimd", gwt[:].rearrange("p (c n) -> p c n", n=8),
          i_gwt.rearrange("(c p) n -> p c n", p=128), ld)
    s.dma("sync", b1t[:].rearrange("p (c n) -> p c n", n=8),
          i_b1t[0:1408, :].rearrange("(c p) n -> p c n", p=128), ld)
    s.dma("sync", selr[:], i_sel8, ld)
    s.dma("sync", sel2r[:], i_sel2, ld)
    s.dma("sync", iotar[:], i_iotap, ld)
    wln = s.dma("sync", lnb[:].rearrange("p (c n) -> p c n", n=1024),
                i_ln[:, 0:1024].rearrange("(c p) n -> p c n", p=128), ld)
    s.op("vector", lambda h: h.memset(onesc[:], 1.0))
    s.op("gpsimd", lambda h: h.iota(iom[:], [[128, 16]], base=0,
                                    channel_multiplier=1,
                                    allow_small_or_imprecise_dtypes=True))
    s.wait("scalar", ld, wln)
    for dc in range(ND):
        s.op("scalar", lambda h, dc=dc: h.activation(
            scr[:, 0:1024], lnb[:, dc * 1024:(dc + 1) * 1024], AF.Copy,
            accum_out=sums[:, dc:dc + 1]),
            incs=[(sv, 1)] if dc == ND - 1 else ())
    t_s1 = sv.inc()
    for dc in range(ND):
        if dc > 0:
            s.wait("sync", sv, sv.n)
        w2_ = s.dma("sync", hq[:, 0:1024],
                    i_ln[dc * 128:(dc + 1) * 128, 1024:2048], ld)
        s.wait("scalar", ld, w2_)
        s.op("scalar", lambda h, dc=dc: h.activation(
            scr[:, 1024:2048], hq[:, 0:1024], AF.Copy,
            accum_out=idxf[:, dc:dc + 1]),
            incs=[(sv, 1)])
        sv.inc()
    s.wait("vector", sv, sv.n)
    s.op("vector", lambda h: h.tensor_add(sums[:], sums[:], idxf[:]),
         incs=[(vv, 1)])
    t_sum = vv.inc()
    s.op("vector", lambda h: h.tensor_copy(sumsb[:], sums[:]),
         waits=[(vv, t_sum)], incs=[(vv, 1)])
    t_sb = vv.inc()
    s.wait("tensor", vv, t_sb)
    s.wait("tensor", ld, 16 * 2)
    for dc in range(ND):
        s.op("tensor", lambda h, dc=dc: h.matmul(
            psm[0:1, 0:8], sumsb[:, dc:dc + 1], gwt[:, dc * 8:dc * 8 + 8],
            start=(dc == 0), stop=(dc == ND - 1)),
            incs=[(tsm, 1)] if dc == ND - 1 else ())
    t_mm = tsm.inc()
    def rop(fn, wait_prev=True):
        s.op("vector", fn, waits=[(vv, vv.n)], incs=[(vv, 1)])
        vv.inc()
    s.op("vector", lambda h: h.tensor_scalar(row[:, 0:8], psm[0:1, 0:8],
                                             1.0 / TB, None, ALU.mult),
         waits=[(tsm, t_mm)], incs=[(vv, 1)])
    vv.inc()
    rop(lambda h: h.scalar_tensor_tensor(row[:, 0:8], selr[:], -1e30,
                                         row[:, 0:8], ALU.mult, ALU.add))
    rop(lambda h: h.reduce_max(row[:, 16:17], row[:, 0:8], AX.X, ALU.max)
        if False else h.reduce_max(row[:, 16:17], row[:, 0:8], AX.X))
    rop(lambda h: h.tensor_scalar(row[:, 8:16], row[:, 0:8], row[:, 16:17],
                                  None, ALU.subtract))
    s.op("scalar", lambda h: h.activation(row[:, 8:16], row[:, 8:16], AF.Exp),
         waits=[(vv, vv.n)], incs=[(sv, 1)])
    sv.inc()
    s.wait("vector", sv, sv.n)
    rop(lambda h: h.tensor_reduce(row[:, 17:18], row[:, 8:16], AX.X, ALU.add))
    rop(lambda h: h.reciprocal(row[:, 18:19], row[:, 17:18]))
    rop(lambda h: h.tensor_scalar(row[:, 8:16], row[:, 8:16], row[:, 18:19],
                                  None, ALU.mult))
    rop(lambda h: h.max(m8[:], row[:, 8:16]))
    rop(lambda h: h.tensor_mul(row[:, 20:22], m8[:, 0:2], sel2r[:]))
    rop(lambda h: h.tensor_reduce(row[:, 22:23], row[:, 20:22], AX.X, ALU.add))
    rop(lambda h: h.tensor_reduce(row[:, 23:24], m8[:, 0:2], AX.X, ALU.add))
    rop(lambda h: h.reciprocal(row[:, 24:25], row[:, 23:24]))
    rop(lambda h: h.tensor_mul(row[:, 25:26], row[:, 22:23], row[:, 24:25]))
    rop(lambda h: h.tensor_scalar(row8[:], row[:, 8:16], m8[:, 0:1], None,
                                  ALU.is_ge))
    rop(lambda h: h.scalar_tensor_tensor(row8[:], row8[:], -1000.0, iotar[:],
                                         ALU.mult, ALU.add))
    rop(lambda h: h.tensor_reduce(row[:, 26:27], row8[:], AX.X, ALU.min))
    rop(lambda h: h.tensor_scalar(row8[:], iotar[:], row[:, 26:27], None,
                                  ALU.subtract))
    rop(lambda h: h.tensor_scalar(row8[:], row8[:], 1000.0, None, ALU.is_equal))
    rop(lambda h: h.scalar_tensor_tensor(row[:, 28:36], row8[:], -2.0,
                                         row[:, 8:16], ALU.mult, ALU.add))
    rop(lambda h: h.tensor_scalar(row[:, 28:36], row[:, 28:36], m8[:, 1:2],
                                  None, ALU.is_ge))
    rop(lambda h: h.scalar_tensor_tensor(row[:, 28:36], row[:, 28:36], -1000.0,
                                         iotar[:], ALU.mult, ALU.add))
    rop(lambda h: h.tensor_reduce(row[:, 27:28], row[:, 28:36], AX.X, ALU.min))
    rop(lambda h: h.tensor_mul(row[:, 36:37], row[:, 26:27], sel2r[:, 0:1]))
    rop(lambda h: h.scalar_tensor_tensor(row[:, 37:38], row[:, 27:28],
                                         1.0, row[:, 36:37], ALU.mult,
                                         ALU.add) if False else
        h.scalar_tensor_tensor(row[:, 37:38], sel2r[:, 1:2], 1.0,
                               row[:, 36:37], ALU.mult, ALU.add))
    rop(lambda h: h.tensor_mul(row[:, 38:39], row[:, 27:28], sel2r[:, 1:2]))
    rop(lambda h: h.tensor_add(row[:, 39:40], row[:, 36:37], row[:, 38:39]))
    rop(lambda h: h.tensor_scalar(row[:, 40:48], iotar[:], 1000.0, None,
                                  ALU.subtract))
    rop(lambda h: h.tensor_scalar(row[:, 40:48], row[:, 40:48],
                                  row[:, 39:40], None, ALU.is_equal))
    t_gate = vv.n

    s.op("vector", lambda h: h.tensor_scalar(row[:, 41 - 41 + 28:29],
                                             row[:, 39:40], float(D), None,
                                             ALU.mult),
         waits=[(vv, t_gate)], incs=[(vv, 1)])
    t_base = vv.inc()
    s.wait("tensor", vv, t_base)
    bc_jobs = [(mywc, row[:, 25:26], 1), (basec, row[:, 28:29], 1),
               (oh6bc, row[:, 40:48], 8)]
    for bi, (dst, srcr, ncol) in enumerate(bc_jobs):
        s.op("tensor", lambda h, srcr=srcr, ncol=ncol: h.matmul(
            psm[0:128, 8:8 + ncol], onesc[:], srcr, start=True, stop=True),
            incs=[(tsm, 1)])
        tb = tsm.inc()
        s.op("vector", lambda h, dst=dst, ncol=ncol: h.tensor_copy(
            dst[:, 0:ncol], psm[0:128, 8:8 + ncol]), waits=[(tsm, tb)],
            incs=[(vv, 1)])
        tv = vv.inc()
        s.wait("tensor", vv, tv)
    s.op("vector", lambda h: h.tensor_scalar(idxf[:], iom[:], basec[:, 0:1],
                                             None, ALU.add),
         waits=[(vv, vv.n)], incs=[(vv, 1)])
    t_i = vv.inc()
    s.op("vector", lambda h: h.tensor_copy(idxu[:], idxf[:]),
         waits=[(vv, t_i)], incs=[(vv, 1)])
    t_iu = vv.inc()
    s.op("vector", lambda h: h.tensor_scalar(idxf[:], row[:, 39:40],
                                             1408.0, None, ALU.mult),
         waits=[(vv, vv.n)], incs=[(vv, 1)]) if False else None
    s.op("vector", lambda h: h.tensor_scalar(
        row[:, 29:30], row[:, 39:40], 1408.0, None, ALU.mult),
        waits=[(vv, t_iu)], incs=[(vv, 1)])
    t_b2 = vv.inc()
    s.op("tensor", lambda h: h.matmul(psm[0:128, 8:9], onesc[:],
                                      row[:, 29:30], start=True, stop=True),
         waits=[(vv, t_b2)], incs=[(tsm, 1)])
    t_mm2 = tsm.inc()
    s.op("vector", lambda h: h.tensor_copy(basec[:], psm[0:128, 8:9]),
         waits=[(tsm, t_mm2)], incs=[(vv, 1)])
    t_b2c = vv.inc()
    s.op("vector", lambda h: h.tensor_scalar(idxf[:], iom[:], basec[:, 0:1],
                                             None, ALU.add),
         waits=[(vv, t_b2c)], incs=[(vv, 1)])
    t_i2 = vv.inc()
    s.op("vector", lambda h: h.tensor_copy(idx2u[:], idxf[:]),
         waits=[(vv, t_i2)], incs=[(vv, 1)])
    t_iu2 = vv.inc()
    for hc in range(11):
        s.op("vector", lambda h, hc=hc: h.tensor_mul(
            scr[:, 0:8], b1t[:, hc * 8:hc * 8 + 8], oh6bc[:]),
            waits=[(vv, vv.n)], incs=[(vv, 1)])
        ta = vv.inc()
        s.op("vector", lambda h, hc=hc: h.tensor_reduce(
            b1c[:, hc:hc + 1], scr[:, 0:8], AX.X, ALU.add),
            waits=[(vv, ta)], incs=[(vv, 1)])
        vv.inc()
    s.op("vector", lambda h: h.tensor_copy(idxrep[:],
                                           basec[:, 0:1]),
         waits=[(vv, t_b2c)], incs=[(vv, 1)]) if False else None
    s.op("tensor", lambda h: h.matmul(psm[0:128, 8:9], onesc[:],
                                      row[:, 39:40], start=True, stop=True),
         waits=[(vv, vv.n)], incs=[(tsm, 1)])
    t_mm3 = tsm.inc()
    s.op("vector", lambda h: h.tensor_copy(basec[:], psm[0:128, 8:9]),
         waits=[(tsm, t_mm3)], incs=[(vv, 1)])
    t_me = vv.inc()
    s.op("vector", lambda h: h.tensor_copy(idxrep[:], basec[:]),
         waits=[(vv, t_me)], incs=[(vv, 1)])
    t_irep = vv.inc()
    s.wait("gpsimd", vv, max(t_iu, t_iu2, t_irep))
    s.wait("sync", vv, vv.n)
    s.dma("sync", o_row, row[:], osm)
    s.dma("sync", o_m8, m8[:], osm)
    s.dma("sync", o_idx, idxu[:], osm)
    s.dma("sync", o_idx2, idx2u[:], osm)
    s.dma("sync", o_irep, idxrep[:], osm)
    s.wait("sync", osm, osm.n)
    if stop == 1:
        s.build()
        p.close()
        return p
    wb2 = s.dma("gpsimd", b2rep[:], None, gsm) if False else None
    s.op("gpsimd", lambda h: h.indirect_dma_start(
        out=b2rep[:], out_offset=None, in_=i_b2,
        in_offset=IndirectOffsetOnAxis(ap=idxrep[:, 0:1], axis=0)),
        incs=[(gsm, 16)])
    s.wait("gpsimd", gsm, gsm.inc(16))
    s.build()

    s = Sched(p)
    for i in range(ND):
        bi = i % 2
        if i >= 2:
            s.wait("gpsimd", vv, gcast[i - 2])
        s.op("gpsimd", lambda h, i=i, bi=bi: h.indirect_dma_start(
            out=w1raw[bi][:], out_offset=None, in_=i_w1r,
            in_offset=IndirectOffsetOnAxis(ap=idxu[:, i:i + 1], axis=0)),
            incs=[(gsm, 16)])
        wg = gsm.inc(16)
        s.wait("gpsimd", gsm, wg)
        s.wait("vector", gsm, wg)
        s.op("vector", lambda h, i=i, bi=bi: h.tensor_copy(
            w1b[:, i * 1408:(i + 1) * 1408], w1raw[bi][:]), incs=[(vv, 1)])
        if i == 0:
            gcast = {}
        gcast[i] = vv.inc()
    for i in range(11):
        bi = i % 2
        if i >= 2:
            s.wait("gpsimd", vv, gcast2[i - 2])
        s.op("gpsimd", lambda h, i=i, bi=bi: h.indirect_dma_start(
            out=w2raw[bi][:], out_offset=None, in_=i_w2r,
            in_offset=IndirectOffsetOnAxis(ap=idx2u[:, i:i + 1], axis=0)),
            incs=[(gsm, 16)])
        wg = gsm.inc(16)
        s.wait("gpsimd", gsm, wg)
        s.wait("vector", gsm, wg)
        s.op("vector", lambda h, i=i, bi=bi: h.tensor_copy(
            w2b[:, i * D:(i + 1) * D], w2raw[bi][:]), incs=[(vv, 1)])
        if i == 0:
            gcast2 = {}
        gcast2[i] = vv.inc()
    s.wait("vector", vv, vv.n)
    s.build()
    if stop == 2:
        p.close()
        return p

    s = Sched(p)
    s.op("vector", lambda h: h.tensor_scalar(b2rep[:], b2rep[:],
                                             mywc[:, 0:1], None, ALU.mult),
         incs=[(vv, 1)])
    vv.inc()
    s.build()

    for half in range(2):
        s = Sched(p)
        if half == 1:
            wlh = s.dma("sync", lnb[:].rearrange("p (c n) -> p c n", n=1024),
                        i_ln[:, 1024:2048].rearrange("(c p) n -> p c n",
                                                     p=128), ld)
            for e in ("tensor",):
                s.wait(e, ld, wlh)
        mp = MMPipe(s, pmm, tsm, None, xsm)
        bm = BufMgr(s)
        stj = [0]
        for hc in range(11):
            for th in range(2):
                mms = []
                for dc in range(ND):
                    def f(h, b, dc=dc, hc=hc, th=th):
                        return h.matmul(
                            b[:], w1b[:, dc * 1408 + hc * 128:
                                      dc * 1408 + hc * 128 + 128],
                            lnb[:, dc * 1024 + th * 512:
                                dc * 1024 + th * 512 + 512],
                            start=(dc == 0), stop=(dc == ND - 1))
                    mms.append(f)
                def evf(h, b, hc=hc, th=th):
                    return h.activation(
                        hq[:, hc * 1024 + th * 512:hc * 1024 + th * 512 + 512],
                        b[:], AF.Gelu, bias=b1c[:, hc:hc + 1])
                mp.job(mms, [("scalar", evf, True)])
        for tt in range(8):
            for osl in range(4):
                mms = []
                for hc in range(11):
                    def f(h, b, hc=hc, tt=tt, osl=osl):
                        return h.matmul(
                            b[:], hq[:, hc * 1024 + tt * 128:
                                     hc * 1024 + tt * 128 + 128],
                            w2b[:, hc * D + osl * 512:hc * D + osl * 512 + 512],
                            start=(hc == 0), stop=(hc == 10))
                    mms.append(f)
                j = stj[0]
                stj[0] += 1
                buf = ev[j % 8]
                bm.guard("vector", buf, osm)
                def evf(h, b, buf=buf, osl=osl):
                    return h.scalar_tensor_tensor(
                        buf[:], b[:], mywc[:, 0:1],
                        b2rep[:, osl * 512:osl * 512 + 512],
                        ALU.mult, ALU.add)
                def after(lx, buf=buf, tt=tt, osl=osl, half=half):
                    s.op("sync", lambda h: h.dma_start(
                        out=o_out[half * 1024 + tt * 128:
                                  half * 1024 + (tt + 1) * 128,
                                  osl * 512:osl * 512 + 512], in_=buf[:]),
                        waits=[(xsm, lx)], incs=[(osm, 16)])
                    bm.mark(buf, osm.inc(16))
                mp.job(mms, [("vector", evf, True)], after)
        s.build()
    p.close()
    return p





def build_F2M1():
    p = Prog()
    nc = p.nc
    i_s0 = p.din("slot0", [TLOC, D], F32)
    i_s1 = p.din("slot1", [TLOC, D], F32)
    i_x1 = p.din("x1", [TLOC, D], F32)
    i_wint = p.din("W_inT", [D, 2 * DIN], F32)
    i_ident = p.din("ident", [128, 128], BF16)
    o_x2 = p.dout("x2", [TLOC, D], F32)
    o_xz = p.dout("xz_fm", [2 * DIN, TLOC], BF16)

    t0 = p.sb("t0", [128, D], F32)
    t1 = p.sb("t1", [128, D], F32)
    x2t = p.sb("x2t", [128, D], F32)
    ln2t = p.sb("ln2t", [128, D], BF16)
    lnscr = p.sb("lnscr", [128, D], F32)
    st8 = p.sb("st8", [128, 16], F32)
    ident = p.sb("identt", [128, 128], BF16)
    ln2fm = p.sb("ln2fm", [128, ND * TLOC], BF16)
    wsl = [p.sb(f"wsl{i}", [128, ND * 512], BF16) for i in range(2)]
    ev = [p.sb(f"ev{i}", [128, 512], BF16) for i in range(8)]

    pstr = p.ps("pstr", [128, 1024], BF16)
    pmm = [p.ps(f"pmm{i}", [128, 512], F32) for i in range(6)]

    ld = p.sem("ld")
    sv = p.sem("sv")
    vv = p.sem("vv")
    tsm = p.sem("tsm")
    xsm = p.sem("xsm")
    osm = p.sem("osm")

    s = Sched(p)
    p.enable_consts(s, ld)
    wid = s.dma("sync", ident[:], i_ident, ld)
    s.wait("tensor", ld, wid)
    s.wait("scalar", ld, 16)
    tp_ = MMPipe(s, [pstr], tsm, None, xsm)
    for tt in range(NT):
        sl = slice(tt * 128, (tt + 1) * 128)
        w0 = s.dma("sync", t0[:], i_s0[sl, :], ld)
        w1 = s.dma("sync", t1[:], i_s1[sl, :], ld)
        wx = s.dma("sync", x2t[:], i_x1[sl, :], ld)
        s.wait("vector", ld, w1)
        s.op("vector", lambda h: h.tensor_add(t0[:], t0[:], t1[:]),
             incs=[(vv, 1)])
        ta = vv.inc()
        s.wait("scalar", vv, ta)
        emit_ln(s, p, t0[:], t1[:], st8[:, 0:8], sv, vv, D, lnscr[:])
        s.wait("vector", ld, wx)
        s.op("vector", lambda h: h.tensor_add(x2t[:], x2t[:], t1[:]),
             waits=[(vv, vv.n)], incs=[(vv, 1)])
        tx2 = vv.inc()
        s.wait("sync", vv, tx2)
        s.dma("sync", o_x2[sl, :], x2t[:], osm)
        s.wait("scalar", vv, tx2)
        emit_ln(s, p, x2t[:], ln2t[:], st8[:, 8:16], sv, vv, D, lnscr[:])
        s.wait("tensor", vv, vv.n)
        for dc in range(ND):
            def tfn(h, b, dc=dc):
                return h.transpose(b[:, 0:128], ln2t[:, dc * 128:(dc + 1) * 128],
                                   ident[:])
            def efn(h, b, dc=dc, tt=tt):
                return h.tensor_copy(
                    ln2fm[:, dc * TLOC + tt * 128:dc * TLOC + (tt + 1) * 128],
                    b[:, 0:128])
            tp_.job([tfn], [("vector", efn, True)])
        s.wait("sync", vv, vv.n)
        s.wait("sync", xsm, xsm.n)
    s.build()

    s = Sched(p)
    mp = MMPipe(s, pmm, tsm, None, xsm)
    bm = BufMgr(s)
    stj = [0]
    wprev = [0, 0]
    for slab in range(16):
        bi = slab % 2
        if wprev[bi]:
            s.wait("gpsimd", tsm, wprev[bi])
        wld = s.dma("gpsimd", wsl[bi][:].rearrange("p (c n) -> p c n", n=512),
                    i_wint[:, slab * 512:(slab + 1) * 512].rearrange(
                        "(c p) n -> p c n", p=128), ld)
        first = True
        for nch in range(4):
            for th in range(2):
                mms = []
                for dc in range(ND):
                    def f(h, b, dc=dc, nch=nch, th=th, bi=bi):
                        return h.matmul(
                            b[:],
                            wsl[bi][:, dc * 512 + nch * 128:dc * 512 + nch * 128 + 128],
                            ln2fm[:, dc * TLOC + th * 512:dc * TLOC + th * 512 + 512],
                            start=(dc == 0), stop=(dc == ND - 1))
                    mms.append(f)
                j = stj[0]
                stj[0] += 1
                buf = ev[j % 8]
                bm.guard("vector", buf, osm)
                def evf(h, b, buf=buf):
                    return h.tensor_copy(buf[:], b[:])
                def after(lx, buf=buf, slab=slab, nch=nch, th=th):
                    s.op("sync", lambda h: h.dma_start(
                        out=o_xz[slab * 512 + nch * 128:slab * 512 + (nch + 1) * 128,
                                 th * 512:th * 512 + 512], in_=buf[:]),
                        waits=[(xsm, lx)], incs=[(osm, 16)])
                    bm.mark(buf, osm.inc(16))
                mp.job(mms, [("vector", evf, True)], after,
                       extra_tensor_waits=[(ld, wld)] if first else ())
                first = False
        wprev[bi] = tsm.n
    s.wait("sync", osm, osm.n)
    s.build()
    p.close()
    return p



def build_M2():
    p = Prog()
    nc = p.nc
    i_xh = p.din("xh_fm", [DIN, TLOC + 3], BF16)
    i_cw = p.din("conv_w", [128, 32 * 4], F32)
    i_cb = p.din("conv_b", [128, 32], F32)
    i_xpt = p.din("x_projT", [DIN, 160], F32)
    i_dpt = p.din("dt_projT", [128, DIN], F32)
    i_dpb2 = p.din("dpb2", [128, 32], F32)
    o_xc = p.dout("xc_fm", [DIN, TLOC], F32)
    o_dt = p.dout("dt_fm", [DIN, TLOC], F32)
    o_bc = p.dout("bc_rows", [32, TLOC], F32)

    xht = p.sb("xht", [128, TLOC + 3], BF16)
    cw = p.sb("cw", [128, 32 * 4], F32)
    cb = p.sb("cb", [128, 32], F32)
    acc = p.sb("acc", [128, TLOC], F32)
    xcf = p.sb("xcf", [128, TLOC], F32)
    xcb = p.sb("xcb", [128, 32 * TLOC], BF16)
    xpt = p.sb("xpt", [128, 32 * 160], BF16)
    dpt = p.sb("dpt", [128, DIN], BF16)
    dpb = p.sb("dpb", [128, 32], F32)
    dtlow = p.sb("dtlow", [128, TLOC], BF16)
    u_ = p.sb("u_", [128, 512], F32)
    uc_ = p.sb("uc_", [128, 512], F32)
    ev = [p.sb(f"ev{i}", [128, 512], F32) for i in range(4)]

    pmm = [p.ps(f"pmm{i}", [128, 512], F32) for i in range(6)]

    ld = p.sem("ld")
    sv = p.sem("sv")
    vv = p.sem("vv")
    tsm = p.sem("tsm")
    xsm = p.sem("xsm")
    osm = p.sem("osm")

    s = Sched(p)
    p.enable_consts(s, ld)
    s.dma("sync", cw[:], i_cw, ld)
    wcb = s.dma("sync", cb[:], i_cb, ld)
    s.dma("gpsimd", xpt[:].rearrange("p (c n) -> p c n", n=160),
          i_xpt.rearrange("(c p) n -> p c n", p=128), ld)
    wdp = s.dma("gpsimd", dpt[:], i_dpt, ld)
    s.dma("sync", dpb[:], i_dpb2, ld)
    s.wait("scalar", ld, ld.n)
    s.wait("vector", ld, ld.n)
    for dc in range(32):
        wx = s.dma("sync", xht[:], i_xh[dc * 128:(dc + 1) * 128, :], ld)
        s.wait("vector", ld, wx)
        s.op("vector", lambda h, dc=dc: h.tensor_scalar(
            acc[:], xht[:, 0:TLOC], cw[:, 4 * dc:4 * dc + 1], None, ALU.mult),
            incs=[(vv, 1)])
        tv = vv.inc()
        for j in range(1, 4):
            s.op("vector", lambda h, dc=dc, j=j: h.scalar_tensor_tensor(
                acc[:], xht[:, j:j + TLOC], cw[:, 4 * dc + j:4 * dc + j + 1],
                acc[:], ALU.mult, ALU.add), waits=[(vv, tv)], incs=[(vv, 1)])
            tv = vv.inc()
        s.op("scalar", lambda h, dc=dc: h.activation(
            xcf[:], acc[:], AF.Silu, bias=cb[:, dc:dc + 1]),
            waits=[(vv, tv)], incs=[(sv, 1)])
        ts_ = sv.inc()
        s.op("vector", lambda h, dc=dc: h.tensor_copy(
            xcb[:, dc * TLOC:(dc + 1) * TLOC], xcf[:]), waits=[(sv, ts_)],
            incs=[(vv, 1)])
        tv2 = vv.inc()
        s.wait("sync", sv, ts_)
        s.dma("sync", o_xc[dc * 128:(dc + 1) * 128, :], xcf[:], osm)
        s.wait("sync", vv, tv)
        s.wait("scalar", osm, osm.n)
    s.build()

    s = Sched(p)
    mp = MMPipe(s, pmm, tsm, None, xsm)
    for nch in range(2):
        ncols = 128 if nch == 0 else 32
        for th in range(2):
            mms = []
            for dc in range(32):
                def f(h, b, dc=dc, nch=nch, th=th, ncols=ncols):
                    return h.matmul(
                        b[0:ncols, :],
                        xpt[:, dc * 160 + nch * 128:dc * 160 + nch * 128 + ncols],
                        xcb[:, dc * TLOC + th * 512:dc * TLOC + th * 512 + 512],
                        start=(dc == 0), stop=(dc == 31))
                mms.append(f)
            if nch == 0:
                def evf(h, b, th=th):
                    return h.tensor_copy(dtlow[:, th * 512:th * 512 + 512],
                                         b[0:128, :])
            else:
                def evf(h, b, th=th):
                    return h.tensor_copy(ev[th][:, 0:512][0:32, :], b[0:32, :])
            def after(lx, nch=nch, th=th):
                if nch == 1:
                    s.op("sync", lambda h, th=th: h.dma_start(
                        out=o_bc[:, th * 512:th * 512 + 512],
                        in_=ev[th][0:32, 0:512]),
                        waits=[(xsm, lx)], incs=[(osm, 16)])
                    osm.inc(16)
            mp.job(mms, [("vector", evf, True)], after)
    for dc in range(32):
        for th in range(2):
            mms = [lambda h, b, dc=dc, th=th: h.matmul(
                b[:], dpt[:, dc * 128:(dc + 1) * 128],
                dtlow[:, th * 512:th * 512 + 512], start=True, stop=True)]
            steps = [
                ("vector", lambda h, b, dc=dc: h.tensor_scalar(
                    u_[:], b[:], dpb[:, dc:dc + 1], None, ALU.add), True),
                ("vector", lambda h, b: h.tensor_scalar(
                    uc_[:], u_[:], 70.0, None, ALU.min), False),
                ("scalar", lambda h, b: h.activation(uc_[:], uc_[:], AF.Exp),
                 False),
                ("scalar", lambda h, b: h.activation(uc_[:], uc_[:], AF.Ln,
                                                     bias=1.0), False),
                ("vector", lambda h, b, dc=dc: h.tensor_max(
                    u_[:], uc_[:], u_[:]), False),
            ]
            def after(lx, dc=dc, th=th):
                s.op("sync", lambda h, dc=dc, th=th: h.dma_start(
                    out=o_dt[dc * 128:(dc + 1) * 128, th * 512:th * 512 + 512],
                    in_=u_[:]), waits=[(xsm, lx)], incs=[(osm, 16)])
                osm.inc(16)
                s.wait("vector", osm, osm.n)
            mp.job(mms, steps, after)
    s.wait("sync", osm, osm.n)
    s.build()
    p.close()
    return p



def build_M3():
    p = Prog()
    nc = p.nc
    TF = 2048
    CK = 256
    NCK = TF // CK
    i_dt = p.din("dt", [2048, TF], F32)
    i_xc = p.din("xc", [2048, TF], F32)
    i_z = p.din("z_fm", [2048, TF], BF16)
    i_B = p.din("Brows", [16, TF], F32)
    i_C = p.din("Crows", [16, TF], F32)
    i_al = p.din("A_log", [128, 16 * 16], F32)
    i_D = p.din("Dcol", [128, 16], F32)
    i_wot = p.din("WoutT_sl", [2048, D], F32)
    o_part = p.dout("part", [TF, D], F32)

    dtq = p.sb("dtq", [128, 16 * CK], F32)
    xcq = p.sb("xcq", [128, 16 * CK], F32)
    dxq = p.sb("dxq", [128, 16 * CK], BF16)
    zq = p.sb("zq", [128, 16 * CK], BF16)
    yq = p.sb("yq", [128, 16 * CK], F32)
    yzb = p.sb("yzb", [128, 16 * CK], BF16)
    Bbc = p.sb("Bbc", [128, 16 * CK], BF16)
    Cbc = p.sb("Cbc", [128, 16 * CK], BF16)
    Brow = p.sb("Brow", [1, 16 * CK], F32)
    Crow = p.sb("Crow", [1, 16 * CK], F32)
    onesc = p.sb("onesc", [1, 128], F32)
    acol = p.sb("acol", [128, 16 * 16], F32)
    aq = [p.sb(f"aq{i}", [128, CK], F32) for i in range(2)]
    bq = [p.sb(f"bq{i}", [128, CK], BF16) for i in range(2)]
    hq = p.sb("hq", [128, 16 * CK], BF16)
    pq = p.sb("pq", [128, 16 * CK], BF16)
    bqa = p.sb("bqa", [128, 16 * CK], BF16)
    Hc = p.sb("Hc", [128, 16 * 16], F32)
    Dc = p.sb("Dc", [128, 16], F32)
    wsl = p.sb("wsl0", [128, 16 * 512], BF16)
    ev = [p.sb(f"ev{i}", [128, 512], F32) for i in range(6)]

    pmm = [p.ps(f"pmm{i}", [128, 512], F32) for i in range(6)]

    ld = p.sem("ld")
    sv = p.sem("sv")
    vv = p.sem("vv")
    tsm = p.sem("tsm")
    xsm = p.sem("xsm")
    osm = p.sem("osm")

    s = Sched(p)
    p.enable_consts(s, ld)
    wal = s.dma("sync", acol[:], i_al, ld)
    s.dma("sync", Dc[:], i_D, ld)
    s.op("vector", lambda h: h.memset(onesc[:], 1.0))
    s.op("vector", lambda h: h.memset(Hc[:], 0.0))
    s.wait("scalar", ld, wal)
    s.op("scalar", lambda h: h.activation(acol[:], acol[:], AF.Exp),
         incs=[(sv, 1)])
    ts0 = sv.inc()
    s.op("vector", lambda h: h.tensor_scalar(acol[:], acol[:], -1.0, None,
                                             ALU.mult), waits=[(sv, ts0)],
         incs=[(vv, 1)])
    vv.inc()
    s.build()

    for ck in range(NCK):
        s = Sched(p)
        c0 = ck * CK
        wdt = s.dma("sync", dtq[:].rearrange("p (c n) -> p c n", n=CK),
                    i_dt[:, c0:c0 + CK].rearrange("(c p) n -> p c n", p=128), ld)
        wxc = s.dma("sync", xcq[:].rearrange("p (c n) -> p c n", n=CK),
                    i_xc[:, c0:c0 + CK].rearrange("(c p) n -> p c n", p=128), ld)
        wz = s.dma("sync", zq[:].rearrange("p (c n) -> p c n", n=CK),
                   i_z[:, c0:c0 + CK].rearrange("(c p) n -> p c n", p=128), ld)
        wB = s.dma("sync", Brow[:].rearrange("p (a n) -> p a n", n=CK),
                   i_B[:, c0:c0 + CK].rearrange("(p a) n -> p a n", p=1), ld)
        wC = s.dma("sync", Crow[:].rearrange("p (a n) -> p a n", n=CK),
                   i_C[:, c0:c0 + CK].rearrange("(p a) n -> p a n", p=1), ld)
        s.wait("tensor", ld, wC)
        mpb = MMPipe(s, pmm[0:3], tsm, None, xsm)
        for sx in range(16):
            for dst, rowt in ((Bbc, Brow), (Cbc, Crow)):
                def f(h, b, sx=sx, rowt=rowt):
                    return h.matmul(b[0:128, 0:CK], onesc[:],
                                    rowt[0:1, sx * CK:(sx + 1) * CK],
                                    start=True, stop=True)
                def e(h, b, sx=sx, dst=dst):
                    return h.tensor_copy(dst[:, sx * CK:(sx + 1) * CK],
                                         b[0:128, 0:CK])
                mpb.job([f], [("vector", e, True)])
        s.wait("vector", ld, wxc)
        s.op("vector", lambda h: h.tensor_mul(dxq[:], dtq[:], xcq[:]),
             waits=[(vv, vv.n)], incs=[(vv, 1)])
        vv.inc()
        s.wait("scalar", ld, wdt)
        s.wait("scalar", vv, vv.n)
        s.wait("vector", xsm, xsm.n)
        tts_hist = {}
        for dtl in range(16):
            sl = slice(dtl * CK, (dtl + 1) * CK)
            dsl = dxq[:, sl]
            dx_b = bass.AP(dsl.tensor, dsl.offset,
                           [[dsl.ap[0][0], 128], [0, 16], [1, CK]])
            s.op("vector", lambda h, dx_b=dx_b: h.tensor_mul(
                bqa[:].rearrange("p (a n) -> p a n", n=CK), dx_b,
                Bbc[:].rearrange("p (a n) -> p a n", n=CK)),
                waits=[(vv, vv.n)], incs=[(vv, 1)])
            t_ball = vv.inc()
            for sx in range(16):
                it = dtl * 16 + sx
                ssl = slice(sx * CK, (sx + 1) * CK)
                hc = Hc[:, dtl * 16 + sx:dtl * 16 + sx + 1]
                wa = [(vv, tts_hist[it - 2])] if (it - 2) in tts_hist else []
                s.op("scalar", lambda h, sl=sl, dtl=dtl, sx=sx, it=it:
                     h.activation(aq[it % 2][:], dtq[:, sl], AF.Exp,
                                  scale=acol[:, dtl * 16 + sx:
                                             dtl * 16 + sx + 1]),
                     waits=wa, incs=[(sv, 1)])
                t_a = sv.inc()
                s.op("vector", lambda h, hc=hc, it=it, ssl=ssl:
                     h.tensor_tensor_scan(
                         hq[:, ssl], aq[it % 2][:], bqa[:, ssl], hc,
                         ALU.mult, ALU.add),
                     waits=[(vv, t_ball), (sv, t_a)], incs=[(vv, 1)])
                tts_hist[it] = vv.inc()
            hq0 = hq[:]
            hq_last = bass.AP(hq0.tensor, hq0.offset + CK - 1,
                              [[hq0.ap[0][0], 128], [CK, 16]])
            s.op("vector", lambda h, dtl=dtl, hq_last=hq_last: h.tensor_copy(
                Hc[:, dtl * 16:(dtl + 1) * 16], hq_last),
                waits=[(vv, vv.n)], incs=[(vv, 1)])
            vv.inc()
            s.op("vector", lambda h: h.tensor_mul(pq[:], hq[:], Cbc[:]),
                 waits=[(vv, vv.n)], incs=[(vv, 1)])
            t_p = vv.inc()
            pq0 = pq[:]
            pq_ts = bass.AP(pq0.tensor, pq0.offset,
                            [[pq0.ap[0][0], 128], [1, CK], [CK, 16]])
            s.op("vector", lambda h, sl=sl, pq_ts=pq_ts: h.tensor_reduce(
                yq[:, sl], pq_ts, AX.X, ALU.add),
                waits=[(vv, t_p)], incs=[(vv, 1)])
            vv.inc()
            s.op("vector", lambda h, sl=sl, dtl=dtl: h.scalar_tensor_tensor(
                yq[:, sl], xcq[:, sl], Dc[:, dtl:dtl + 1], yq[:, sl],
                ALU.mult, ALU.add), waits=[(vv, vv.n)], incs=[(vv, 1)])
            td = vv.inc()
            s.wait("vector", ld, wz)
            s.op("vector", lambda h, sl=sl: h.tensor_mul(
                yzb[:, sl], yq[:, sl], zq[:, sl]), waits=[(vv, td)],
                incs=[(vv, 1)])
            vv.inc()
        s.wait("tensor", vv, vv.n)
        mp = MMPipe(s, pmm, tsm, None, xsm)
        bm = BufMgr(s)
        stj = [0]
        wpr = [0]
        for osl in range(4):
            if wpr[0]:
                s.wait("gpsimd", tsm, wpr[0])
            wld = s.dma("gpsimd", wsl[:].rearrange("p (c n) -> p c n", n=512),
                        i_wot[:, osl * 512:(osl + 1) * 512].rearrange(
                            "(c p) n -> p c n", p=128), ld)
            first = True
            for ts128 in range(2):
                mms = []
                for dtl in range(16):
                    def f(h, b, dtl=dtl, ts128=ts128):
                        return h.matmul(
                            b[:], yzb[:, dtl * CK + ts128 * 128:
                                      dtl * CK + ts128 * 128 + 128],
                            wsl[:, dtl * 512:dtl * 512 + 512],
                            start=(dtl == 0), stop=(dtl == 15))
                    mms.append(f)
                j = stj[0]
                stj[0] += 1
                buf = ev[j % 6]
                bm.guard("vector", buf, osm)
                def evf(h, b, buf=buf):
                    return h.tensor_copy(buf[:], b[:])
                def after(lx, buf=buf, osl=osl, ts128=ts128, c0=c0):
                    s.op("sync", lambda h: h.dma_start(
                        out=o_part[c0 + ts128 * 128:c0 + (ts128 + 1) * 128,
                                   osl * 512:osl * 512 + 512], in_=buf[:]),
                        waits=[(xsm, lx)], incs=[(osm, 16)])
                    bm.mark(buf, osm.inc(16))
                mp.job(mms, [("vector", evf, True)], after,
                       extra_tensor_waits=[(ld, wld)] if first else ())
                first = False
            wpr[0] = tsm.n
        s.wait("sync", vv, vv.n)
        s.wait("sync", sv, sv.n)
        s.wait("sync", tsm, tsm.n)
        if ck == NCK - 1:
            s.wait("sync", osm, osm.n)
        s.build()
    p.close()
    return p



def build_M5():
    p = Prog()
    nc = p.nc
    i_p0 = p.din("part0", [TLOC, D], F32)
    i_p1 = p.din("part1", [TLOC, D], F32)
    i_x2 = p.din("x2", [TLOC, D], F32)
    i_ident = p.din("ident", [128, 128], BF16)
    o_x3 = p.dout("x3", [TLOC, D], F32)
    o_ln3 = p.dout("ln3_tok", [TLOC, D], BF16)
    o_ln3fm = p.dout("ln3_fm", [D, TLOC], BF16)

    t0 = p.sb("t0", [128, D], F32)
    t1 = p.sb("t1", [128, D], F32)
    x3t = p.sb("x3t", [128, D], F32)
    ln3t = p.sb("ln3t", [128, D], BF16)
    lnscr = p.sb("lnscr", [128, D], F32)
    st8 = p.sb("st8", [128, 16], F32)
    ident = p.sb("identt", [128, 128], BF16)
    ogfm = p.sb("ogfm", [128, ND * 128], BF16)
    pstr = p.ps("pstr", [128, 1024], BF16)

    ld = p.sem("ld")
    sv = p.sem("sv")
    vv = p.sem("vv")
    tsm = p.sem("tsm")
    xsm = p.sem("xsm")
    osm = p.sem("osm")

    s = Sched(p)
    p.enable_consts(s, ld)
    wid = s.dma("sync", ident[:], i_ident, ld)
    s.wait("tensor", ld, wid)
    s.wait("scalar", ld, 16)
    tp_ = MMPipe(s, [pstr], tsm, None, xsm)
    for tt in range(NT):
        sl = slice(tt * 128, (tt + 1) * 128)
        w0 = s.dma("sync", t0[:], i_p0[sl, :], ld)
        w1 = s.dma("sync", t1[:], i_p1[sl, :], ld)
        wx = s.dma("sync", x3t[:], i_x2[sl, :], ld)
        s.wait("vector", ld, w1)
        s.op("vector", lambda h: h.tensor_add(t0[:], t0[:], t1[:]),
             incs=[(vv, 1)])
        ta = vv.inc()
        s.wait("scalar", vv, ta)
        emit_ln(s, p, t0[:], t1[:], st8[:, 0:8], sv, vv, D, lnscr[:])
        s.wait("vector", ld, wx)
        s.op("vector", lambda h: h.tensor_add(x3t[:], x3t[:], t1[:]),
             waits=[(vv, vv.n)], incs=[(vv, 1)])
        tx3 = vv.inc()
        s.wait("sync", vv, tx3)
        s.dma("sync", o_x3[sl, :], x3t[:], osm)
        s.wait("scalar", vv, tx3)
        emit_ln(s, p, x3t[:], ln3t[:], st8[:, 8:16], sv, vv, D, lnscr[:])
        tln = vv.n
        s.wait("sync", vv, tln)
        s.dma("sync", o_ln3[sl, :], ln3t[:], osm)
        s.wait("tensor", vv, tln)
        for dc in range(ND):
            def tfn(h, b, dc=dc):
                return h.transpose(b[:, 0:128], ln3t[:, dc * 128:(dc + 1) * 128],
                                   ident[:])
            def efn(h, b, dc=dc):
                return h.tensor_copy(ogfm[:, dc * 128:(dc + 1) * 128],
                                     b[:, 0:128])
            tp_.job([tfn], [("vector", efn, True)])
        s.wait("sync", xsm, xsm.n)
        s.dma("sync", o_ln3fm[:, sl].rearrange("(c p) t -> p c t", p=128),
              ogfm[:].rearrange("p (c t) -> p c t", t=128), osm)
        s.wait("sync", osm, osm.n)
        s.wait("vector", osm, osm.n)
    s.build()
    p.close()
    return p



def build_G1():
    p = Prog()
    nc = p.nc
    i_ln3fm = p.din("ln3_fm", [D, TLOC], BF16)
    i_wg = p.din("w_gate", [D, 64], F32)
    i_wsu = p.din("w_shared_up", [D, 1024], F32)
    i_wsd = p.din("w_shared_down", [512, D], F32)
    i_iota64 = p.din("iota64_bc", [128, 64], F32)
    o_sh = p.dout("sharedout", [TLOC, D], F32)
    o_m = p.dout("mmat", [TLOC, 64], F32)
    o_widx = p.dout("widx", [TLOC, 8], F32)
    o_wn = p.dout("wnorm", [TLOC, 8], F32)
    o_sc = p.dout("sc_dbg", [TLOC, 64], F32)

    lnfm = p.sb("lnfm", [128, ND * TLOC], BF16)
    wgb = p.sb("wgb", [128, ND * 64], BF16)
    wsub = p.sb("wsub", [128, ND * 1024], BF16)
    wsdb = p.sb("wsdb", [128, 4 * D], BF16)
    sufm = p.sb("sufm", [128, 8 * TLOC], F32)
    hsh = p.sb("hsh", [128, 4 * TLOC], BF16)
    sc = p.sb("sc", [128, 64], F32)
    scw = p.sb("scw", [128, 64], F32)
    m8t = p.sb("m8t", [128, 8], F32)
    macc = p.sb("macc", [128, 64], F32)
    widxt = p.sb("widxt", [128, 8], F32)
    wnt = p.sb("wnt", [128, 8], F32)
    io64 = p.sb("io64", [128, 64], F32)
    r1 = p.sb("r1", [128, 64], F32)
    r2 = p.sb("r2", [128, 1], F32)
    ev = [p.sb(f"ev{i}", [128, 512], F32) for i in range(6)]

    pmm = [p.ps(f"pmm{i}", [128, 512], F32) for i in range(6)]

    ld = p.sem("ld")
    sv = p.sem("sv")
    vv = p.sem("vv")
    tsm = p.sem("tsm")
    xsm = p.sem("xsm")
    osm = p.sem("osm")

    s = Sched(p)
    p.enable_consts(s, ld)
    s.dma("gpsimd", wgb[:].rearrange("p (c n) -> p c n", n=64),
          i_wg.rearrange("(c p) n -> p c n", p=128), ld)
    s.dma("gpsimd", wsub[:].rearrange("p (c n) -> p c n", n=1024),
          i_wsu.rearrange("(c p) n -> p c n", p=128), ld)
    s.dma("gpsimd", wsdb[:].rearrange("p (c n) -> p c n", n=D),
          i_wsd.rearrange("(c p) n -> p c n", p=128), ld)
    wio = s.dma("sync", io64[:], i_iota64, ld)
    wln = s.dma("sync", lnfm[:].rearrange("p (c n) -> p c n", n=TLOC),
                i_ln3fm.rearrange("(c p) n -> p c n", p=128), ld)
    s.wait("tensor", ld, ld.n)
    s.wait("vector", ld, ld.n)
    s.wait("scalar", ld, 16)

    mp = MMPipe(s, pmm, tsm, None, xsm)
    bm = BufMgr(s)
    for nch in range(8):
        for th in range(2):
            mms = []
            for dc in range(ND):
                def f(h, b, dc=dc, nch=nch, th=th):
                    return h.matmul(
                        b[:], wsub[:, dc * 1024 + nch * 128:
                                   dc * 1024 + nch * 128 + 128],
                        lnfm[:, dc * TLOC + th * 512:dc * TLOC + th * 512 + 512],
                        start=(dc == 0), stop=(dc == ND - 1))
                mms.append(f)
            def evf(h, b, nch=nch, th=th):
                return h.tensor_copy(
                    sufm[:, nch * TLOC + th * 512:nch * TLOC + th * 512 + 512],
                    b[:])
            mp.job(mms, [("vector", evf, True)])
    s.wait("scalar", xsm, xsm.n)
    for u in range(4):
        s.op("scalar", lambda h, u=u: h.activation(
            sufm[:, (4 + u) * TLOC:(5 + u) * TLOC],
            sufm[:, (4 + u) * TLOC:(5 + u) * TLOC], AF.Silu), incs=[(sv, 1)])
        tsx = sv.inc()
        s.op("vector", lambda h, u=u: h.tensor_mul(
            hsh[:, u * TLOC:(u + 1) * TLOC], sufm[:, (4 + u) * TLOC:(5 + u) * TLOC],
            sufm[:, u * TLOC:(u + 1) * TLOC]), waits=[(sv, tsx)],
            incs=[(vv, 1)])
        vv.inc()
    s.wait("tensor", vv, vv.n)
    stj = [0]
    for tt in range(NT):
        for osl in range(4):
            mms = []
            for hc in range(4):
                def f(h, b, hc=hc, tt=tt, osl=osl):
                    return h.matmul(
                        b[:], hsh[:, hc * TLOC + tt * 128:hc * TLOC + tt * 128 + 128],
                        wsdb[:, hc * D + osl * 512:hc * D + osl * 512 + 512],
                        start=(hc == 0), stop=(hc == 3))
                mms.append(f)
            j = stj[0]
            stj[0] += 1
            buf = ev[j % 6]
            bm.guard("vector", buf, osm)
            def evf(h, b, buf=buf):
                return h.tensor_copy(buf[:], b[:])
            def after(lx, buf=buf, tt=tt, osl=osl):
                s.op("sync", lambda h: h.dma_start(
                    out=o_sh[tt * 128:(tt + 1) * 128, osl * 512:osl * 512 + 512],
                    in_=buf[:]), waits=[(xsm, lx)], incs=[(osm, 16)])
                bm.mark(buf, osm.inc(16))
            mp.job(mms, [("vector", evf, True)], after)
    sc_guard = [0, 0]
    for tt in range(NT):
        mms = []
        for dc in range(ND):
            def f(h, b, dc=dc, tt=tt):
                return h.matmul(
                    b[0:128, 0:64], wgb[:, dc * 64:(dc + 1) * 64],
                    lnfm[:, dc * TLOC + tt * 128:dc * TLOC + tt * 128 + 128],
                    start=(dc == 0), stop=(dc == ND - 1))
            mms.append(f)
        mms = []
        for dc in range(ND):
            def f(h, b, dc=dc, tt=tt):
                return h.matmul(
                    b[0:128, 0:64],
                    lnfm[:, dc * TLOC + tt * 128:dc * TLOC + tt * 128 + 128],
                    wgb[:, dc * 64:(dc + 1) * 64],
                    start=(dc == 0), stop=(dc == ND - 1))
            mms.append(f)
        steps = [("scalar",
                  lambda h, b: h.activation(sc[:], b[0:128, 0:64], AF.Sigmoid),
                  True)]
        etw = []
        if sc_guard[0]:
            etw = [(vv, sc_guard[0]), (osm, sc_guard[1])]
        xv = mp.job(mms, steps, extra_tensor_waits=etw)
        s.wait("sync", xsm, xv)
        s.dma("sync", o_sc[tt * 128:(tt + 1) * 128, :], sc[:], osm)
        s.op("vector", lambda h: h.tensor_copy(scw[:], sc[:]),
             waits=[(xsm, xv)], incs=[(vv, 1)])
        tw = vv.inc()
        sc_guard = [tw, osm.n]
        s.op("vector", lambda h: h.memset(macc[:], 0.0), incs=[(vv, 1)])
        tm0 = vv.inc()
        s.op("vector", lambda h: h.max(m8t[:], scw[:]),
             waits=[(vv, tw)], incs=[(vv, 1)])
        tmx = vv.inc()
        for k in range(6):
            s.op("vector", lambda h, k=k: h.tensor_scalar(
                r1[:], scw[:], m8t[:, k:k + 1], None, ALU.is_ge),
                waits=[(vv, vv.n)], incs=[(vv, 1)])
            ta = vv.inc()
            s.op("vector", lambda h: h.scalar_tensor_tensor(
                r1[:], r1[:], -1000.0, io64[:], ALU.mult, ALU.add),
                waits=[(vv, ta)], incs=[(vv, 1)])
            tb = vv.inc()
            s.op("vector", lambda h, k=k: h.tensor_reduce(
                widxt[:, k:k + 1], r1[:], AX.X, ALU.min),
                waits=[(vv, tb)], incs=[(vv, 1)])
            tc = vv.inc()
            s.op("vector", lambda h, k=k: h.tensor_scalar(
                r1[:], io64[:], widxt[:, k:k + 1], None, ALU.subtract),
                waits=[(vv, tc)], incs=[(vv, 1)])
            td = vv.inc()
            s.op("vector", lambda h: h.tensor_scalar(
                r1[:], r1[:], 1000.0, None, ALU.is_equal),
                waits=[(vv, td)], incs=[(vv, 1)])
            te = vv.inc()
            s.op("vector", lambda h: h.tensor_add(macc[:], macc[:], r1[:]),
                 waits=[(vv, te), (vv, tm0)], incs=[(vv, 1)])
            tf_ = vv.inc()
            s.op("vector", lambda h: h.scalar_tensor_tensor(
                scw[:], r1[:], -2.0, scw[:], ALU.mult, ALU.add),
                waits=[(vv, tf_)], incs=[(vv, 1)])
            vv.inc()
        s.op("vector", lambda h: h.tensor_reduce(r2[:], m8t[:, 0:6], AX.X,
                                                 ALU.add),
             waits=[(vv, vv.n)], incs=[(vv, 1)])
        tg = vv.inc()
        s.op("vector", lambda h: h.reciprocal(r2[:], r2[:]),
             waits=[(vv, tg)], incs=[(vv, 1)])
        th_ = vv.inc()
        s.op("vector", lambda h: h.tensor_scalar(wnt[:], m8t[:], r2[:, 0:1],
                                                 None, ALU.mult),
             waits=[(vv, th_)], incs=[(vv, 1)])
        ti_ = vv.inc()
        s.wait("sync", vv, ti_)
        sl = slice(tt * 128, (tt + 1) * 128)
        s.dma("sync", o_m[sl, :], macc[:], osm)
        s.dma("sync", o_widx[sl, :], widxt[:], osm)
        s.dma("sync", o_wn[sl, :], wnt[:], osm)
        s.wait("vector", osm, osm.n)
    s.wait("sync", osm, osm.n)
    s.build()
    p.close()
    return p



def build_G2():
    p = Prog()
    nc = p.nc
    NTOK = 8192
    NTI = 64
    i_ln3 = p.din("ln3_all", [NTOK, D], BF16)
    i_m = p.din("m_my", [NTOK, 8], F32)
    i_wup = p.din("wup_sl", [8 * D, 512], F32)
    i_wdn = p.din("wdn_sl", [8 * 256, D], F32)
    i_tri = p.din("tri128x", [128, 128], F32)
    i_off8 = p.din("off8", [128, 8], F32)
    i_ident = p.din("ident", [128, 128], BF16)
    o_outr = p.dout("out_r", [8 * CAP, D], BF16)
    o_pos = p.dout("pos_full", [NTOK, 8], F32)

    x_disp = nc.dram_tensor("x_disp", [8 * CAP + 128, D], BF16)
    bas_d = nc.dram_tensor("bas_d", [NTI, 8], F32)

    m_all = p.sb("m_all", [128, NTI * 8], F32)
    trashc = p.sb("trashc", [128, 1], F32)
    pos_all = p.sb("pos_all", [128, NTI * 8], F32)
    didx = p.sb("didx", [128, NTI * 8], U32)
    ddf = p.sb("ddf", [128, NTI * 8], F32)
    tri = p.sb("tri", [128, 128], F32)
    off8 = p.sb("off8t", [128, 8], F32)
    onescol = p.sb("onescol", [128, 1], F32)
    onesrow = p.sb("onesrow", [1, 128], F32)
    psC_sb = p.sb("psC_sb", [8, NTI], F32)
    basrow = p.sb("basrow", [1, NTI * 8], F32)
    ident = p.sb("identt", [128, 128], BF16)
    identf = p.sb("identtf", [128, 128], F32)
    lnt = [p.sb(f"lnt{i}", [128, D], BF16) for i in range(2)]
    xdfm = p.sb("xdfm", [128, ND * CAP], BF16)
    wup = p.sb("wup", [128, ND * 512], BF16)
    wdn = p.sb("wdn", [128, 2 * D], BF16)
    hfm = p.sb("hfm", [128, 4 * CAP], F32)
    hr = p.sb("hr", [128, 2 * CAP], BF16)
    ev = [p.sb(f"ev{i}", [128, 512], BF16) for i in range(6)]

    pmm = [p.ps(f"pmm{i}", [128, 512], F32) for i in range(4)]
    pst = p.ps("pstG", [128, 1024], BF16)
    pstf = p.ps("pstGf", [128, 512], F32)

    ld = p.sem("ld")
    gsm = p.sem("gsm")
    sv = p.sem("sv")
    vv = p.sem("vv")
    tsm = p.sem("tsm")
    xsm = p.sem("xsm")
    osm = p.sem("osm")

    s = Sched(p)
    p.enable_consts(s, ld)
    s.dma("sync", tri[:], i_tri, ld)
    s.dma("sync", off8[:], i_off8, ld)
    s.dma("sync", ident[:], i_ident, ld)
    wm = s.dma("sync", m_all[:].rearrange("p (a n) -> p a n", n=8),
               i_m.rearrange("(a p) n -> p a n", p=128), ld)
    s.op("gpsimd", lambda h: h.iota(trashc[:], [[1, 1]], base=8 * CAP,
                                    channel_multiplier=1,
                                    allow_small_or_imprecise_dtypes=True),
         incs=[(vv, 1)])
    vv.inc()
    s.op("vector", lambda h: h.memset(onescol[:], 1.0))
    s.op("vector", lambda h: h.memset(onesrow[:], 1.0), incs=[(vv, 1)])
    tone = vv.inc()
    s.wait("tensor", ld, wm)
    s.wait("tensor", vv, tone)
    s.op("vector", lambda h: h.tensor_copy(identf[:], ident[:]),
         incs=[(vv, 1)])
    tidf = vv.inc()
    for ti in range(NTI):
        s.op("tensor", lambda h, ti=ti: h.matmul(
            pmm[3][0:8, ti:ti + 1], m_all[:, ti * 8:(ti + 1) * 8], onescol[:],
            start=True, stop=True),
            incs=[(tsm, 1)] if ti == NTI - 1 else ())
    t_cs = tsm.inc()
    s.op("vector", lambda h: h.tensor_copy(psC_sb[:], pmm[3][0:8, 0:NTI]),
         waits=[(tsm, t_cs)], incs=[(vv, 1)])
    t_e = vv.inc()
    s.op("vector", lambda h: h.memset(ddf[0:8, 0:1], 0.0),
         waits=[(vv, t_e)], incs=[(vv, 1)])
    t0_ = vv.inc()
    s.op("vector", lambda h: h.memset(ddf[0:8, NTI:NTI + NTI], 1.0),
         incs=[(vv, 1)])
    t1_ = vv.inc()
    s.op("vector", lambda h: h.tensor_tensor_scan(
        ddf[0:8, 1:NTI], ddf[0:8, NTI:NTI + NTI - 1], psC_sb[:, 0:NTI - 1],
        0.0, ALU.mult, ALU.add), waits=[(vv, t1_), (vv, t0_)], incs=[(vv, 1)])
    t2_ = vv.inc()
    s.wait("sync", vv, t2_)
    wb = s.dma("sync", bass.AP(bas_d.ap().tensor, 0, [[1, 8], [8, NTI]]),
               ddf[0:8, 0:NTI], ld, allow_slow_non_contiguous=True)
    s.wait("sync", ld, wb)
    wb2 = s.dma("sync", basrow[:].rearrange("p (a n) -> p a n", n=8),
                bas_d.ap().rearrange("(p a) n -> p a n", p=1), ld)
    s.wait("tensor", ld, wb2)
    for ti in range(NTI):
        bank = pmm[ti % 3]
        if ti >= 3:
            s.wait("tensor", vv, posev[ti - 3])
        s.op("tensor", lambda h, ti=ti, bank=bank: h.matmul(
            bank[0:128, 0:8], tri[:], m_all[:, ti * 8:(ti + 1) * 8],
            start=True, stop=False))
        s.op("tensor", lambda h, ti=ti, bank=bank: h.matmul(
            bank[0:128, 0:8], onesrow[:], basrow[:, ti * 8:(ti + 1) * 8],
            start=False, stop=True), incs=[(tsm, 1)])
        tmm = tsm.inc()
        s.op("vector", lambda h, ti=ti, bank=bank: h.tensor_copy(
            pos_all[:, ti * 8:(ti + 1) * 8], bank[0:128, 0:8]),
            waits=[(tsm, tmm)], incs=[(vv, 1)])
        if ti == 0:
            posev = {}
        posev[ti] = vv.inc()
    s.wait("sync", vv, posev[NTI - 1])
    wps = s.dma("sync", o_pos.rearrange("(a p) n -> p a n", p=128),
                pos_all[:].rearrange("p (a n) -> p a n", n=8), osm)
    s.wait("vector", osm, wps)
    s.wait("vector", vv, posev[NTI - 1])
    for ti in range(NTI):
        slt = slice(ti * 8, (ti + 1) * 8)
        s.op("vector", lambda h, slt=slt: h.tensor_scalar(
            ddf[:, slt], pos_all[:, slt], float(CAP), None, ALU.is_lt),
            incs=[(vv, 1)])
        ta = vv.inc()
        s.op("vector", lambda h, slt=slt: h.tensor_mul(
            ddf[:, slt], ddf[:, slt], m_all[:, slt]),
            waits=[(vv, ta)], incs=[(vv, 1)])
        tb = vv.inc()
        s.op("vector", lambda h, slt=slt: h.tensor_add(
            pos_all[:, slt], pos_all[:, slt], off8[:]),
            waits=[(vv, tb)], incs=[(vv, 1)])
        tc = vv.inc()
        s.op("vector", lambda h, slt=slt: h.tensor_scalar(
            pos_all[:, slt], pos_all[:, slt], trashc[:, 0:1], None,
            ALU.subtract), waits=[(vv, tc)], incs=[(vv, 1)])
        td = vv.inc()
        s.op("vector", lambda h, slt=slt: h.tensor_mul(
            ddf[:, slt], ddf[:, slt], pos_all[:, slt]),
            waits=[(vv, td)], incs=[(vv, 1)])
        te = vv.inc()
        s.op("vector", lambda h, slt=slt: h.tensor_scalar(
            ddf[:, slt], ddf[:, slt], trashc[:, 0:1], None, ALU.add),
            waits=[(vv, te)], incs=[(vv, 1)])
        tf_ = vv.inc()
        s.op("vector", lambda h, slt=slt: h.tensor_copy(didx[:, slt],
                                                        ddf[:, slt]),
            waits=[(vv, tf_)], incs=[(vv, 1)])
        vv.inc()
    s.build()

    lw = {}
    for tg in range(16):
        s = Sched(p)
        for tl in range(4):
            ti = tg * 4 + tl
            bi = ti % 2
            if tl >= 2:
                s.wait("sync", gsm, lw[ti - 2])
            wt = s.dma("sync", lnt[bi][:], i_ln3[ti * 128:(ti + 1) * 128, :],
                       ld)
            s.wait("gpsimd", ld, wt)
            for e in range(8):
                s.op("gpsimd", lambda h, ti=ti, e=e, bi=bi: h.indirect_dma_start(
                    out=x_disp.ap(), out_offset=IndirectOffsetOnAxis(
                        ap=didx[:, ti * 8 + e:ti * 8 + e + 1], axis=0),
                    in_=lnt[bi][:], in_offset=None),
                    incs=[(gsm, 16)])
                lw[ti] = gsm.inc(16)
                s.wait("gpsimd", gsm, lw[ti])
        if tg == 15:
            s.wait("gpsimd", gsm, gsm.n)
        s.build()

    for e in range(8):
        s = Sched(p)
        wu = s.dma("gpsimd", wup[:].rearrange("p (c n) -> p c n", n=512),
                   i_wup[e * D:(e + 1) * D, :].rearrange("(c p) n -> p c n",
                                                         p=128), ld)
        wd = s.dma("gpsimd", wdn[:].rearrange("p (c n) -> p c n", n=D),
                   i_wdn[e * 256:(e + 1) * 256, :].rearrange(
                       "(c p) n -> p c n", p=128), ld)
        tp_ = MMPipe(s, [pst], tsm, None, xsm)
        for st in range(8):
            bi = st % 2
            wt = s.dma("sync", lnt[bi][:],
                       x_disp.ap()[e * CAP + st * 128:e * CAP + (st + 1) * 128, :],
                       ld)
            s.wait("tensor", ld, wt)
            for dc in range(ND):
                def tfn(h, b, dc=dc, bi=bi):
                    return h.transpose(b[:, 0:128],
                                       lnt[bi][:, dc * 128:(dc + 1) * 128],
                                       ident[:])
                def efn(h, b, dc=dc, st=st):
                    return h.tensor_copy(
                        xdfm[:, dc * CAP + st * 128:dc * CAP + st * 128 + 128],
                        b[:, 0:128])
                tp_.job([tfn], [("vector", efn, True)])
            if st >= 1:
                pass
            s.wait("sync", tsm, tsm.n)
        mp = MMPipe(s, pmm, tsm, None, xsm)
        s.wait("tensor", ld, wu)
        s.wait("vector", xsm, xsm.n)
        for uc in range(4):
            for sh in range(2):
                mms = []
                for dc in range(ND):
                    def f(h, b, dc=dc, uc=uc, sh=sh):
                        return h.matmul(
                            b[:], wup[:, dc * 512 + uc * 128:dc * 512 + uc * 128 + 128],
                            xdfm[:, dc * CAP + sh * 512:dc * CAP + sh * 512 + 512],
                            start=(dc == 0), stop=(dc == ND - 1))
                    mms.append(f)
                def evf(h, b, uc=uc, sh=sh):
                    return h.tensor_copy(
                        hfm[:, uc * CAP + sh * 512:uc * CAP + sh * 512 + 512],
                        b[:])
                mp.job(mms, [("vector", evf, True)])
        s.wait("scalar", xsm, xsm.n)
        for u in range(2):
            s.op("scalar", lambda h, u=u: h.activation(
                hfm[:, u * CAP:(u + 1) * CAP], hfm[:, u * CAP:(u + 1) * CAP],
                AF.Silu), incs=[(sv, 1)])
            tsx = sv.inc()
            s.op("vector", lambda h, u=u: h.tensor_mul(
                hr[:, u * CAP:(u + 1) * CAP], hfm[:, u * CAP:(u + 1) * CAP],
                hfm[:, (2 + u) * CAP:(3 + u) * CAP]), waits=[(sv, tsx)],
                incs=[(vv, 1)])
            vv.inc()
        s.wait("tensor", vv, vv.n)
        s.wait("tensor", ld, wd)
        bm = BufMgr(s)
        stj = [0]
        for st in range(8):
            for osl in range(4):
                mms = []
                for hc in range(2):
                    def f(h, b, hc=hc, st=st, osl=osl):
                        return h.matmul(
                            b[:], hr[:, hc * CAP + st * 128:hc * CAP + st * 128 + 128],
                            wdn[:, hc * D + osl * 512:hc * D + osl * 512 + 512],
                            start=(hc == 0), stop=(hc == 1))
                    mms.append(f)
                j = stj[0]
                stj[0] += 1
                buf = ev[j % 6]
                bm.guard("vector", buf, osm)
                def evf(h, b, buf=buf):
                    return h.tensor_copy(buf[:], b[:])
                def after(lx, buf=buf, st=st, osl=osl, e=e):
                    s.op("sync", lambda h: h.dma_start(
                        out=o_outr[e * CAP + st * 128:e * CAP + (st + 1) * 128,
                                   osl * 512:osl * 512 + 512], in_=buf[:]),
                        waits=[(xsm, lx)], incs=[(osm, 16)])
                    bm.mark(buf, osm.inc(16))
                mp.job(mms, [("vector", evf, True)], after)
        if e == 7:
            s.wait("sync", osm, osm.n)
        s.build()
    p.close()
    return p




def build_G3():
    p = Prog()
    nc = p.nc
    i_outr = p.din("outr_all", [E64 * CAP, D], BF16)
    i_pos = p.din("pos_my", [TLOC, 64], F32)
    i_widx = p.din("widx", [TLOC, 8], F32)
    i_wn = p.din("wnorm", [TLOC, 8], F32)
    i_sh = p.din("sharedout", [TLOC, D], F32)
    i_x3 = p.din("x3", [TLOC, D], F32)
    i_io64 = p.din("iota64p", [128, 64], F32)
    o_x4 = p.dout("x4", [TLOC, D], F32)

    post = p.sb("post", [128, 64], F32)
    wit = p.sb("wit", [128, 8], F32)
    wnt = p.sb("wnt", [128, 8], F32)
    io64 = p.sb("io64", [128, 64], F32)
    r1 = p.sb("r1", [128, 64], F32)
    dstc = p.sb("dstc", [128, 8], F32)
    pvc = p.sb("pvc", [128, 8], F32)
    vvc = p.sb("vvc", [128, 8], F32)
    wtl = p.sb("wtl", [128, 8], F32)
    dstu = p.sb("dstu", [128, 8], U32)
    gb = [p.sb(f"gb{i}", [128, D], BF16) for i in range(2)]
    yac = p.sb("yac", [128, D], F32)
    sht = p.sb("sht", [128, D], F32)
    x4t = p.sb("x4t", [128, D], F32)
    lnt_ = p.sb("lnt_", [128, D], F32)
    st8 = p.sb("st8", [128, 8], F32)
    lnscr = p.sb("lnscr", [128, D], F32)

    ld = p.sem("ld")
    gsm = p.sem("gsm")
    sv = p.sem("sv")
    vv = p.sem("vv")
    osm = p.sem("osm")

    nblk = 0
    for tt in range(NT):
        s = Sched(p)
        if tt == 0:
            p.enable_consts(s, ld)
            wio = s.dma("sync", io64[:], i_io64, ld)
            s.op("vector", lambda h: h.memset(gb[0][:], 0.0))
            s.op("vector", lambda h: h.memset(gb[1][:], 0.0))
            s.wait("scalar", ld, 16)
        sl = slice(tt * 128, (tt + 1) * 128)
        wp = s.dma("sync", post[:], i_pos[sl, :], ld)
        wi = s.dma("sync", wit[:], i_widx[sl, :], ld)
        wn = s.dma("sync", wnt[:], i_wn[sl, :], ld)
        wsh = s.dma("sync", sht[:], i_sh[sl, :], ld)
        wx3 = s.dma("sync", x4t[:], i_x3[sl, :], ld)
        s.wait("vector", ld, ld.n)
        for k in range(TOPK6):
            s.op("vector", lambda h, k=k: h.tensor_scalar(
                r1[:], io64[:], wit[:, k:k + 1], None, ALU.is_equal),
                waits=[(vv, vv.n)], incs=[(vv, 1)])
            ta = vv.inc()
            s.op("vector", lambda h: h.scalar_tensor_tensor(
                r1[:], io64[:], 1024.0, r1[:], ALU.mult, ALU.mult),
                waits=[(vv, ta)], incs=[(vv, 1)])
            tb = vv.inc()
            s.op("vector", lambda h, k=k: h.tensor_scalar(
                lnscr[:, 0:64], io64[:], wit[:, k:k + 1], None, ALU.is_equal),
                waits=[(vv, tb)], incs=[(vv, 1)])
            tc = vv.inc()
            s.op("vector", lambda h: h.tensor_mul(
                lnscr[:, 0:64], lnscr[:, 0:64], post[:]),
                waits=[(vv, tc)], incs=[(vv, 1)])
            td = vv.inc()
            s.op("vector", lambda h: h.tensor_add(r1[:], r1[:],
                                                  lnscr[:, 0:64]),
                 waits=[(vv, td)], incs=[(vv, 1)])
            te = vv.inc()
            s.op("vector", lambda h, k=k: h.tensor_reduce(
                dstc[:, k:k + 1], r1[:], AX.X, ALU.add),
                waits=[(vv, te)], incs=[(vv, 1)])
            tf_ = vv.inc()
            s.op("vector", lambda h, k=k: h.scalar_tensor_tensor(
                pvc[:, k:k + 1], wit[:, k:k + 1], -1024.0, dstc[:, k:k + 1],
                ALU.mult, ALU.add), waits=[(vv, tf_)], incs=[(vv, 1)])
            tg = vv.inc()
            s.op("vector", lambda h, k=k: h.tensor_scalar(
                vvc[:, k:k + 1], pvc[:, k:k + 1], float(CAP), None, ALU.is_lt),
                waits=[(vv, tg)], incs=[(vv, 1)])
            th_ = vv.inc()
            s.op("vector", lambda h, k=k: h.tensor_mul(
                wtl[:, k:k + 1], wnt[:, k:k + 1], vvc[:, k:k + 1]),
                waits=[(vv, th_)], incs=[(vv, 1)])
            ti_ = vv.inc()
            s.op("vector", lambda h, k=k: h.tensor_mul(
                dstc[:, k:k + 1], dstc[:, k:k + 1], vvc[:, k:k + 1]),
                waits=[(vv, ti_)], incs=[(vv, 1)])
            tj = vv.inc()
            s.op("vector", lambda h, k=k: h.tensor_copy(
                dstu[:, k:k + 1], dstc[:, k:k + 1]),
                waits=[(vv, tj)], incs=[(vv, 1)])
            vv.inc()
        for k in range(TOPK6):
            bi = k % 2
            s.wait("gpsimd", vv, vv.n)
            if k >= 2:
                s.wait("gpsimd", vv, gdone[k - 2])
            s.op("gpsimd", lambda h, k=k, bi=bi: h.indirect_dma_start(
                out=gb[bi][:], out_offset=None, in_=i_outr,
                in_offset=IndirectOffsetOnAxis(ap=dstu[:, k:k + 1], axis=0)),
                incs=[(gsm, 16)])
            wg = gsm.inc(16)
            s.wait("gpsimd", gsm, wg)
            s.wait("vector", gsm, wg)
            if k == 0:
                s.op("vector", lambda h, bi=bi: h.tensor_scalar(
                    yac[:], gb[bi][:], wtl[:, 0:1], None, ALU.mult),
                    incs=[(vv, 1)])
            else:
                s.op("vector", lambda h, k=k, bi=bi: h.scalar_tensor_tensor(
                    yac[:], gb[bi][:], wtl[:, k:k + 1], yac[:],
                    ALU.mult, ALU.add), incs=[(vv, 1)])
            if k == 0:
                gdone = {}
            gdone[k] = vv.inc()
        s.op("vector", lambda h: h.tensor_add(yac[:], yac[:], sht[:]),
             waits=[(vv, vv.n)], incs=[(vv, 1)])
        ty = vv.inc()
        s.wait("scalar", vv, ty)
        emit_ln(s, p, yac[:], lnt_[:], st8[:, 0:8], sv, vv, D, lnscr[:])
        s.op("vector", lambda h: h.tensor_add(x4t[:], x4t[:], lnt_[:]),
             waits=[(vv, vv.n)], incs=[(vv, 1)])
        tx4 = vv.inc()
        s.wait("sync", vv, tx4)
        s.dma("sync", o_x4[sl, :], x4t[:], osm)
        s.wait("sync", osm, osm.n)
        s.build()
    p.close()
    return p



_PROGS = {}


def _prog(name, builder):
    if name not in _PROGS:
        _PROGS[name] = builder()
    return _PROGS[name]


def _run(p, in_maps):
    res = run_bass_kernel_spmd(p.nc, in_maps, core_ids=list(range(NCORES)))
    return res.results


def kernel(**inp):
    x = _f32(inp["L0_attn_x"])
    ident = np.eye(128).astype(BFNP)
    tri64 = np.triu(np.ones((CHK, CHK), np.float32))
    mask64 = np.triu(np.ones((CHK, CHK), np.float32)).astype(BFNP)
    rst2048 = np.ones((64, 2048), np.float32)
    rst2048[:, ::64] = 0.0
    rst16 = np.ones((64, 15), np.float32)
    iota1000_8 = (np.arange(8, dtype=np.float32) + 1000.0)[None, :]
    io64bc = np.tile(np.arange(64, dtype=np.float32)[None, :] + 1000.0,
                     (128, 1))
    io64p = np.tile(np.arange(64, dtype=np.float32)[None, :], (128, 1))
    tri128x = np.triu(np.ones((128, 128), np.float32), 1)
    off8 = np.tile((np.arange(8, dtype=np.float32) * CAP)[None, :], (128, 1))
    padsel = np.array([[0, 0, 0, 0, 0, 0, 1, 1]], np.float32)

    pA1 = _prog("A1", build_A1)
    bgk2 = _f32(inp["L0_attn_bgk2"])
    mapsA1 = []
    for c in range(NCORES):
        b, hh = c // 2, c % 2
        mapsA1.append({
            "x": _f32(x[b, hh * TLOC:(hh + 1) * TLOC]),
            "wq": _f32(inp["L0_attn_Wq"]), "wk": _f32(inp["L0_attn_Wk"]),
            "wv": _f32(inp["L0_attn_Wv"]), "wg": _f32(inp["L0_attn_Wg"]),
            "wgk1": _f32(inp["L0_attn_Wgk1"]),
            "wgk2": _f32(inp["L0_attn_Wgk2"]),
            "bgk_bc": np.tile(bgk2[None, :], (128, 1)),
            "bgk_col": np.ascontiguousarray(bgk2.reshape(8, 128).T),
            "ident": ident, "cconst": pA1.const_host(),
        })
    rA1 = _run(pA1, mapsA1)

    pA2 = _prog("A2", build_A2)
    normw_bc = np.tile(_f32(inp["L0_attn_norm_w"])[None, :], (128, 1))
    mapsA2 = []
    for c in range(NCORES):
        b, hh = c // 2, c % 2
        psel = np.zeros((64, 8), np.float32)
        if hh == 1:
            psel[:, c - 1] = 1.0
        r = rA1[c]
        mapsA2.append({
            "q_fm": r["q_fm"], "k_fm": r["k_fm"], "k_tok": r["k_tok"],
            "v_tok": r["v_tok"], "g_tok": r["g_tok"], "gk_fm": r["gk_fm"],
            "gk_tok": r["gk_tok"], "wo": _f32(inp["L0_attn_Wo"]),
            "x_res": _f32(x[b, hh * TLOC:(hh + 1) * TLOC]),
            "normw_bc": normw_bc, "pairsel": psel, "tri64": tri64,
            "mask64": mask64, "rst2048": rst2048, "rst16": rst16,
            "ident": ident, "cconst": pA2.const_host(),
        })
    rA2 = _run(pA2, mapsA2)

    pF1 = _prog("F1", build_F1)
    gwt = np.zeros((D, 8), np.float32)
    gwt[:, :6] = np.ascontiguousarray(inp["L0_ffn_gate_W"].T)
    w1r = _f32(np.asarray(inp["L0_ffn_W1"]).reshape(6 * D, 1408))
    w2r = _f32(np.asarray(inp["L0_ffn_W2"]).reshape(6 * 1408, D))
    b1t = np.zeros((1408, 8), np.float32)
    b1t[:, :6] = np.ascontiguousarray(inp["L0_ffn_b1"].T)
    b2r = _f32(inp["L0_ffn_b2"])
    ln1c = {b: np.concatenate([np.asarray(rA2[2 * b]["ln1_fm"]),
                               np.asarray(rA2[2 * b + 1]["ln1_fm"])], axis=1)
            for b in range(4)}
    mapsF1 = []
    for c in range(NCORES):
        b, k = c // 2, c % 2
        sel2 = np.zeros((1, 2), np.float32)
        sel2[0, k] = 1.0
        mapsF1.append({
            "ln1_fm": ln1c[b], "gate_WT": gwt, "W1r": w1r, "W2r": w2r,
            "b1T": b1t, "b2rows": b2r, "sel2": sel2, "sel8": padsel,
            "iota1e9": iota1000_8, "cconst": pF1.const_host(),
        })
    rF1 = _run(pF1, mapsF1)

    pF2 = _prog("F2M1", build_F2M1)
    wint = np.ascontiguousarray(_f32(inp["L1_attn_in_proj_w"]).T)
    mapsF2 = []
    for c in range(NCORES):
        b, hh = c // 2, c % 2
        sl = slice(hh * TLOC, (hh + 1) * TLOC)
        mapsF2.append({
            "slot0": np.asarray(rF1[2 * b]["out_slot"])[sl],
            "slot1": np.asarray(rF1[2 * b + 1]["out_slot"])[sl],
            "x1": np.asarray(rA2[c]["x1"]),
            "W_inT": wint, "ident": ident, "cconst": pF2.const_host(),
        })
    rF2 = _run(pF2, mapsF2)

    pM2 = _prog("M2", build_M2)
    cw = _f32(inp["L1_attn_conv_w"]).reshape(32, 128, 4).transpose(1, 0, 2)\
        .reshape(128, 128)
    cb = np.ascontiguousarray(_f32(inp["L1_attn_conv_b"]).reshape(32, 128).T)
    xpt = np.ascontiguousarray(_f32(inp["L1_attn_x_proj_w"]).T)
    dpt = np.ascontiguousarray(_f32(inp["L1_attn_dt_proj_w"]).T)
    dpb2 = np.ascontiguousarray(
        (2.0 * _f32(inp["L1_attn_dt_proj_b"])).reshape(32, 128).T)
    mapsM2 = []
    for c in range(NCORES):
        b, hh = c // 2, c % 2
        xz = np.asarray(rF2[c]["xz_fm"])
        xh = xz[0:DIN]
        if hh == 0:
            halo = np.zeros((DIN, 3), BFNP)
        else:
            halo = np.asarray(rF2[c - 1]["xz_fm"])[0:DIN, TLOC - 3:TLOC]
        mapsM2.append({
            "xh_fm": np.concatenate([halo, xh], axis=1),
            "conv_w": cw, "conv_b": cb, "x_projT": xpt, "dt_projT": dpt,
            "dpb2": dpb2, "cconst": pM2.const_host(),
        })
    rM2 = _run(pM2, mapsM2)

    pM3 = _prog("M3", build_M3)
    al = _f32(inp["L1_attn_A_log"])
    Dv = _f32(inp["L1_attn_D"])
    wout = np.ascontiguousarray(_f32(inp["L1_attn_out_proj_w"]).T)
    mapsM3 = []
    for c in range(NCORES):
        b, dg = c // 2, c % 2
        rsl = slice(2048 * dg, 2048 * (dg + 1))
        dt_c = np.concatenate([np.asarray(rM2[2 * b]["dt_fm"])[rsl],
                               np.asarray(rM2[2 * b + 1]["dt_fm"])[rsl]], 1)
        xc_c = np.concatenate([np.asarray(rM2[2 * b]["xc_fm"])[rsl],
                               np.asarray(rM2[2 * b + 1]["xc_fm"])[rsl]], 1)
        z_c = np.concatenate(
            [np.asarray(rF2[2 * b]["xz_fm"])[DIN + 2048 * dg:DIN + 2048 * (dg + 1)],
             np.asarray(rF2[2 * b + 1]["xz_fm"])[DIN + 2048 * dg:DIN + 2048 * (dg + 1)]],
            1)
        B_c = np.concatenate([np.asarray(rM2[2 * b]["bc_rows"])[0:16],
                              np.asarray(rM2[2 * b + 1]["bc_rows"])[0:16]], 1)
        C_c = np.concatenate([np.asarray(rM2[2 * b]["bc_rows"])[16:32],
                              np.asarray(rM2[2 * b + 1]["bc_rows"])[16:32]], 1)
        al_sl = al[rsl].reshape(16, 128, 16).transpose(1, 0, 2).reshape(128, 256)
        D_sl = np.ascontiguousarray(Dv[rsl].reshape(16, 128).T)
        mapsM3.append({
            "dt": dt_c, "xc": xc_c, "z_fm": z_c, "Brows": B_c, "Crows": C_c,
            "A_log": np.ascontiguousarray(al_sl), "Dcol": D_sl,
            "WoutT_sl": np.ascontiguousarray(wout[rsl]),
            "cconst": pM3.const_host(),
        })
    rM3 = _run(pM3, mapsM3)

    pM5 = _prog("M5", build_M5)
    mapsM5 = []
    for c in range(NCORES):
        b, hh = c // 2, c % 2
        sl = slice(hh * TLOC, (hh + 1) * TLOC)
        mapsM5.append({
            "part0": np.asarray(rM3[2 * b]["part"])[sl],
            "part1": np.asarray(rM3[2 * b + 1]["part"])[sl],
            "x2": np.asarray(rF2[c]["x2"]),
            "ident": ident, "cconst": pM5.const_host(),
        })
    rM5 = _run(pM5, mapsM5)

    pG1 = _prog("G1", build_G1)
    mapsG1 = []
    for c in range(NCORES):
        mapsG1.append({
            "ln3_fm": np.asarray(rM5[c]["ln3_fm"]),
            "w_gate": _f32(inp["L1_ffn_w_gate"]),
            "w_shared_up": _f32(inp["L1_ffn_w_shared_up"]),
            "w_shared_down": _f32(inp["L1_ffn_w_shared_down"]),
            "iota64_bc": io64bc, "cconst": pG1.const_host(),
        })
    rG1 = _run(pG1, mapsG1)

    pG2 = _prog("G2", build_G2)
    ln3_all = np.concatenate([np.asarray(rM5[c]["ln3_tok"])
                              for c in range(NCORES)], axis=0)
    m_full = np.concatenate([np.asarray(rG1[c]["mmat"])
                             for c in range(NCORES)], axis=0)
    wup = _f32(np.asarray(inp["L1_ffn_w_up"]))
    wdn = _f32(np.asarray(inp["L1_ffn_w_down"]))
    mapsG2 = []
    for c in range(NCORES):
        mapsG2.append({
            "ln3_all": ln3_all,
            "m_my": np.ascontiguousarray(m_full[:, 8 * c:8 * c + 8]),
            "wup_sl": np.ascontiguousarray(
                wup[8 * c:8 * c + 8].reshape(8 * D, 512)),
            "wdn_sl": np.ascontiguousarray(
                wdn[8 * c:8 * c + 8].reshape(8 * 256, D)),
            "tri128x": tri128x, "off8": off8, "ident": ident,
            "cconst": pG2.const_host(),
        })
    rG2 = _run(pG2, mapsG2)

    pG3 = _prog("G3", build_G3)
    outr_all = np.concatenate([np.asarray(rG2[c]["out_r"])
                               for c in range(NCORES)], axis=0)
    pos_full = np.concatenate([np.asarray(rG2[c]["pos_full"])
                               for c in range(NCORES)], axis=1)
    mapsG3 = []
    for c in range(NCORES):
        sl = slice(c * TLOC, (c + 1) * TLOC)
        mapsG3.append({
            "outr_all": outr_all,
            "pos_my": np.ascontiguousarray(pos_full[sl]),
            "widx": np.asarray(rG1[c]["widx"]),
            "wnorm": np.asarray(rG1[c]["wnorm"]),
            "sharedout": np.asarray(rG1[c]["sharedout"]),
            "x3": np.asarray(rM5[c]["x3"]),
            "iota64p": io64p, "cconst": pG3.const_host(),
        })
    rG3 = _run(pG3, mapsG3)

    out = np.empty((4, T, D), np.float32)
    for c in range(NCORES):
        b, hh = c // 2, c % 2
        out[b, hh * TLOC:(hh + 1) * TLOC] = np.asarray(rG3[c]["x4"])
    return out
